# version 11 (final)
# baseline (speedup 1.0000x reference)
from concourse.bass_utils import run_bass_kernel_spmd


"""Kernel-building framework for the mk_hybrid_d2 megakernel.

Op scheduler with buffer-granularity dependency tracking:
- Ops are recorded in global program order, each on one engine.
- Each engine has ONE semaphore; every instruction bumps it (+1 compute,
  +16 DMA). Cross-engine deps become wait_ge(producer_engine_sem, count).
- Same-engine deps: emitted as waits too (safe mode), except tensor->tensor
  (PE FIFO is safe; waits would kill matmul throughput).
- Buffers: named resources (tensor, key). Tracker adds RAW/WAR/WAW deps.
"""
import numpy as np
import ml_dtypes
import concourse.bass as bass
import concourse.mybir as mybir
from concourse.bass_core_rust import IndirectOffsetOnAxis

F32 = mybir.dt.float32
BF16 = mybir.dt.bfloat16
I32 = mybir.dt.int32
I16 = mybir.dt.int16
AF = mybir.ActivationFunctionType
ALU = mybir.AluOpType
AX = mybir.AxisListType
BF = ml_dtypes.bfloat16

ENGINES = ("sync", "scalar", "vector", "tensor", "gpsimd")


class Op:
    __slots__ = ("engine", "emit", "inc", "seq", "cum", "waits", "tag")

    def __init__(self, engine, emit, inc, tag=""):
        self.engine = engine
        self.emit = emit
        self.inc = inc
        self.seq = None
        self.cum = None
        self.waits = {}
        self.tag = tag


class K:

    def __init__(self, nc):
        self.nc = nc
        self.ops = []
        self.counts = {(e, kind): 0 for e in ENGINES for kind in ("c", "d")}
        self.bufs = {}
        self.trust_same_engine = {"tensor"}
        self.dma_bounds = {e: set() for e in ENGINES}
        self.barrier_waits = {}

    def barrier(self):
        for e in ENGINES:
            for kind in ("c", "d"):
                v = self.counts[(e, kind)]
                if v:
                    self.barrier_waits[(e, kind)] = v
            v = self.counts[(e, "d")]
            if v:
                self.dma_bounds[e].add(v)

    def op(self, engine, emit, reads=(), writes=(), dma=False, deps=(), tag=""):
        inc = 16 if dma else 1
        kind = "d" if dma else "c"
        o = Op(engine, emit, inc, tag)
        dep_ops = list(deps)
        for r in reads:
            ent = self.bufs.get(r)
            if ent and ent[0] is not None:
                dep_ops.append(ent[0])
        for w in writes:
            ent = self.bufs.get(w)
            if ent:
                if ent[0] is not None:
                    dep_ops.append(ent[0])
                dep_ops.extend(ent[1])
        for d in dep_ops:
            if d is None:
                continue
            if d.inc == 16:
                dk = (d.engine, "d")
                val = self.counts[dk]
                self.dma_bounds[d.engine].add(val)
            else:
                if d.engine == engine and engine in self.trust_same_engine:
                    continue
                dk = (d.engine, "c")
                val = d.cum
            cur = o.waits.get(dk, -1)
            if val > cur:
                o.waits[dk] = val
        for dk, val in self.barrier_waits.items():
            if dk == (engine, "c"):
                continue
            if val > o.waits.get(dk, -1):
                o.waits[dk] = val
        self.counts[(engine, kind)] += inc
        o.cum = self.counts[(engine, kind)]
        o.seq = len(self.ops)
        self.ops.append(o)
        for r in reads:
            ent = self.bufs.setdefault(r, [None, []])
            ent[1].append(o)
        for w in writes:
            self.bufs[w] = [o, []]
        return o

    def emit(self, block):
        nc = self.nc
        sems = {}
        for e in ENGINES:
            for kind in ("c", "d"):
                if self.counts[(e, kind)] or (e == "sync" and kind == "d"):
                    sems[(e, kind)] = nc.alloc_semaphore(f"ksem_{e}_{kind}")
        per_engine = {e: [] for e in ENGINES}
        for o in self.ops:
            per_engine[o.engine].append(o)

        def mk(engine):
            ops = per_engine[engine]
            bounds = sorted(self.dma_bounds[engine])

            def body(eng):
                waited = {sk: 0 for sk in sems}
                bi = 0
                for o in ops:
                    if o.inc == 16:
                        pre = o.cum - o.inc
                        dk = (engine, "d")
                        while bi < len(bounds) and bounds[bi] <= pre:
                            b = bounds[bi]
                            bi += 1
                            if b > waited[dk]:
                                eng.wait_ge(sems[dk], b)
                                waited[dk] = b
                    for sk, val in sorted(o.waits.items(), key=str):
                        if val > waited[sk]:
                            eng.wait_ge(sems[sk], val)
                            waited[sk] = val
                    inst = o.emit(eng)
                    inst.then_inc(sems[(engine, "d" if o.inc == 16 else "c")], o.inc)
            return body

        for e in ENGINES:
            if per_engine[e] or e == "sync":
                getattr(block, e)(mk(e))
        final = {sk: self.counts[sk] for sk in sems}

        def tail(eng):
            for sk, v in final.items():
                if v:
                    eng.wait_ge(sems[sk], v)
        block.sync(tail)
        self.sems = sems



def ceil_div(a, b):
    return (a + b - 1) // b


class Const:

    def __init__(self, nc, k):
        self.nc = nc
        self.k = k
        self.cache = {}

    def dram(self, name, arr):
        if name not in self.cache:
            self.cache[name] = self.nc.inline_tensor(np.ascontiguousarray(arr), f"c_{name}").ap()
        return self.cache[name]


def np_bir_dtype(dt):
    return {F32: np.float32, BF16: BF, I32: np.int32, I16: np.int16}[dt]

"""Sublayer emitters for the mk_hybrid megakernel. All activations are
feature-major (fm): SBUF tiles [128 dims, T tokens], d_model=1024 = 8 chunks.

Conventions:
- T = 2048 tokens per core, D = 1024, NC = 8 chunks, TT = 512 token tile.
- fm buffers are [128, nchunks, T] tensors; buffer key = (name, chunk).
- Weights pre-staged in DRAM as bf16 in lhsT orientation ([d_in, d_out],
  rows contiguous) unless noted; tiles loaded [128, 128].
"""
import numpy as np

T = 2048
D = 1024
NC = D // 128
TT = 512
NTT = T // TT
EPS_LN = 1e-5


class Ctx:
    def __init__(self, nc, k):
        self.nc = nc
        self.k = k
        self.const = Const(nc, k)
        self.uid = 0
        self._sc_tile = None
        self._sc_map = {}

    def name(self, p):
        self.uid += 1
        return f"{p}{self.uid}"

    def sconst(self, val):
        val = float(val)
        if self._sc_tile is None:
            self._sc_tile = self.nc.alloc_sbuf_tensor("scconsts", [128, 32], F32)
        if val not in self._sc_map:
            idx = len(self._sc_map)
            assert idx < 32
            key = ("sconst", val)
            t = self._sc_tile
            self.k.op("vector", lambda e, idx=idx, val=val: e.memset(t[:, idx:idx + 1], val),
                      writes=[key])
            self.nc.const_aps.register(F32, val, t[:, idx:idx + 1])
            self._sc_map[val] = key
        return self._sc_map[val]


class PsumPool:

    def __init__(self, ctx, n, shape, dtype=F32, prefix="pp"):
        self.slots = []
        for i in range(n):
            t = ctx.nc.alloc_psum_tensor(ctx.name(prefix), list(shape), dtype)
            self.slots.append((t, ctx.name(prefix + "k")))
        self.i = 0

    def next(self):
        s = self.slots[self.i % len(self.slots)]
        self.i += 1
        return s


class SbufPool:
    def __init__(self, ctx, n, shape, dtype=F32, prefix="sp"):
        self.slots = []
        for i in range(n):
            t = ctx.nc.alloc_sbuf_tensor(ctx.name(prefix), list(shape), dtype)
            self.slots.append((t, ctx.name(prefix + "k")))
        self.i = 0

    def next(self):
        s = self.slots[self.i % len(self.slots)]
        self.i += 1
        return s


def load_const_tile(ctx, arr, name, dtype=F32):
    nc, k = ctx.nc, ctx.k
    t = nc.alloc_sbuf_tensor(ctx.name(name), list(arr.shape), dtype)
    d = ctx.const.dram(ctx.name(name + "_d"), arr)
    key = ctx.name("cbuf")
    k.op("sync", lambda e: e.dma_start(out=t[...], in_=d), writes=[key], dma=True)
    return t, key


def setup_common(ctx):
    ctx.ones1_128, ctx.ones1_128_key = load_const_tile(
        ctx, np.ones((1, 128), np.float32), "ones1x128")
    ctx.ones1_128b, ctx.ones1_128b_key = load_const_tile(
        ctx, np.ones((1, 128), BF), "ones1x128b", BF16)
    ctx.onescol, ctx.onescol_key = load_const_tile(
        ctx, np.ones((128, 1), np.float32), "ones128x1")
    ctx.onescol_b, ctx.onescol_b_key = load_const_tile(
        ctx, np.ones((128, 1), BF), "ones128x1b", BF16)
    ctx.identb, ctx.identb_key = load_const_tile(
        ctx, np.eye(128, dtype=BF), "identb", BF16)
    ctx.identf, ctx.identf_key = load_const_tile(
        ctx, np.eye(128, dtype=np.float32), "identf", F32)


def bcast_rows(ctx, src_ap, src_key, out_sb, out_key, width, pp, nparts=128,
               lhsT=None, lhsT_key=None, nrows=1, out_dt_tiles=None):
    nc, k = ctx.nc, ctx.k
    if lhsT is None:
        sdt = src_ap.tensor.dtype if hasattr(src_ap, 'tensor') else src_ap.dtype
        if sdt == BF16:
            lhsT, lhsT_key = ctx.ones1_128b, ctx.ones1_128b_key
        else:
            lhsT, lhsT_key = ctx.ones1_128, ctx.ones1_128_key
    for t0 in range(0, width, TT):
        w = min(TT, width - t0)
        ps, pkey = pp.next()
        sk = src_key(t0) if callable(src_key) else src_key
        k.op("tensor", lambda e, t0=t0, w=w, ps=ps, sk=sk: e.matmul(
            ps[0:nparts, 0:w], lhsT[0:nrows, 0:nparts], src_ap[0:nrows, t0:t0 + w],
            start=True, stop=True),
            reads=[sk, lhsT_key], writes=[pkey])
        k.op("vector", lambda e, t0=t0, w=w, ps=ps: e.tensor_copy(
            out=out_sb[0:nparts, t0:t0 + w], in_=ps[0:nparts, 0:w]),
            reads=[pkey], writes=[(out_key, t0)])


class XRes:

    def __init__(self, t, keys):
        self.t = t
        self.keys = keys

    def get(self, ctx, ci, t0, w):
        return self.t[:, ci, t0:t0 + w], self.keys[ci]


class XDram:

    def __init__(self, ctx, dram_ap, dram_key, nslots=3, width=TT, pool=None):
        self.dram = dram_ap
        self.dram_key = dram_key
        self.pool = pool if pool is not None else SbufPool(
            ctx, nslots, [128, width], F32, "xd")

    def get(self, ctx, ci, t0, w):
        sl, key = self.pool.next()
        ctx.k.op("sync", lambda e, ci=ci, t0=t0, w=w, sl=sl: e.dma_start(
            out=sl[:, 0:w], in_=self.dram[ci * 128:(ci + 1) * 128, t0:t0 + w]),
            reads=[self.dram_key], writes=[key], dma=True)
        return sl[:, 0:w], key


def reduce_feat(ctx, xsrc, nchunks, outrow, outrow_key, pp, square=None,
                scale=None):
    nc, k = ctx.nc, ctx.k
    for t0 in range(0, T, TT):
        ps, pkey = pp.next()
        for ci in range(nchunks):
            xap, xkey = xsrc.get(ctx, ci, t0, TT)
            if square is not None:
                sq, sqk = square
                k.op("vector", lambda e, xap=xap: e.tensor_mul(
                    out=sq[:, 0:TT], in0=xap, in1=xap),
                    reads=[xkey], writes=[sqk])
                rhs, rk = sq[:, 0:TT], sqk
            else:
                rhs, rk = xap, xkey
            rdt = rhs.tensor.dtype if hasattr(rhs, "tensor") else rhs.dtype
            if rdt == BF16:
                oc_t, oc_k = ctx.onescol_b, ctx.onescol_b_key
            else:
                oc_t, oc_k = ctx.onescol, ctx.onescol_key
            k.op("tensor", lambda e, rhs=rhs, ci=ci, ps=ps, oc_t=oc_t: e.matmul(
                ps[0:1, 0:TT], oc_t[:, :], rhs,
                start=(ci == 0), stop=(ci == nchunks - 1)),
                reads=[rk, oc_k], writes=[pkey])
        if scale is None:
            k.op("vector", lambda e, t0=t0, ps=ps: e.tensor_copy(
                out=outrow[0:1, t0:t0 + TT], in_=ps[0:1, 0:TT]),
                reads=[pkey], writes=[(outrow_key, t0)])
        else:
            k.op("vector", lambda e, t0=t0, ps=ps: e.tensor_scalar(
                out=outrow[0:1, t0:t0 + TT], in0=ps[0:1, 0:TT], scalar1=scale,
                scalar2=None, op0=ALU.mult),
                reads=[pkey], writes=[(outrow_key, t0)])


def layernorm_fm(ctx, xsrc, out, outkeys, sc, lo=None, lokeys=None):
    nc, k = ctx.nc, ctx.k
    murow, murow_k = sc["murow"]
    varrow, varrow_k = sc["varrow"]
    mubc, mubc_k = sc["mubc"]
    rbc, rbc_k = sc["rbc"]
    sq, sqk = sc["sq"]
    reduce_feat(ctx, xsrc, NC, murow, murow_k, sc["pp"], scale=1.0 / D)
    reduce_feat(ctx, xsrc, NC, varrow, varrow_k, sc["pp"], square=sc["sq"],
                scale=1.0 / D)
    mu_reads = [(murow_k, t0) for t0 in range(0, T, TT)]
    ss_reads = [(varrow_k, t0) for t0 in range(0, T, TT)]
    for t0 in range(0, T, TT):
        k.op("vector", lambda e, t0=t0: e.tensor_mul(
            out=sq[0:1, 0:TT], in0=murow[0:1, t0:t0 + TT], in1=murow[0:1, t0:t0 + TT]),
            reads=[(murow_k, t0)], writes=[sqk])
        k.op("vector", lambda e, t0=t0: e.tensor_sub(
            out=varrow[0:1, t0:t0 + TT], in0=varrow[0:1, t0:t0 + TT], in1=sq[0:1, 0:TT]),
            reads=[(varrow_k, t0), sqk], writes=[(varrow_k, t0)])
    k.op("scalar", lambda e: e.activation(out=varrow[:, :], in_=varrow[:, :], func=AF.Sqrt,
                                          bias=EPS_LN, scale=1.0),
         reads=ss_reads + [ctx.sconst(EPS_LN)], writes=ss_reads)
    k.op("vector", lambda e: e.reciprocal(out=varrow[:, :], in_=varrow[:, :]),
         reads=ss_reads, writes=ss_reads)
    bcast_rows(ctx, murow, lambda t0: (murow_k, t0), mubc, mubc_k, T, sc["pp"])
    bcast_rows(ctx, varrow, lambda t0: (varrow_k, t0), rbc, rbc_k, T, sc["pp"])
    for ci in range(NC):
        for t0 in range(0, T, TT):
            xap, xkey = xsrc.get(ctx, ci, t0, TT)
            ov = out[:, ci, t0:t0 + TT]
            if lo is None:
                k.op("vector", lambda e, xap=xap, t0=t0: e.tensor_sub(
                    out=sq[:, 0:TT], in0=xap, in1=mubc[:, t0:t0 + TT]),
                    reads=[xkey, (mubc_k, t0)], writes=[sqk])
                k.op("vector", lambda e, ov=ov, t0=t0: e.tensor_mul(
                    out=ov, in0=sq[:, 0:TT], in1=rbc[:, t0:t0 + TT]),
                    reads=[sqk, (rbc_k, t0)],
                    writes=[outkeys[ci]])
            else:
                f32c, f32k = sc["f32c"]
                lv = lo[:, ci, t0:t0 + TT]
                k.op("vector", lambda e, f32c=f32c, xap=xap, t0=t0: e.tensor_sub(
                    out=f32c[:, 0:TT], in0=xap, in1=mubc[:, t0:t0 + TT]),
                    reads=[xkey, (mubc_k, t0)], writes=[f32k])
                k.op("vector", lambda e, f32c=f32c, t0=t0: e.tensor_mul(
                    out=f32c[:, 0:TT], in0=f32c[:, 0:TT], in1=rbc[:, t0:t0 + TT]),
                    reads=[f32k, (rbc_k, t0)], writes=[f32k])
                k.op("vector", lambda e, ov=ov, f32c=f32c: e.tensor_copy(
                    out=ov, in_=f32c[:, 0:TT]),
                    reads=[f32k], writes=[outkeys[ci]])
                k.op("vector", lambda e, lv=lv, ov=ov, f32c=f32c: e.tensor_sub(
                    out=lv, in0=f32c[:, 0:TT], in1=ov),
                    reads=[f32k, outkeys[ci]], writes=[lokeys[ci]])


def proj_fm(ctx, in_t, in_keys, n_in, w_dram, wstage, wstage_keyb, n_out, pp,
            evac, w_T=False, tok_range=None, rhs_sel=None, n_free=TT):
    nc, k = ctx.nc, ctx.k
    t_lo, t_hi = tok_range or (0, T)
    for oc in range(n_out):
        for ic in range(n_in):
            if w_T:
                k.op("scalar", lambda e, oc=oc, ic=ic: e.dma_start_transpose(
                    wstage[:, ic, :],
                    w_dram[oc * 128:(oc + 1) * 128, ic * 128:(ic + 1) * 128]),
                    writes=[(wstage_keyb, ic)], dma=True)
            else:
                k.op("sync", lambda e, oc=oc, ic=ic: e.dma_start(
                    out=wstage[:, ic, :],
                    in_=w_dram[ic * 128:(ic + 1) * 128, oc * 128:(oc + 1) * 128]),
                    writes=[(wstage_keyb, ic)], dma=True)
        for t0 in range(t_lo, t_hi, n_free):
            w = min(n_free, t_hi - t0)
            ps, pkey = pp.next()
            for ic in range(n_in):
                if rhs_sel is not None:
                    rhs, rk = rhs_sel(ic, t0, w)
                else:
                    rhs, rk = in_t[:, ic, t0:t0 + w], in_keys[ic]
                k.op("tensor", lambda e, oc=oc, ic=ic, rhs=rhs, ps=ps, w=w: e.matmul(
                    ps[:, 0:w], wstage[:, ic, :], rhs,
                    start=(ic == 0), stop=(ic == n_in - 1)),
                    reads=[(wstage_keyb, ic), rk], writes=[pkey])
            evac(oc, t0, w, ps, pkey)


def residual_add(ctx, x, xkeys, src, srckeys, nchunks=NC):
    nc, k = ctx.nc, ctx.k
    for ci in range(nchunks):
        k.op("vector", lambda e, ci=ci: e.tensor_add(
            out=x[:, ci, :], in0=x[:, ci, :], in1=src[:, ci, :]),
            reads=[xkeys[ci], srckeys[ci]], writes=[xkeys[ci]])

"""ffn0: x += swiglu_int8(LN(x)).  HIDDEN=3072 (24 chunks), dequant 1/127.

Weights (prepped bf16 DRAM):
  wg_bf, wu_bf: [3072, 1024]  (lhsT via transpose-load: [d 128, h 128])
  wd_bf:       [1024, 3072]  (lhsT via transpose-load: [h 128, o 128])
Scale folding: silu(s*G) via activation scale; final evac multiplies s^2.
"""
import numpy as np

HID = 3072
NHC = HID // 128
S_DQ = 1.0 / 127.0
HALF = 1024
FT = 512


def alloc_ffn0(ctx):
    nc = ctx.nc
    a = {}
    a["h"] = nc.alloc_sbuf_tensor("ffn_h", [128, NHC, HALF], BF16)
    a["wg"] = nc.alloc_sbuf_tensor("ffn_wg", [128, NC, 128], BF16)
    a["wu"] = nc.alloc_sbuf_tensor("ffn_wu", [128, NC, 128], BF16)
    a["wd"] = nc.alloc_sbuf_tensor("ffn_wd", [128, NHC, 128], BF16)
    a["sg"] = nc.alloc_sbuf_tensor("ffn_sg", [128, FT], F32)
    a["psG"] = nc.alloc_psum_tensor("ffn_psG", [128, FT], F32)
    a["psU"] = nc.alloc_psum_tensor("ffn_psU", [128, FT], F32)
    a["psY"] = nc.alloc_psum_tensor("ffn_psY", [128, FT], F32)
    return a


def step_ffn0x(ctx, xsrc, xout_writer, xln, xlnkeys, wgT_bf, wuT_bf, wdT_bf, a, lnsc,
               tag="f0", wdram_keys=()):
    nc, k = ctx.nc, ctx.k
    layernorm_fm(ctx, xsrc, xln, xlnkeys, lnsc)
    h = a["h"]
    wgT3 = wgT_bf.rearrange("(ic p) h -> p ic h", p=128)
    wuT3 = wuT_bf.rearrange("(ic p) h -> p ic h", p=128)
    wdT3 = wdT_bf.rearrange("(hc p) d -> p hc d", p=128)
    for half in range(2):
        th0 = half * HALF
        hk = (tag, half, "h")
        for hc in range(NHC):
            b = hc % 2
            k.op("sync", lambda e, hc=hc, b=b: e.dma_start(
                out=a["wg"][:, b, :, :], in_=wgT3[:, :, hc * 128:(hc + 1) * 128]),
                reads=list(wdram_keys), writes=[(tag, "wg", b)], dma=True)
            k.op("sync", lambda e, hc=hc, b=b: e.dma_start(
                out=a["wu"][:, b, :, :], in_=wuT3[:, :, hc * 128:(hc + 1) * 128]),
                reads=list(wdram_keys), writes=[(tag, "wu", b)], dma=True)
            for tt in range(0, HALF, FT):
                t0 = th0 + tt
                p = (tt // FT) % 2
                psG = a["psG"][:, p, :]
                psU = a["psU"][:, p, :]
                for ic in range(NC):
                    k.op("tensor", lambda e, b=b, ic=ic, t0=t0, psG=psG: e.matmul(
                        psG, a["wg"][:, b, ic, :], xln[:, ic, t0:t0 + FT],
                        start=(ic == 0), stop=(ic == NC - 1)),
                        reads=[(tag, "wg", b), xlnkeys[ic]], writes=[(tag, "psG", p)])
                sgp = a["sg"][:, p, :]
                k.op("scalar", lambda e, sgp=sgp, psG=psG: e.activation(
                    out=sgp, in_=psG, func=AF.Silu, scale=S_DQ),
                    reads=[(tag, "psG", p)], writes=[(tag, "sg", p)])
                for ic in range(NC):
                    k.op("tensor", lambda e, b=b, ic=ic, t0=t0, psU=psU: e.matmul(
                        psU, a["wu"][:, b, ic, :], xln[:, ic, t0:t0 + FT],
                        start=(ic == 0), stop=(ic == NC - 1)),
                        reads=[(tag, "wu", b), xlnkeys[ic]], writes=[(tag, "psU", p)])
                k.op("vector", lambda e, hc=hc, tt=tt, psU=psU, sgp=sgp: e.tensor_mul(
                    out=h[:, hc, tt:tt + FT], in0=psU, in1=sgp),
                    reads=[(tag, "psU", p), (tag, "sg", p)], writes=[(hk, hc)])
        for oc in range(NC):
            b = oc % 2
            k.op("sync", lambda e, oc=oc, b=b: e.dma_start(
                out=a["wd"][:, b, :, :], in_=wdT3[:, :, oc * 128:(oc + 1) * 128]),
                reads=list(wdram_keys), writes=[(tag, "wd", b)], dma=True)
            for tt in range(0, HALF, FT):
                t0 = th0 + tt
                p = (tt // FT) % 2
                psY = a["psY"][:, p, :]
                for hc in range(NHC):
                    k.op("tensor", lambda e, b=b, hc=hc, tt=tt, psY=psY: e.matmul(
                        psY, a["wd"][:, b, hc, :], h[:, hc, tt:tt + FT],
                        start=(hc == 0), stop=(hc == NHC - 1)),
                        reads=[(tag, "wd", b), (hk, hc)], writes=[(tag, "psY", p)])
                xout_writer(oc, t0, psY, (tag, "psY", p))


def step_ffn0(ctx, x, xkeys, xln, xlnkeys, wg_bf, wu_bf, wd_bf, a, lnsc, tag="f0",
              wdram_keys=()):
    nc, k = ctx.nc, ctx.k
    layernorm_fm(ctx, XRes(x, xkeys), xln, xlnkeys, lnsc)
    h = a["h"]
    for half in range(2):
        th0 = half * HALF
        hk = (tag, half, "h")
        for hc in range(NHC):
            for ic in range(NC):
                k.op("scalar", lambda e, hc=hc, ic=ic: e.dma_start_transpose(
                    a["wg"][:, ic, :], wg_bf[hc * 128:(hc + 1) * 128, ic * 128:(ic + 1) * 128]),
                    reads=list(wdram_keys), writes=[(tag, "wg", ic)], dma=True)
                k.op("scalar", lambda e, hc=hc, ic=ic: e.dma_start_transpose(
                    a["wu"][:, ic, :], wu_bf[hc * 128:(hc + 1) * 128, ic * 128:(ic + 1) * 128]),
                    reads=list(wdram_keys), writes=[(tag, "wu", ic)], dma=True)
            for tt in range(0, HALF, FT):
                t0 = th0 + tt
                for ic in range(NC):
                    k.op("tensor", lambda e, hc=hc, ic=ic, t0=t0: e.matmul(
                        a["psG"][:, :], a["wg"][:, ic, :], xln[:, ic, t0:t0 + FT],
                        start=(ic == 0), stop=(ic == NC - 1)),
                        reads=[(tag, "wg", ic), xlnkeys[ic]], writes=[(tag, "psG")])
                k.op("scalar", lambda e: e.activation(
                    out=a["sg"][:, :], in_=a["psG"][:, :], func=AF.Silu, scale=S_DQ),
                    reads=[(tag, "psG")], writes=[(tag, "sg")])
                for ic in range(NC):
                    k.op("tensor", lambda e, hc=hc, ic=ic, t0=t0: e.matmul(
                        a["psU"][:, :], a["wu"][:, ic, :], xln[:, ic, t0:t0 + FT],
                        start=(ic == 0), stop=(ic == NC - 1)),
                        reads=[(tag, "wu", ic), xlnkeys[ic]], writes=[(tag, "psU")])
                k.op("vector", lambda e, hc=hc, tt=tt: e.tensor_mul(
                    out=h[:, hc, tt:tt + FT], in0=a["psU"][:, :], in1=a["sg"][:, :]),
                    reads=[(tag, "psU"), (tag, "sg")], writes=[(hk, hc)])
        for oc in range(NC):
            for hc in range(NHC):
                k.op("scalar", lambda e, oc=oc, hc=hc: e.dma_start_transpose(
                    a["wd"][:, hc, :], wd_bf[oc * 128:(oc + 1) * 128, hc * 128:(hc + 1) * 128]),
                    reads=list(wdram_keys), writes=[(tag, "wd", hc)], dma=True)
            for tt in range(0, HALF, FT):
                t0 = th0 + tt
                for hc in range(NHC):
                    k.op("tensor", lambda e, oc=oc, hc=hc, tt=tt: e.matmul(
                        a["psY"][:, :], a["wd"][:, hc, :], h[:, hc, tt:tt + FT],
                        start=(hc == 0), stop=(hc == NHC - 1)),
                        reads=[(tag, "wd", hc), (hk, hc)], writes=[(tag, "psY")])
                k.op("vector", lambda e, oc=oc, t0=t0: e.scalar_tensor_tensor(
                    out=x[:, oc, t0:t0 + FT], in0=a["psY"][:, :], scalar=S_DQ * S_DQ,
                    in1=x[:, oc, t0:t0 + FT], op0=ALU.mult, op1=ALU.add),
                    reads=[(tag, "psY"), xkeys[oc]], writes=[xkeys[oc]])

"""Causal MHA sublayer (attn_a0 / attn_b0 variants), feature-major.

attn_a0: x += rope_mha(LN(x), W.T) + bo            (rope; weights applied x@W.T)
attn_b0: x += LNaff(mha(LN(x)) @ Wo + bo)          (no rope; gamma/beta post-LN)

Scores: S^T layout [k-part, q-free]; exp(S/8 - M_h) with per-head bound
M_h = (max|q|^2 + max|k|^2)/16; softmax normalizer via ones-column in V2.
xln tensor is reused as O_fm after projections; q/kk reused for b0's y/z.
"""
import numpy as np

NH = 16
HD = 64


def alloc_mha(ctx):
    nc = ctx.nc
    a = {}
    a["q"] = nc.alloc_sbuf_tensor("mha_q", [128, NC, T], BF16)
    a["kk"] = nc.alloc_sbuf_tensor("mha_k", [128, NC, T], BF16)
    a["v2"] = nc.alloc_sbuf_tensor("mha_v2", [128, 16, 65 * (NH // 2)], BF16)
    a["E"] = nc.alloc_sbuf_tensor("mha_E", [128, T], BF16)
    a["wst"] = nc.alloc_sbuf_tensor("mha_wst", [128, NC, 128], BF16)
    a["wstv"] = nc.alloc_sbuf_tensor("mha_wstv", [128, NC, 512], BF16)
    a["bias"] = nc.alloc_sbuf_tensor("mha_bias", [128, NC], F32)
    a["bias2"] = nc.alloc_sbuf_tensor("mha_bias2", [128, NC], F32)
    a["bias3"] = nc.alloc_sbuf_tensor("mha_bias3", [128, NC], F32)
    a["brow"] = nc.alloc_sbuf_tensor("mha_brow", [1, D], BF16)
    a["bbc"] = nc.alloc_sbuf_tensor("mha_bbc", [128, D], BF16)
    a["nrm"] = nc.alloc_sbuf_tensor("mha_nrm", [1, 2 * NH, 4], F32)
    a["mrow"] = nc.alloc_sbuf_tensor("mha_mrow", [1, NH], F32)
    a["mcols"] = nc.alloc_sbuf_tensor("mha_mcols", [128, NH], F32)
    a["srow"] = nc.alloc_sbuf_tensor("mha_srow", [1, T], F32)
    a["gb"] = nc.alloc_sbuf_tensor("mha_gb", [128, 2 * NC], F32)
    a["stage"] = SbufPool(ctx, 2, [128, TT], F32, "mha_stage")
    a["ropeA"] = nc.alloc_sbuf_tensor("mha_ropeA", [128, 1, TT], F32)
    a["ropeB"] = nc.alloc_sbuf_tensor("mha_ropeB", [128, 1, TT], F32)
    a["psS"] = nc.alloc_psum_tensor("mha_psS", [128, 1024], F32)
    a["psO"] = [nc.alloc_psum_tensor(f"mha_psO{i}", [65, TT], F32) for i in range(4)]
    a["triu"], a["triu_key"] = load_const_tile(
        ctx, np.triu(np.ones((128, 128), dtype=BF)), "triu", BF16)
    return a


def rope_tables(ctx, base, f32=False):
    half = HD // 2
    theta = (1.0 / base) ** (np.arange(half, dtype=np.float64) / half)
    ang = np.arange(T, dtype=np.float64)[:, None] * theta[None, :]
    cos = np.cos(ang).T.astype(np.float32)
    sin = np.sin(ang).T.astype(np.float32)
    cos128 = np.tile(cos, (4, 1))
    sinp = np.concatenate([-sin, sin, -sin, sin], axis=0)
    if f32:
        ct, ck = load_const_tile(ctx, cos128, f"ropecf{int(base)}", F32)
        st, sk = load_const_tile(ctx, sinp, f"ropesf{int(base)}", F32)
    else:
        ct, ck = load_const_tile(ctx, cos128.astype(BF), f"ropec{int(base)}", BF16)
        st, sk = load_const_tile(ctx, sinp.astype(BF), f"ropes{int(base)}", BF16)
    return (ct, ck, st, sk)


def _rope_chunk(ctx, a, qt, qkeys, ci, rt, tag):
    nc, k = ctx.nc, ctx.k
    ct, ck, st, sk = rt
    c1, c2 = a["c1"], a["c2"]
    k.op("vector", lambda e, ci=ci: e.tensor_mul(out=c1[:, :], in0=qt[:, ci, :], in1=ct[:, :]),
         reads=[qkeys[ci], ck], writes=[("mc1",)])
    for j in range(4):
        so, do = 32 * (j ^ 1), 32 * j
        k.op("vector", lambda e, ci=ci, so=so, do=do: e.tensor_copy(
            out=c2[do:do + 32, :], in_=qt[so:so + 32, ci, :]),
            reads=[qkeys[ci]], writes=[("mc2",)])
    k.op("vector", lambda e: e.tensor_mul(out=c2[:, :], in0=c2[:, :], in1=st[:, :]),
         reads=[("mc2",), sk], writes=[("mc2",)])
    k.op("vector", lambda e, ci=ci: e.tensor_add(
        out=qt[:, ci, :], in0=c1[:, :], in1=c2[:, :]),
        reads=[("mc1",), ("mc2",)], writes=[qkeys[ci]])


def _proj_to_fm(ctx, a, rhs_t, rhs_keys, w_bf, wdep, out, outkeys, bias_col, bias_key,
                pp, tag, w_T, evac_override=None):
    nc, k = ctx.nc, ctx.k
    for oc in range(NC):
        for ic in range(NC):
            if w_T:
                k.op("scalar", lambda e, oc=oc, ic=ic: e.dma_start_transpose(
                    a["wst"][:, ic, :],
                    w_bf[oc * 128:(oc + 1) * 128, ic * 128:(ic + 1) * 128]),
                    reads=[wdep], writes=[("mwst", ic)], dma=True)
            else:
                k.op("sync", lambda e, oc=oc, ic=ic: e.dma_start(
                    out=a["wst"][:, ic, :],
                    in_=w_bf[ic * 128:(ic + 1) * 128, oc * 128:(oc + 1) * 128]),
                    reads=[wdep], writes=[("mwst", ic)], dma=True)
        for t0 in range(0, T, TT):
            ps, pkey = pp.next()
            for ic in range(NC):
                rr = rhs_t[:, ic, t0:t0 + TT]
                k.op("tensor", lambda e, oc=oc, ic=ic, rr=rr, ps=ps: e.matmul(
                    ps[:, 0:TT], a["wst"][:, ic, :], rr,
                    start=(ic == 0), stop=(ic == NC - 1)),
                    reads=[("mwst", ic), rhs_keys[ic]], writes=[pkey])
            if evac_override is not None:
                evac_override(oc, t0, ps, pkey)
            elif bias_col is not None:
                bc = bias_col[:, oc:oc + 1]
                k.op("scalar", lambda e, oc=oc, t0=t0, ps=ps, bc=bc: e.activation(
                    out=out[:, oc, t0:t0 + TT], in_=ps[:, 0:TT], func=AF.Identity,
                    bias=bc, scale=1.0),
                    reads=[pkey, bias_key], writes=[outkeys[oc]])
            else:
                k.op("scalar", lambda e, oc=oc, t0=t0, ps=ps: e.copy(
                    out=out[:, oc, t0:t0 + TT], in_=ps[:, 0:TT]),
                    reads=[pkey], writes=[outkeys[oc]])


def _maxsq_rows(ctx, a, src, srckeys, off, tag, pp):
    nc, k = ctx.nc, ctx.k
    c1 = a["ropeA"][:, 0, :]
    c1k = ("ropeA",)
    for ci in range(NC):
        for qt in range(4):
            k.op("vector", lambda e, ci=ci, qt=qt: e.tensor_mul(
                out=c1[:, :], in0=src[:, ci, qt * TT:(qt + 1) * TT],
                in1=src[:, ci, qt * TT:(qt + 1) * TT]),
                reads=[srckeys[ci]], writes=[c1k])
            for blk in range(2):
                h = 2 * ci + blk
                lhs = ctx.onescol[64 * blk:64 * blk + 64, :]
                ps, pkey = pp.next()
                rr = c1[64 * blk:64 * blk + 64, :]
                k.op("tensor", lambda e, rr=rr, lhs=lhs, ps=ps: e.matmul(
                    ps[0:1, 0:TT], lhs, rr, start=True, stop=True),
                    reads=[c1k, ctx.onescol_key], writes=[pkey])
                ov = a["nrm"][0:1, off + h, qt:qt + 1]
                k.op("vector", lambda e, ov=ov, ps=ps: e.reduce_max(
                    out=ov, in_=ps[0:1, 0:TT], axis=AX.X),
                    reads=[pkey], writes=[("mnrm",)])


def step_mha(ctx, a, xsrc, xout_writer, xln, xlnkeys, W, lnsc, pp, cfg):
    nc, k = ctx.nc, ctx.k
    tag = ctx.name("mha")
    q, kk, v2, E = a["q"], a["kk"], a["v2"], a["E"]
    qkeys = [("mq", ci) for ci in range(NC)]
    kkeys = [("mk", ci) for ci in range(NC)]
    Ekey = ("mE",)

    layernorm_fm(ctx, xsrc, xln, xlnkeys, lnsc)

    def load_bias_col(name, tile):
        if W.get(name) is None:
            return None, None
        ap, dep = W[name]
        key = ("mbias", id(tile) % 7)
        k.op("sync", lambda e, ap=ap, tile=tile: e.dma_start(out=tile[:, :], in_=ap),
             reads=[dep], writes=[key], dma=True)
        return tile, key

    def mk_rope_evac(out_t, outkeys_t, bcol, bck):
        ct, ck2, st_, sk2 = cfg["rope"]

        def evac(oc, t0, ps, pkey):
            ra = a["ropeA"][:, 0, :]
            rb = a["ropeB"][:, 0, :]
            rak, rbk = ("ropeA",), ("ropeB",)
            reads = [pkey] + ([bck] if bck is not None else [])
            if bcol is not None:
                bc = bcol[:, oc:oc + 1]
                k.op("scalar", lambda e, ps=ps, ra=ra, bc=bc: e.activation(
                    out=ra[:, :], in_=ps[:, 0:TT], func=AF.Identity, bias=bc, scale=1.0),
                    reads=reads, writes=[rak])
            else:
                k.op("scalar", lambda e, ps=ps, ra=ra: e.copy(out=ra[:, :], in_=ps[:, 0:TT]),
                     reads=reads, writes=[rak])
            for j in range(4):
                so, do = 32 * (j ^ 1), 32 * j
                k.op("vector", lambda e, ra=ra, rb=rb, so=so, do=do: e.tensor_copy(
                    out=rb[do:do + 32, :], in_=ra[so:so + 32, :]),
                    reads=[rak], writes=[rbk])
            k.op("vector", lambda e, ra=ra, t0=t0: e.tensor_mul(
                out=ra[:, :], in0=ra[:, :], in1=ct[:, t0:t0 + TT]),
                reads=[rak, ck2], writes=[rak])
            k.op("vector", lambda e, rb=rb, t0=t0: e.tensor_mul(
                out=rb[:, :], in0=rb[:, :], in1=st_[:, t0:t0 + TT]),
                reads=[rbk, sk2], writes=[rbk])
            k.op("vector", lambda e, oc=oc, t0=t0, ra=ra, rb=rb: e.tensor_add(
                out=out_t[:, oc, t0:t0 + TT], in0=ra[:, :], in1=rb[:, :]),
                reads=[rak, rbk], writes=[outkeys_t[oc]])
        return evac

    bq, bqk = load_bias_col("bq", a["bias"])
    if cfg.get("rope") is not None:
        _proj_to_fm(ctx, a, xln, xlnkeys, W["q"][0], W["q"][2], None, None, None, None,
                    pp, tag + "q", W["q"][1], evac_override=mk_rope_evac(q, qkeys, bq, bqk))
        bk, bkk = load_bias_col("bk", a["bias2"])
        _proj_to_fm(ctx, a, xln, xlnkeys, W["k"][0], W["k"][2], None, None, None, None,
                    pp, tag + "k", W["k"][1], evac_override=mk_rope_evac(kk, kkeys, bk, bkk))
    else:
        _proj_to_fm(ctx, a, xln, xlnkeys, W["q"][0], W["q"][2], q, qkeys, bq, bqk,
                    pp, tag + "q", W["q"][1])
        bk, bkk = load_bias_col("bk", a["bias2"])
        _proj_to_fm(ctx, a, xln, xlnkeys, W["k"][0], W["k"][2], kk, kkeys, bk, bkk,
                    pp, tag + "k", W["k"][1])

    _maxsq_rows(ctx, a, q, qkeys, 0, tag + "nq", pp)
    _maxsq_rows(ctx, a, kk, kkeys, NH, tag + "nk", pp)
    nrm_rd = [("mnrm",)]
    k.op("vector", lambda e: e.reduce_max(
        out=a["nrm"][0:1, :, 0:1], in_=a["nrm"][0:1, :, :], axis=AX.X),
        reads=nrm_rd, writes=[("mnrm",)])
    k.op("vector", lambda e: e.scalar_tensor_tensor(
        out=a["mrow"][0:1, :], in0=a["nrm"][0:1, 0:NH, 0], scalar=0.0,
        in1=a["nrm"][0:1, NH:2 * NH, 0], op0=ALU.add, op1=ALU.add),
        reads=[("mnrm",)], writes=[("mrow0",)])
    k.op("vector", lambda e: e.tensor_scalar(
        out=a["mrow"][0:1, :], in0=a["mrow"][0:1, :], scalar1=-1.0 / 16.0, scalar2=None,
        op0=ALU.mult),
        reads=[("mrow0",)], writes=[("mrowF",)])
    for h in range(NH):
        ps, pkey = pp.next()
        rr = a["mrow"][0:1, h:h + 1]
        k.op("tensor", lambda e, rr=rr, ps=ps: e.matmul(
            ps[0:128, 0:1], ctx.ones1_128[:, :], rr, start=True, stop=True),
            reads=[("mrowF",), ctx.ones1_128_key], writes=[pkey])
        ov = a["mcols"][:, h:h + 1]
        k.op("vector", lambda e, ov=ov, ps=ps: e.tensor_copy(out=ov, in_=ps[0:128, 0:1]),
             reads=[pkey], writes=[("mcol", h)])

    if W.get("bv") is not None:
        ap, dep = W["bv"]
        k.op("gpsimd", lambda e, ap=ap: e.dma_start(out=a["brow"][:, :], in_=ap),
             reads=[dep], writes=[("mbrow",)], dma=True)
        bcast_rows(ctx, a["brow"], ("mbrow",), a["bbc"], ("mbbc"), D, pp)
        bbc_rd = [(("mbbc"), t0) for t0 in range(0, D, TT)]
    else:
        bbc_rd = None
    wv_bf, wv_T, wv_dep = W["v"]
    O_keys = xlnkeys
    psO = a["psO"]
    for halfo in range(2):
        o0 = halfo * 512
        k.op("vector", lambda e: e.memset(
            v2.rearrange("p c (h x) -> p c h x", x=65)[:, :, :, 64:65], 1.0),
            writes=[("mv2o",)])
        for ic in range(NC):
            if wv_T:
                for j in range(4):
                    k.op("scalar", lambda e, ic=ic, j=j, o0=o0: e.dma_start_transpose(
                        a["wstv"][:, ic, j * 128:(j + 1) * 128],
                        wv_bf[o0 + j * 128:o0 + (j + 1) * 128, ic * 128:(ic + 1) * 128]),
                        reads=[wv_dep], writes=[("mwstv", ic)], dma=True)
            else:
                k.op("sync", lambda e, ic=ic, o0=o0: e.dma_start(
                    out=a["wstv"][:, ic, :], in_=wv_bf[ic * 128:(ic + 1) * 128, o0:o0 + 512]),
                    reads=[wv_dep], writes=[("mwstv", ic)], dma=True)
        for tc in range(16):
            ps, pkey = pp.next()
            for ic in range(NC):
                rr = xln[:, ic, tc * 128:(tc + 1) * 128]
                k.op("tensor", lambda e, ic=ic, rr=rr, ps=ps: e.matmul(
                    ps[:, 0:512], rr, a["wstv"][:, ic, :],
                    start=(ic == 0), stop=(ic == NC - 1)),
                    reads=[("mwstv", ic), xlnkeys[ic]], writes=[pkey])
            outv = v2[:, tc, :].rearrange("p (h x) -> p h x", x=65)[:, :, 0:64]
            psv = ps[:, 0:512].rearrange("p (h x) -> p h x", x=64)
            if bbc_rd is not None:
                bv_sl = a["bbc"][:, o0:o0 + 512].rearrange("p (h x) -> p h x", x=64)
                k.op("vector", lambda e, outv=outv, psv=psv, bv_sl=bv_sl: e.tensor_add(
                    out=outv, in0=psv, in1=bv_sl),
                    reads=[pkey] + bbc_rd, writes=[("mv2", tc)])
            else:
                k.op("vector", lambda e, outv=outv, psv=psv: e.tensor_copy(
                    out=outv, in_=psv), reads=[pkey], writes=[("mv2", tc)])
        _mha_heads(ctx, a, cfg, tag, q, qkeys, kk, kkeys, v2, E, Ekey, qkeys,
                   psO, pp, range(8 * halfo, 8 * halfo + 8), halfo)

    _mha_post(ctx, a, cfg, tag, q, qkeys, kk, kkeys, xln, xlnkeys, W, lnsc, pp,
              xout_writer, load_bias_col)


def _mha_heads(ctx, a, cfg, tag, q, qkeys, kk, kkeys, v2, E, Ekey, O_keys,
               psO, pp, heads, halfo):
    nc, k = ctx.nc, ctx.k
    for h in heads:
        hh = h - 8 * halfo
        hc, blk = h // 2, h % 2
        b0 = 64 * blk
        for kc in range(16):
            qlo = kc * 128
            qlo_f = (qlo // TT) * TT
            for q0 in range(qlo_f, T, 1024):
                wq = min(1024, T - q0)
                for qq in range(q0, q0 + wq, TT):
                    lhs = kk[b0:b0 + 64, hc, kc * 128:(kc + 1) * 128]
                    rr = q[b0:b0 + 64, hc, qq:qq + TT]
                    k.op("tensor", lambda e, lhs=lhs, rr=rr, qq=qq, q0=q0: e.matmul(
                        a["psS"][:, qq - q0:qq - q0 + TT], lhs, rr, start=True, stop=True),
                        reads=[kkeys[hc], qkeys[hc]], writes=[("mpsS",)])
                mc = a["mcols"][:, h:h + 1]
                k.op("scalar", lambda e, q0=q0, wq=wq, mc=mc: e.activation(
                    out=E[:, q0:q0 + wq], in_=a["psS"][:, 0:wq], func=AF.Exp,
                    scale=0.125, bias=mc),
                    reads=[("mpsS",), ("mcol", h)], writes=[Ekey])
            if qlo > qlo_f:
                k.op("vector", lambda e, qlo_f=qlo_f, qlo=qlo: e.memset(
                    E[:, qlo_f:qlo], 0.0), writes=[Ekey])
            k.op("vector", lambda e, qlo=qlo: e.tensor_mul(
                out=E[:, qlo:qlo + 128], in0=E[:, qlo:qlo + 128], in1=a["triu"][:, :]),
                reads=[Ekey, a["triu_key"]], writes=[Ekey])
            lhs_v = v2[:, kc, 65 * hh:65 * hh + 65]
            for qt in range(qlo_f // TT, 4):
                rr = E[:, qt * TT:(qt + 1) * TT]
                k.op("tensor", lambda e, lhs_v=lhs_v, rr=rr, qt=qt, kc=kc: e.matmul(
                    psO[qt][:, :], lhs_v, rr,
                    start=(kc == 0), stop=(kc == min(qt * 4 + 3, 15))),
                    reads=[("mv2", kc), ("mv2o",), Ekey],
                    writes=[("mpsO", qt)])
        for qt in range(4):
            t0 = qt * TT
            sv = a["srow"][0:1, t0:t0 + TT]
            k.op("vector", lambda e, sv=sv, qt=qt: e.tensor_copy(
                out=sv, in_=psO[qt][64:65, :]),
                reads=[("mpsO", qt)], writes=[("msrow",)])
            k.op("vector", lambda e, sv=sv: e.reciprocal(out=sv, in_=sv),
                 reads=[("msrow",)], writes=[("msrow",)])
            ps, pkey = pp.next()
            outp = ps[b0:b0 + 64, 0:TT]
            k.op("tensor", lambda e, sv=sv, outp=outp, b0=b0: e.matmul(
                outp, ctx.ones1_128[:, 0:64], sv, start=True, stop=True,
                tile_position=(0, b0)),
                reads=[("msrow",), ctx.ones1_128_key], writes=[pkey])
            st1, k1 = a["stage"].next()
            k.op("scalar", lambda e, st1=st1, qt=qt, b0=b0: e.copy(
                out=st1[b0:b0 + 64, :], in_=psO[qt][0:64, :]),
                reads=[("mpsO", qt)], writes=[k1])
            ov = q[b0:b0 + 64, hc, t0:t0 + TT]
            k.op("vector", lambda e, ov=ov, st1=st1, ps=ps, b0=b0: e.tensor_mul(
                out=ov, in0=st1[b0:b0 + 64, :], in1=ps[b0:b0 + 64, 0:TT]),
                reads=[k1, pkey], writes=[O_keys[hc]])

def _mha_post(ctx, a, cfg, tag, q, qkeys, kk, kkeys, xln, xlnkeys, W, lnsc, pp,
              xout_writer, load_bias_col):
    nc, k = ctx.nc, ctx.k
    if cfg["post"] == "res":
        bo, bok = load_bias_col("bo", a["bias3"])

        def evac(oc, t0, ps, pkey):
            xout_writer(oc, t0, ps, pkey, (bo, bok, oc))
        _proj_to_fm(ctx, a, q, qkeys, W["o"][0], W["o"][2], None, None, None, None,
                    pp, tag + "o", W["o"][1], evac_override=evac)
    else:
        bo, bok = load_bias_col("bo", a["bias3"])
        ykeys = []
        for oc in range(NC):
            yk = ("my", oc)
            ykeys.append(yk)

        def evacy(oc, t0, ps, pkey):
            bc = bo[:, oc:oc + 1]
            k.op("scalar", lambda e, oc=oc, t0=t0, ps=ps, bc=bc: e.activation(
                out=xln[:, oc, t0:t0 + TT], in_=ps[:, 0:TT], func=AF.Identity,
                bias=bc, scale=1.0),
                reads=[pkey, bok], writes=[xlnkeys[oc], ykeys[oc]])
        _proj_to_fm(ctx, a, q, qkeys, W["o"][0], W["o"][2], None, None, None, None,
                    pp, tag + "o", W["o"][1], evac_override=evacy)
        zkeys = [("mz", ci) for ci in range(NC)]
        for ci in range(NC):
            k.op("vector", lambda e, ci=ci: e.memset(kk[0:1, ci, 0:1], 0.0),
                 writes=[kkeys[ci], zkeys[ci]])
        layernorm_fm(ctx, XRes(xln, ykeys), kk, zkeys, lnsc)
        gbap, gbdep = cfg["gamma_beta"]
        k.op("sync", lambda e: e.dma_start(out=a["gb"][:, :], in_=gbap),
             reads=[gbdep], writes=[("mgb",)], dma=True)
        for oc in range(NC):
            g = a["gb"][:, oc:oc + 1]
            b = a["gb"][:, NC + oc:NC + oc + 1]
            k.op("vector", lambda e, oc=oc, g=g, b=b: e.tensor_scalar(
                out=kk[:, oc, :], in0=kk[:, oc, :], scalar1=g, scalar2=b,
                op0=ALU.mult, op1=ALU.add),
                reads=[zkeys[oc], ("mgb",)], writes=[zkeys[oc], kkeys[oc]])
        for oc in range(NC):
            for t0 in range(0, T, TT):
                xout_writer(oc, t0, None, None, ("z", kk, zkeys[oc]))

"""attn_a1 (cosine-sim counter block): x += concat([d, counter]) @ W2 where
d = LN(x), normed = d/|d|, sim = normed @ normed.T,
counter = softplus(rowsum(relu(sim)) @ W1 + b1).

Needs from caller: xln tensor (+keys) for d; a dict with scratch reused from
alloc_mha (q -> normed storage, kk -> counter storage, E, psS, srow/c1...).
W2 stored [2048, 1024] (lhsT as-is); W1 supplied as [1, 1024] row; b1 as
[128, 8] cols.
"""
import numpy as np


def step_a1(ctx, a, xsrc, xout_writer, xln, xlnkeys, wp, lnsc, pp, psC):
    nc, k = ctx.nc, ctx.k
    nrm = a["q"]
    nkeys = [("mq", ci) for ci in range(NC)]
    cnt = a["kk"]
    ckeys = [("mk", ci) for ci in range(NC)]
    E, Ekey = a["E"], ("mE",)
    psS = a["psS"]

    layernorm_fm(ctx, xsrc, xln, xlnkeys, lnsc)

    srow = a["srow"]
    reduce_feat(ctx, XRes(xln, xlnkeys), NC, srow, "a1sq", pp, square=(a["c1"][0:128, 0:TT], ("mc1",)))
    sq_rd = [("a1sq", t0) for t0 in range(0, T, TT)]
    k.op("vector", lambda e: e.tensor_scalar(
        out=srow[0:1, :], in0=srow[0:1, :], scalar1=1e-12, scalar2=None, op0=ALU.max),
        reads=sq_rd, writes=sq_rd)
    k.op("scalar", lambda e: e.activation(out=srow[0:1, :], in_=srow[0:1, :], func=AF.Sqrt),
         reads=sq_rd, writes=sq_rd)
    k.op("vector", lambda e: e.reciprocal(out=srow[0:1, :], in_=srow[0:1, :]),
         reads=sq_rd, writes=sq_rd)
    rbc, rbc_k = lnsc["rbc"]
    bcast_rows(ctx, srow, sq_rd[0], rbc, "a1rbc", T, pp)
    rbc_rd = [("a1rbc", t0) for t0 in range(0, T, TT)]
    for ci in range(NC):
        k.op("vector", lambda e, ci=ci: e.tensor_mul(
            out=nrm[:, ci, :], in0=xln[:, ci, :], in1=rbc[:, :]),
            reads=[xlnkeys[ci]] + rbc_rd, writes=[nkeys[ci]])

    for kc in range(16):
        for q0 in range(0, T, 1024):
            for qq in range(q0, q0 + 1024, TT):
                for ic in range(NC):
                    lhs = nrm[:, ic, kc * 128:(kc + 1) * 128]
                    rr = nrm[:, ic, qq:qq + TT]
                    k.op("tensor", lambda e, lhs=lhs, rr=rr, qq=qq, q0=q0, ic=ic: e.matmul(
                        psS[:, qq - q0:qq - q0 + TT], lhs, rr,
                        start=(ic == 0), stop=(ic == NC - 1)),
                        reads=[nkeys[ic]], writes=[("mpsS",)])
            k.op("scalar", lambda e, q0=q0: e.activation(
                out=E[:, q0:q0 + 1024], in_=psS[:, 0:1024], func=AF.Relu),
                reads=[("mpsS",)], writes=[Ekey])
        for t0 in range(0, T, TT):
            rr = E[:, t0:t0 + TT]
            k.op("tensor", lambda e, rr=rr, kc=kc, t0=t0: e.matmul(
                psC[0:1, t0:t0 + TT], ctx.onescol_b[:, :], rr,
                start=(kc == 0), stop=(kc == 15)),
                reads=[Ekey, ctx.onescol_b_key], writes=[("a1psC", t0)])
    for t0 in range(0, T, TT):
        k.op("vector", lambda e, t0=t0: e.tensor_copy(
            out=srow[0:1, t0:t0 + TT], in_=psC[0:1, t0:t0 + TT]),
            reads=[("a1psC", t0)], writes=[("a1cr", t0)])
    cr_rd = [("a1cr", t0) for t0 in range(0, T, TT)]

    w1ap, w1dep = wp["w1row"]
    b1ap, b1dep = wp["b1"]
    k.op("gpsimd", lambda e: e.dma_start(out=a["brow"][:, :], in_=w1ap),
         reads=[w1dep], writes=[("mbrow",)], dma=True)
    k.op("sync", lambda e: e.dma_start(out=a["bias"][:, :], in_=b1ap),
         reads=[b1dep], writes=[("mbias1",)], dma=True)
    c1 = a["c1"]
    for dc in range(NC):
        for t0 in range(0, T, TT):
            ps, pkey = pp.next()
            lhs = a["brow"][0:1, dc * 128:(dc + 1) * 128]
            rr = srow[0:1, t0:t0 + TT]
            k.op("tensor", lambda e, lhs=lhs, rr=rr, ps=ps: e.matmul(
                ps[:, 0:TT], lhs, rr, start=True, stop=True),
                reads=[("mbrow",), ("a1cr", t0)], writes=[pkey])
            st1, k1 = a["stage"].next()
            st2, k2 = a["stage"].next()
            bc = a["bias"][:, dc:dc + 1]
            k.op("scalar", lambda e, ps=ps, st1=st1, bc=bc: e.activation(
                out=st1[:, :], in_=ps[:, 0:TT], func=AF.Identity, bias=bc, scale=1.0),
                reads=[pkey, ("mbias1",)], writes=[k1])
            k.op("scalar", lambda e, st1=st1, st2=st2: e.activation(
                out=st2[:, :], in_=st1[:, :], func=AF.Abs),
                reads=[k1], writes=[k2])
            k.op("scalar", lambda e, st2=st2: e.activation(
                out=st2[:, :], in_=st2[:, :], func=AF.Exp, scale=-1.0),
                reads=[k2], writes=[k2])
            k.op("scalar", lambda e, st2=st2: e.activation(
                out=st2[:, :], in_=st2[:, :], func=AF.Ln, bias=1.0, scale=1.0),
                reads=[k2, ctx.sconst(1.0)], writes=[k2])
            k.op("vector", lambda e, st1=st1: e.tensor_relu(out=st1[:, :], in_=st1[:, :]),
                 reads=[k1], writes=[k1])
            ov = cnt[:, dc, t0:t0 + TT]
            k.op("vector", lambda e, ov=ov, st1=st1, st2=st2: e.tensor_add(
                out=ov, in0=st1[:, :], in1=st2[:, :]),
                reads=[k1, k2], writes=[ckeys[dc]])

    w2ap, w2dep = wp["w2"]
    wst = a["wst"]
    for oc in range(NC):
        for ic in range(NC):
            k.op("sync", lambda e, oc=oc, ic=ic: e.dma_start(
                out=wst[:, ic, :], in_=w2ap[ic * 128:(ic + 1) * 128, oc * 128:(oc + 1) * 128]),
                reads=[w2dep], writes=[("mwst", ic)], dma=True)
            k.op("sync", lambda e, oc=oc, ic=ic: e.dma_start(
                out=a["wstv"][:, ic, 0:128],
                in_=w2ap[1024 + ic * 128:1024 + (ic + 1) * 128, oc * 128:(oc + 1) * 128]),
                reads=[w2dep], writes=[("mwstv", ic)], dma=True)
        for t0 in range(0, T, TT):
            ps, pkey = pp.next()
            for ic in range(NC):
                rr = xln[:, ic, t0:t0 + TT]
                k.op("tensor", lambda e, ic=ic, rr=rr, ps=ps: e.matmul(
                    ps[:, 0:TT], wst[:, ic, :], rr, start=(ic == 0), stop=False),
                    reads=[("mwst", ic), xlnkeys[ic]], writes=[pkey])
            for ic in range(NC):
                rr = cnt[:, ic, t0:t0 + TT]
                k.op("tensor", lambda e, ic=ic, rr=rr, ps=ps: e.matmul(
                    ps[:, 0:TT], a["wstv"][:, ic, 0:128], rr,
                    start=False, stop=(ic == NC - 1)),
                    reads=[("mwstv", ic), ckeys[ic]], writes=[pkey])
            xout_writer(oc, t0, ps, pkey, None)

"""attn_b1: x += xi_final  where
  xi0 = LN(x); xi1 = xi0 + G(rms(xi0, n1w)); xi2 = xi1 + L(rms(xi1, n2w))
G = global latent attention (16 q-heads, 4 kv-heads via latent 256, rope base
1e6, qk_norm, causal); L = local GQA (window 256, rope base 1e4, qk_norm).

Permutation to [e|o] rope basis applied post-projection via a constant
block-diag permutation matmul (P128).
"""
import numpy as np

NH = 16
KVH = 4
HD = 64
EPS_B1 = 1e-6
WIN = 256


def perm128_const(ctx):
    P = np.zeros((128, 128), dtype=BF)
    for h in range(2):
        for j in range(32):
            P[64 * h + 2 * j, 64 * h + j] = 1
            P[64 * h + 2 * j + 1, 64 * h + 32 + j] = 1
    t, key = load_const_tile(ctx, P, "P128", BF16)
    return t, key


def _permute_chunks(ctx, a, tens, keys, chunks, pp, Pt, Pk):
    nc, k = ctx.nc, ctx.k
    for ci in chunks:
        for t0 in range(0, T, TT):
            ps, pkey = pp.next()
            rr = tens[:, ci, t0:t0 + TT]
            k.op("tensor", lambda e, rr=rr, ps=ps: e.matmul(
                ps[:, 0:TT], Pt[:, :], rr, start=True, stop=True),
                reads=[keys[ci], Pk], writes=[pkey])
            k.op("vector", lambda e, rr=rr, ps=ps: e.tensor_copy(out=rr, in_=ps[:, 0:TT]),
                 reads=[pkey], writes=[keys[ci]])


def _qknorm(ctx, a, tens, keys, chunks, pp, tag):
    nc, k = ctx.nc, ctx.k
    srow = a["srow"]
    for ci in chunks:
        k.op("vector", lambda e, ci=ci: e.tensor_mul(
            out=a["c1"][:, :], in0=tens[:, ci, :], in1=tens[:, ci, :]),
            reads=[keys[ci]], writes=[("mc1",)])
        for blk in range(2):
            b0 = 64 * blk
            lhs = ctx.onescol_b[b0:b0 + 64, :]
            for t0 in range(0, T, TT):
                ps, pkey = pp.next()
                rr = a["c1"][b0:b0 + 64, t0:t0 + TT]
                k.op("tensor", lambda e, rr=rr, lhs=lhs, ps=ps: e.matmul(
                    ps[0:1, 0:TT], lhs, rr, start=True, stop=True),
                    reads=[("mc1",), ctx.onescol_b_key], writes=[pkey])
                sv = srow[0:1, t0:t0 + TT]
                k.op("vector", lambda e, sv=sv, ps=ps: e.tensor_scalar(
                    out=sv, in0=ps[0:1, 0:TT], scalar1=1.0 / HD, scalar2=None, op0=ALU.mult),
                    reads=[pkey], writes=[("msrow",)])
            k.op("scalar", lambda e: e.activation(out=srow[0:1, :], in_=srow[0:1, :],
                                                  func=AF.Sqrt, bias=EPS_B1, scale=1.0),
                 reads=[("msrow",), ctx.sconst(EPS_B1)], writes=[("msrow",)])
            k.op("vector", lambda e: e.reciprocal(out=srow[0:1, :], in_=srow[0:1, :]),
                 reads=[("msrow",)], writes=[("msrow",)])
            for t0 in range(0, T, TT):
                ps, pkey = pp.next()
                rr = srow[0:1, t0:t0 + TT]
                outp = ps[b0:b0 + 64, 0:TT]
                k.op("tensor", lambda e, rr=rr, outp=outp, b0=b0: e.matmul(
                    outp, ctx.ones1_128[:, 0:64], rr, start=True, stop=True,
                    tile_position=(0, b0)),
                    reads=[("msrow",), ctx.ones1_128_key], writes=[pkey])
                ov = tens[b0:b0 + 64, ci, t0:t0 + TT]
                k.op("vector", lambda e, ov=ov, ps=ps, b0=b0: e.tensor_mul(
                    out=ov, in0=ov, in1=ps[b0:b0 + 64, 0:TT]),
                    reads=[pkey, keys[ci]], writes=[keys[ci]])


def _rmsnorm_w(ctx, xi, xikeys, out, outkeys, wcol, wcol_key, lnsc, pp):
    nc, k = ctx.nc, ctx.k
    murow, murow_k = lnsc["murow"]
    rbc, rbc_k = lnsc["rbc"]
    reduce_feat(ctx, XRes(xi, xikeys), NC, murow, murow_k, pp,
                square=lnsc["sq"], scale=1.0 / D)
    mr = [(murow_k, t0) for t0 in range(0, T, TT)]
    k.op("scalar", lambda e: e.activation(out=murow[0:1, :], in_=murow[0:1, :],
                                          func=AF.Sqrt, bias=EPS_B1, scale=1.0),
         reads=mr + [ctx.sconst(EPS_B1)], writes=mr)
    k.op("vector", lambda e: e.reciprocal(out=murow[0:1, :], in_=murow[0:1, :]),
         reads=mr, writes=mr)
    bcast_rows(ctx, murow, lambda t0: (murow_k, t0), rbc, rbc_k, T, pp)
    for ci in range(NC):
        for t0 in range(0, T, TT):
            ov = out[:, ci, t0:t0 + TT]
            xa = xi[:, ci, t0:t0 + TT]
            wc = wcol[:, ci:ci + 1]
            k.op("vector", lambda e, ov=ov, xa=xa, wc=wc, t0=t0: e.scalar_tensor_tensor(
                out=ov, in0=xa, scalar=wc, in1=rbc[:, t0:t0 + TT],
                op0=ALU.mult, op1=ALU.mult),
                reads=[xikeys[ci], (rbc_k, t0), wcol_key], writes=[outkeys[ci]])


def _attn_heads_b1(ctx, a, q, qkeys, kk, kkeys, v2, kv_of_h, E, Ekey, pp, psS, psO,
                   window=None):
    nc, k = ctx.nc, ctx.k
    O, O_keys = a["o"], a["okeys"]
    ks = a["kstage"]
    for h in range(NH):
        hc, blk = h // 2, h % 2
        b0 = 64 * blk
        g = h // (NH // KVH)
        gc, gblk = g // 2, g % 2
        gb0 = 64 * gblk
        if h % (NH // KVH) == 0:
            for dst0 in range(0, 128, 64):
                src = kk[gb0:gb0 + 64, gc, :]
                k.op("vector", lambda e, src=src, dst0=dst0: e.tensor_copy(
                    out=ks[dst0:dst0 + 64, :], in_=src),
                    reads=[kkeys[gc]], writes=[("mkstage",)])
        for kc in range(16):
            qlo = kc * 128
            if window is None:
                qhi = T
            else:
                qhi = min(qlo + 128 + window - 1, T)
            qlo_f = (qlo // TT) * TT
            qhi_f = min(ceil_div(qhi, TT) * TT, T)
            for q0 in range(qlo_f, qhi_f, 1024):
                wq = min(1024, qhi - q0)
                if wq <= 0:
                    continue
                for qq in range(q0, min(q0 + 1024, qhi), TT):
                    wqq = min(TT, qhi - qq)
                    lhs = ks[b0:b0 + 64, kc * 128:(kc + 1) * 128]
                    rr = q[b0:b0 + 64, hc, qq:qq + wqq]
                    k.op("tensor", lambda e, lhs=lhs, rr=rr, qq=qq, q0=q0, wqq=wqq: e.matmul(
                        psS[:, qq - q0:qq - q0 + wqq], lhs, rr, start=True, stop=True),
                        reads=[("mkstage",), qkeys[hc]], writes=[("mpsS",)])
                k.op("scalar", lambda e, q0=q0, wq=wq: e.activation(
                    out=E[:, q0:q0 + wq], in_=psS[:, 0:wq], func=AF.Exp, scale=0.125),
                    reads=[("mpsS",)], writes=[Ekey])
            if qlo > qlo_f:
                k.op("vector", lambda e, qlo_f=qlo_f, qlo=qlo: e.memset(
                    E[:, qlo_f:qlo], 0.0), writes=[Ekey])
            if window is not None and qhi < qhi_f:
                k.op("vector", lambda e, qhi=qhi, qhi_f=qhi_f: e.memset(
                    E[:, qhi:qhi_f], 0.0), writes=[Ekey])
            if window is None:
                k.op("vector", lambda e, qlo=qlo: e.tensor_mul(
                    out=E[:, qlo:qlo + 128], in0=E[:, qlo:qlo + 128], in1=a["triu"][:, :]),
                    reads=[Ekey, a["triu_key"]], writes=[Ekey])
            else:
                wm = min(qhi - qlo, 128 + window - 1)
                k.op("vector", lambda e, qlo=qlo, wm=wm: e.tensor_mul(
                    out=E[:, qlo:qlo + wm], in0=E[:, qlo:qlo + wm],
                    in1=a["wmask"][:, 0:wm]),
                    reads=[Ekey, a["wmask_key"]], writes=[Ekey])
            lhs_v = v2[:, kc, 65 * g:65 * g + 65]
            for qt in range(qlo_f // TT, ceil_div(qhi_f, TT)):
                if window is None:
                    first_kc, last_kc = 0, min(qt * 4 + 3, 15)
                else:
                    kcs = [c for c in range(16)
                           if c * 128 < (qt + 1) * TT and min(c * 128 + 128 + window - 1, T) > qt * TT]
                    first_kc, last_kc = kcs[0], kcs[-1]
                rr = E[:, qt * TT:(qt + 1) * TT]
                k.op("tensor", lambda e, lhs_v=lhs_v, rr=rr, qt=qt, kc=kc,
                     first_kc=first_kc, last_kc=last_kc: e.matmul(
                    psO[qt][:, :], lhs_v, rr,
                    start=(kc == first_kc), stop=(kc == last_kc)),
                    reads=[("mv2", kc), ("mv2o",), Ekey],
                    writes=[("mpsO", qt)])
        srow = a["srow"]
        for qt in range(4):
            ov = O[b0:b0 + 64, hc, qt * TT:(qt + 1) * TT]
            k.op("scalar", lambda e, ov=ov, qt=qt: e.copy(out=ov, in_=psO[qt][0:64, :]),
                 reads=[("mpsO", qt)], writes=[O_keys[hc]])
            sv = srow[0:1, qt * TT:(qt + 1) * TT]
            k.op("vector", lambda e, sv=sv, qt=qt: e.tensor_copy(
                out=sv, in_=psO[qt][64:65, :]),
                reads=[("mpsO", qt)], writes=[("msrow",)])
        k.op("vector", lambda e: e.reciprocal(out=srow[0:1, :], in_=srow[0:1, :]),
             reads=[("msrow",)], writes=[("msrow",)])
        for t0 in range(0, T, TT):
            ps, pkey = pp.next()
            rr = srow[0:1, t0:t0 + TT]
            outp = ps[b0:b0 + 64, 0:TT]
            k.op("tensor", lambda e, rr=rr, outp=outp, b0=b0: e.matmul(
                outp, ctx.ones1_128[:, 0:64], rr, start=True, stop=True,
                tile_position=(0, b0)),
                reads=[("msrow",), ctx.ones1_128_key], writes=[pkey])
            ov = O[b0:b0 + 64, hc, t0:t0 + TT]
            k.op("vector", lambda e, ov=ov, ps=ps, b0=b0: e.tensor_mul(
                out=ov, in0=ov, in1=ps[b0:b0 + 64, 0:TT]),
                reads=[pkey, O_keys[hc]], writes=[O_keys[hc]])


def _proj_generic(ctx, a, rhs_t, rhs_keys, n_in, w_bf, wdep, out, outkeys, n_out,
                  pp, accum_into_x=None):
    nc, k = ctx.nc, ctx.k
    wst = a["wst"]
    for oc in range(n_out):
        for ic in range(n_in):
            k.op("sync", lambda e, oc=oc, ic=ic: e.dma_start(
                out=wst[:, ic, :], in_=w_bf[ic * 128:(ic + 1) * 128, oc * 128:(oc + 1) * 128]),
                reads=[wdep], writes=[("mwst", ic)], dma=True)
        for t0 in range(0, T, TT):
            ps, pkey = pp.next()
            for ic in range(n_in):
                rr = rhs_t[:, ic, t0:t0 + TT]
                k.op("tensor", lambda e, ic=ic, rr=rr, ps=ps: e.matmul(
                    ps[:, 0:TT], wst[:, ic, :], rr, start=(ic == 0), stop=(ic == n_in - 1)),
                    reads=[("mwst", ic), rhs_keys[ic]], writes=[pkey])
            if accum_into_x is not None:
                x, xkeys = accum_into_x
                ov = x[:, oc, t0:t0 + TT]
                k.op("vector", lambda e, ov=ov, ps=ps: e.tensor_add(
                    out=ov, in0=ps[:, 0:TT], in1=ov),
                    reads=[pkey, xkeys[oc]], writes=[xkeys[oc]])
            else:
                ov = out[:, oc, t0:t0 + TT]
                k.op("scalar", lambda e, ov=ov, ps=ps: e.copy(out=ov, in_=ps[:, 0:TT]),
                     reads=[pkey], writes=[outkeys[oc]])


def step_b1(ctx, a, xsrc, xout_writer, W, lnsc, pp):
    nc, k = ctx.nc, ctx.k
    xi, xikeys = a["xi"], a["xikeys"]
    xr, xrkeys = a["xr"], a["xrkeys"]
    q, qkeys = a["q"], [("mq", ci) for ci in range(NC)]
    kk, kkeys = a["kk"], [("mk", ci) for ci in range(NC)]
    lat, latkeys = a["lat"], [("mlat", ci) for ci in range(2)]
    v2, E, Ekey = a["v2g"], a["E"], ("mE",)
    psS, psO = a["psS"], a["psO"]
    Pt, Pk = a["P128"]

    layernorm_fm(ctx, xsrc, xi, xikeys, lnsc)
    mubc_k = lnsc["mubc"][1]
    k.op("vector", lambda e: e.memset(a["E"][0:1, 0:1], 0.0),
         writes=[("mE",)] + [(mubc_k, t0) for t0 in range(0, T, TT)])
    varrow_k = lnsc["varrow"][1]
    k.op("vector", lambda e: e.memset(a["srow"][0:1, 0:1], 0.0),
         writes=[("msrow",)] + [(varrow_k, t0) for t0 in range(0, T, TT)])

    for nm, tile, key in [("n1w", a["bias"], ("mbias1",)), ("n2w", a["bias2"], ("mbias2",))]:
        ap, dep = W[nm]
        k.op("sync", lambda e, ap=ap, tile=tile: e.dma_start(out=tile[:, :], in_=ap),
             reads=[dep], writes=[key], dma=True)

    _rmsnorm_w(ctx, xi, xikeys, xr, xrkeys, a["bias"], ("mbias1",), lnsc, pp)
    _proj_generic(ctx, a, xr, xrkeys, NC, W["g_wq"][0], W["g_wq"][1], q, qkeys, NC, pp)
    _proj_generic(ctx, a, xr, xrkeys, NC, W["g_kv_down"][0], W["g_kv_down"][1],
                  lat, latkeys, 2, pp)
    _proj_generic(ctx, a, lat, latkeys, 2, W["g_k_up"][0], W["g_k_up"][1], kk, kkeys, 2, pp)
    k.op("vector", lambda e: e.memset(
        v2.rearrange("p c (h x) -> p c h x", x=65)[:, :, :, 64:65], 1.0),
        writes=[("mv2o",)])
    wstv = a["wstv"]
    for ic in range(2):
        k.op("sync", lambda e, ic=ic: e.dma_start(
            out=wstv[:, ic, 0:256], in_=W["g_v_up"][0][ic * 128:(ic + 1) * 128, :]),
            reads=[W["g_v_up"][1]], writes=[("mwstv", ic)], dma=True)
    for tc in range(16):
        ps, pkey = pp.next()
        for ic in range(2):
            rr = lat[:, ic, tc * 128:(tc + 1) * 128]
            k.op("tensor", lambda e, ic=ic, rr=rr, ps=ps: e.matmul(
                ps[:, 0:256], rr, wstv[:, ic, 0:256], start=(ic == 0), stop=(ic == 1)),
                reads=[("mwstv", ic), latkeys[ic]], writes=[pkey])
        outv = v2[:, tc, :].rearrange("p (h x) -> p h x", x=65)[:, :, 0:64]
        psv = ps[:, 0:256].rearrange("p (h x) -> p h x", x=64)
        k.op("vector", lambda e, outv=outv, psv=psv: e.tensor_copy(out=outv, in_=psv),
             reads=[pkey], writes=[("mv2", tc)])
    _permute_chunks(ctx, a, q, qkeys, range(NC), pp, Pt, Pk)
    _permute_chunks(ctx, a, kk, kkeys, range(2), pp, Pt, Pk)
    a["rt_load"]("g")
    for ci in range(NC):
        _rope_chunk(ctx, a, q, qkeys, ci, a["rt"], "b1g")
    for ci in range(2):
        _rope_chunk(ctx, a, kk, kkeys, ci, a["rt"], "b1g")
    _qknorm(ctx, a, q, qkeys, range(NC), pp, "b1gq")
    _qknorm(ctx, a, kk, kkeys, range(2), pp, "b1gk")
    _attn_heads_b1(ctx, a, q, qkeys, kk, kkeys, v2, None, E, Ekey, pp, psS, psO,
                   window=None)
    _proj_generic(ctx, a, a["o"], a["okeys"], NC, W["g_wo"][0], W["g_wo"][1],
                  None, None, NC, pp, accum_into_x=(xi, xikeys))

    _rmsnorm_w(ctx, xi, xikeys, xr, xrkeys, a["bias2"], ("mbias2",), lnsc, pp)
    _proj_generic(ctx, a, xr, xrkeys, NC, W["l_wq"][0], W["l_wq"][1], q, qkeys, NC, pp)
    _proj_generic(ctx, a, xr, xrkeys, NC, W["l_wk"][0], W["l_wk"][1], kk, kkeys, 2, pp)
    k.op("vector", lambda e: e.memset(
        v2.rearrange("p c (h x) -> p c h x", x=65)[:, :, :, 64:65], 1.0),
        writes=[("mv2o",)])
    for ic in range(NC):
        k.op("sync", lambda e, ic=ic: e.dma_start(
            out=wstv[:, ic, 0:256], in_=W["l_wv"][0][ic * 128:(ic + 1) * 128, :]),
            reads=[W["l_wv"][1]], writes=[("mwstv", ic)], dma=True)
    for tc in range(16):
        ps, pkey = pp.next()
        for ic in range(NC):
            rr = xr[:, ic, tc * 128:(tc + 1) * 128]
            k.op("tensor", lambda e, ic=ic, rr=rr, ps=ps: e.matmul(
                ps[:, 0:256], rr, wstv[:, ic, 0:256], start=(ic == 0), stop=(ic == NC - 1)),
                reads=[("mwstv", ic), xrkeys[ic]], writes=[pkey])
        outv = v2[:, tc, :].rearrange("p (h x) -> p h x", x=65)[:, :, 0:64]
        psv = ps[:, 0:256].rearrange("p (h x) -> p h x", x=64)
        k.op("vector", lambda e, outv=outv, psv=psv: e.tensor_copy(out=outv, in_=psv),
             reads=[pkey], writes=[("mv2", tc)])
    _permute_chunks(ctx, a, q, qkeys, range(NC), pp, Pt, Pk)
    _permute_chunks(ctx, a, kk, kkeys, range(2), pp, Pt, Pk)
    a["rt_load"]("l")
    for ci in range(NC):
        _rope_chunk(ctx, a, q, qkeys, ci, a["rt"], "b1l")
    for ci in range(2):
        _rope_chunk(ctx, a, kk, kkeys, ci, a["rt"], "b1l")
    _qknorm(ctx, a, q, qkeys, range(NC), pp, "b1lq")
    _qknorm(ctx, a, kk, kkeys, range(2), pp, "b1lk")
    _attn_heads_b1(ctx, a, q, qkeys, kk, kkeys, v2, None, E, Ekey, pp, psS, psO,
                   window=WIN)
    _proj_generic(ctx, a, a["o"], a["okeys"], NC, W["l_wo"][0], W["l_wo"][1],
                  None, None, NC, pp, accum_into_x=(xi, xikeys))

    for oc in range(NC):
        for t0 in range(0, T, TT):
            xout_writer(oc, t0, xi[:, oc, t0:t0 + TT], xikeys[oc], None)

"""MoE sublayer (L1_ffn): x += moe(LN(x)). 64 experts, top-8 w/ group-top4,
CAP=1024 global token-order capacity, SCALE=2.5. Expert-parallel: core c owns
experts 8c..8c+7 and tokens [2048c, 2048c+2048).

Position logic: pos[t,e] = offset_c[e] + local_exclusive_count (local scan);
offset via AllGather of per-core counts + coremask (input) matmul.
Slot table: consumers scatter token-ids at flat slot e*1024+pos into a local
[-1]-initialized table; AllReduce-max merges. Producers gather their slots'
token rows from AllGathered xln; dropped slots get weight 0 consumer-side.
"""
import numpy as np

E_TOT = 64
EPG = 8
K_TOP = 8
CAP = 1024
MOE_SCALE = 2.5
HID_I = 512
LROWS = 514
JUNK = 65540


def alloc_moe(ctx):
    nc = ctx.nc
    a = {}
    a["xln"] = nc.alloc_sbuf_tensor("moe_xln", [128, NC, T], BF16)
    a["xlnkeys"] = [("moexln", ci) for ci in range(NC)]
    a["lgt"] = nc.alloc_sbuf_tensor("moe_lgt", [128, 16, 64], F32)
    a["sc"] = nc.alloc_sbuf_tensor("moe_sc", [128, 16, 64], F32)
    a["wa"] = nc.alloc_sbuf_tensor("moe_wa", [128, 16, 64], F32)
    a["sfc"] = nc.alloc_sbuf_tensor("moe_sfc", [128, 64], F32)
    a["tmp"] = nc.alloc_sbuf_tensor("moe_tmp", [128, 64], F32)
    a["tmp2"] = nc.alloc_sbuf_tensor("moe_tmp2", [128, 64], F32)
    a["gs"] = nc.alloc_sbuf_tensor("moe_gs", [128, 8], F32)
    a["gs2"] = nc.alloc_sbuf_tensor("moe_gs2", [128, 8], F32)
    a["top8"] = nc.alloc_sbuf_tensor("moe_top8", [128, 8], F32)
    a["idxu"] = nc.alloc_sbuf_tensor("moe_idxu", [128, 8], mybir.dt.uint32)
    a["idx8"] = nc.alloc_sbuf_tensor("moe_idx8", [128, 16, 8], F32)
    a["emaj"] = nc.alloc_sbuf_tensor("moe_emaj", [128, 16, 8], F32)
    a["iu1"] = nc.alloc_sbuf_tensor("moe_iu1", [128, 8], mybir.dt.uint32)
    a["iu2"] = nc.alloc_sbuf_tensor("moe_iu2", [128, 8], mybir.dt.uint32)
    a["wn"] = nc.alloc_sbuf_tensor("moe_wn", [128, 16, 8], F32)
    a["srow8"] = nc.alloc_sbuf_tensor("moe_srow8", [128, 8], F32)
    a["wsum"] = nc.alloc_sbuf_tensor("moe_wsum", [128, 1], F32)
    a["bbc"] = nc.alloc_sbuf_tensor("moe_bbc", [128, 64], F32)
    a["gwst"] = nc.alloc_sbuf_tensor("moe_gwst", [128, NC, 128], BF16)
    a["gwst_lo"] = nc.alloc_sbuf_tensor("moe_gwstlo", [128, NC, 128], BF16)
    a["xlo"] = nc.alloc_sbuf_tensor("moe_xlo", [128, NC, T], BF16)
    a["xlokeys"] = [("moexlo", ci) for ci in range(NC)]

    a["ct"] = nc.alloc_sbuf_tensor("moe_ct", [128, 16, 64], F32)
    a["aot"] = nc.alloc_sbuf_tensor("moe_aot", [64, T], F32)
    a["aot2"] = nc.alloc_sbuf_tensor("moe_aot2", [64, T], F32)
    a["cnt"] = nc.alloc_sbuf_tensor("moe_cnt", [64, 1], F32)
    a["call"] = nc.alloc_sbuf_tensor("moe_call", [8, 64], F32)
    a["cmask"] = nc.alloc_sbuf_tensor("moe_cmask", [8, 1], F32)
    a["offc"] = nc.alloc_sbuf_tensor("moe_offc", [64, 1], F32)
    a["lidx"] = nc.alloc_sbuf_tensor("moe_lidx", [128, 16, 8], I32)
    a["lflat"] = nc.alloc_sbuf_tensor("moe_lflat", [128, 16, 8], F32)
    a["tokid"] = nc.alloc_sbuf_tensor("moe_tokid", [128, 16], F32)
    a["slabidx"] = nc.alloc_sbuf_tensor("moe_slabidx", [128, 64], I32)
    a["gidx"] = nc.alloc_sbuf_tensor("moe_gidx", [128, 64], F32)
    a["gidxi"] = nc.alloc_sbuf_tensor("moe_gidxi", [128, 64], I32)
    a["oidx"] = nc.alloc_sbuf_tensor("moe_oidx", [128, 16, 8], I32)
    a["buf_tm"] = [nc.alloc_sbuf_tensor("moe_btm0", [128, 8, 1024], BF16)]
    a["buf_fm"] = nc.alloc_sbuf_tensor("moe_bfm", [128, 8, 1024], BF16)
    a["h_fm"] = nc.alloc_sbuf_tensor("moe_hfm", [128, 4, 1024], BF16)
    a["sg"] = nc.alloc_sbuf_tensor("moe_sg", [128, 512], BF16)
    a["wg_st"] = nc.alloc_sbuf_tensor("moe_wgst", [128, 8, 128], BF16)
    a["wu_st"] = nc.alloc_sbuf_tensor("moe_wust", [128, 8, 128], BF16)
    a["wd_st"] = nc.alloc_sbuf_tensor("moe_wdst", [128, 4, 512], BF16)
    a["ytm"] = nc.alloc_sbuf_tensor("moe_ytm", [128, 1024], BF16)
    a["ydst"] = [nc.alloc_sbuf_tensor(f"moe_yd{i}", [128, 1024], BF16) for i in range(3)]
    a["acc"] = nc.alloc_sbuf_tensor("moe_acc", [128, 1024], F32)
    a["iota64"] = nc.alloc_sbuf_tensor("moe_iota64", [128, 64], F32)
    a["stage"] = SbufPool(ctx, 2, [128, 128], F32, "moe_stage")
    a["psY"] = nc.alloc_psum_tensor("moe_psY", [128, 512], F32)
    a["psG"] = nc.alloc_psum_tensor("moe_psG", [128, 512], F32)
    a["psU"] = nc.alloc_psum_tensor("moe_psU", [128, 512], F32)
    a["pst"] = nc.alloc_psum_tensor("moe_pst", [128, 128], BF16)
    a["pstf"] = nc.alloc_psum_tensor("moe_pstf", [128, 128], F32)
    return a


def moe_prep_weights(ctx, wp):
    k = ctx.k
    for nm in ("wg", "wu", "wd"):
        src, dep = wp[nm + "_i"]
        dst, dkey = wp[nm + "_bf"]
        k.op("gpsimd", lambda e, src=src, dst=dst: e.dma_start(out=dst, in_=src),
             reads=[dep] if dep else [], writes=[dkey], dma=True)


def moe_prep(ctx, a, wp, weights=True):
    nc, k = ctx.nc, ctx.k
    c_iota64 = ctx.const.dram(
        "iota64", np.tile(np.arange(64, dtype=np.float32)[None, :], (128, 1)))
    k.op("sync", lambda e: e.dma_start(out=a["iota64"][:, :], in_=c_iota64),
         writes=[("iota64",)], dma=True)
    if weights:
        moe_prep_weights(ctx, wp)
    src, dep = wp["gate_i"]
    dst, dkey = wp["gate_bf"]
    dstlo, dlokey = wp["gate_lo_bf"]
    g32 = a["aot"][:, 0:D]
    ghib = a["ytm"][0:64, :]
    glob = a["ytm"][64:128, :]
    ytmks = [("mytm", d2) for d2 in range(NC)]
    k.op("sync", lambda e, src=src, g32=g32: e.dma_start(out=g32, in_=src),
         reads=[dep] if dep else [], writes=[("g32x",), ("aotx",)], dma=True)
    k.op("vector", lambda e, ghib=ghib, g32=g32: e.tensor_copy(out=ghib, in_=g32),
         reads=[("g32x",)], writes=[("ghibx",)] + ytmks)
    k.op("vector", lambda e, glob=glob, g32=g32, ghib=ghib: e.tensor_sub(
        out=glob, in0=g32, in1=ghib),
        reads=[("g32x",), ("ghibx",)], writes=[("globx",)] + ytmks)
    k.op("sync", lambda e, dst=dst, ghib=ghib: e.dma_start(out=dst[0:64, :], in_=ghib),
         reads=[("ghibx",)], writes=[dkey], dma=True)
    k.op("sync", lambda e, dstlo=dstlo, glob=glob: e.dma_start(out=dstlo[0:64, :], in_=glob),
         reads=[("globx",)], writes=[dlokey], dma=True)
    k.op("vector", lambda e: e.memset(a["ytm"][0:64, :], 0.0),
         reads=[dkey, dlokey], writes=[("ytmz",)] + ytmks)
    k.op("sync", lambda e, dst=dst: e.dma_start(out=dst[64:128, :], in_=a["ytm"][0:64, :]),
         reads=[("ytmz",)], writes=[dkey], dma=True)
    k.op("sync", lambda e, dstlo=dstlo: e.dma_start(out=dstlo[64:128, :], in_=a["ytm"][0:64, :]),
         reads=[("ytmz",)], writes=[dlokey], dma=True)


def step_moe(ctx, a, xsrc, xout_writer, wp, lnsc, pp, dram, cid, barrier=None):
    nc, k = ctx.nc, ctx.k
    xln, xlnkeys = a["xln"], a["xlnkeys"]
    iota = a["iota64"]
    ag_in = dram["ag_in"]
    layernorm_fm(ctx, xsrc, xln, xlnkeys, lnsc, lo=a["xlo"], lokeys=a["xlokeys"])

    agin_keys = []
    for tc in range(16):
        for dc in range(NC):
            src = xln[:, dc, tc * 128:(tc + 1) * 128]
            k.op("tensor", lambda e, src=src: e.transpose(
                a["pst"][:, :], src, ctx.identb[:, :]),
                reads=[xlnkeys[dc], ctx.identb_key], writes=[("mpst",)])
            k.op("vector", lambda e, dc=dc: e.tensor_copy(
                out=a["ytm"][:, dc * 128:(dc + 1) * 128], in_=a["pst"][:, :]),
                reads=[("mpst",)], writes=[("mytm", dc)])
        kk2 = (cid, "agin", tc)
        agin_keys.append(kk2)
        k.op("sync", lambda e, tc=tc: e.dma_start(
            out=ag_in[tc * 128:(tc + 1) * 128, :], in_=a["ytm"][:, :]),
            reads=[("mytm", d2) for d2 in range(NC)], writes=[kk2], dma=True)
    k.op("gpsimd", lambda e: e.collective_compute(
        "AllGather", ALU.bypass, replica_groups=[list(range(8))],
        ins=[ag_in], outs=[dram["xall"]]),
        reads=agin_keys, writes=[(cid, "xall")])

    gate_bf, gate_dep = wp["gate_bf"]
    gate_lo_bf, gate_lo_dep = wp["gate_lo_bf"]
    for dc in range(NC):
        k.op("scalar", lambda e, dc=dc: e.dma_start_transpose(
            a["gwst"][:, dc, :], gate_bf[:, dc * 128:(dc + 1) * 128]),
            reads=[gate_dep], writes=[("gwst", dc)], dma=True)
        k.op("scalar", lambda e, dc=dc: e.dma_start_transpose(
            a["gwst_lo"][:, dc, :], gate_lo_bf[:, dc * 128:(dc + 1) * 128]),
            reads=[gate_lo_dep], writes=[("gwstlo", dc)], dma=True)
    gb_ap, gb_dep = wp["gate_bias"]
    k.op("sync", lambda e: e.dma_start(out=a["sfc"][0:1, :], in_=gb_ap),
         reads=[gb_dep] if gb_dep else [], writes=[("sfcx",)], dma=True)
    ps, pkey = pp.next()
    k.op("tensor", lambda e, ps=ps: e.matmul(
        ps[0:128, 0:64], ctx.ones1_128[:, :], a["sfc"][0:1, :], start=True, stop=True),
        reads=[("sfcx",), ctx.ones1_128_key], writes=[pkey])
    k.op("vector", lambda e, ps=ps: e.tensor_copy(out=a["bbc"][:, :], in_=ps[0:128, 0:64]),
         reads=[pkey], writes=[("gbbc",)])
    k.op("sync", lambda e: e.dma_start(out=a["tokid"][:, :], in_=dram["tokid"]),
         writes=[("tokid",)], dma=True)
    k.op("sync", lambda e: e.dma_start(out=a["slabidx"][:, :], in_=dram["slabidx"]),
         writes=[("slabidx",)], dma=True)
    k.op("sync", lambda e: e.dma_start(out=a["cmask"][:, :], in_=dram["cmask"]),
         writes=[("cmask",)], dma=True)

    for tc in range(16):
        ps, pkey = pp.next()
        nterm = 4 * NC
        ti = 0
        for ic in range(NC):
            for (lhs_t, lkeys, wst, wkey) in (
                    (xln, xlnkeys, a["gwst"], ("gwst", ic)),
                    (xln, xlnkeys, a["gwst_lo"], ("gwstlo", ic)),
                    (a["xlo"], a["xlokeys"], a["gwst"], ("gwst", ic)),
                    (a["xlo"], a["xlokeys"], a["gwst_lo"], ("gwstlo", ic))):
                lhs = lhs_t[:, ic, tc * 128:(tc + 1) * 128]
                rr = wst[:, ic, 0:64]
                k.op("tensor", lambda e, lhs=lhs, rr=rr, ps=ps, ti=ti: e.matmul(
                    ps[:, 0:64], lhs, rr, start=(ti == 0), stop=(ti == nterm - 1)),
                    reads=[wkey, lkeys[ic]], writes=[pkey])
                ti += 1
        ov = a["lgt"][:, tc, :]
        k.op("vector", lambda e, ov=ov, ps=ps: e.tensor_copy(out=ov, in_=ps[:, 0:64]),
             reads=[pkey], writes=[(cid, "lgt", tc)])

    for tc in range(16):
        sc_t = a["sc"][:, tc, :]
        k.op("scalar", lambda e, tc=tc, sc_t=sc_t: e.activation(
            out=sc_t, in_=a["lgt"][:, tc, :], func=AF.Exp, scale=-1.0),
            reads=[(cid, "lgt", tc)], writes=[(cid, "sc", tc)])
        k.op("vector", lambda e, sc_t=sc_t: e.tensor_scalar(
            out=sc_t, in0=sc_t, scalar1=1.0, scalar2=None, op0=ALU.add),
            reads=[(cid, "sc", tc)], writes=[(cid, "sc", tc)])
        k.op("vector", lambda e, sc_t=sc_t: e.reciprocal(out=sc_t, in_=sc_t),
            reads=[(cid, "sc", tc)], writes=[(cid, "sc", tc)])
        k.op("vector", lambda e, sc_t=sc_t: e.tensor_add(
            out=a["sfc"][:, :], in0=sc_t, in1=a["bbc"][:, :]),
            reads=[(cid, "sc", tc), ("gbbc",)], writes=[("sfcx",)])
        sfc_g = a["sfc"].rearrange("p (g e) -> p g e", e=8)
        tmp_g = a["tmp"].rearrange("p (g e) -> p g e", e=8)
        k.op("vector", lambda e, sfc_g=sfc_g: e.reduce_max(
            out=a["gs"][:, :], in_=sfc_g, axis=AX.X),
            reads=[("sfcx",)], writes=[("gsx",)])
        gbc = a["gs"][:, :, None].broadcast_to((128, 8, 8))
        k.op("vector", lambda e, sfc_g=sfc_g, gbc=gbc, tmp_g=tmp_g: e.tensor_tensor(
            out=tmp_g, in0=sfc_g, in1=gbc, op=ALU.is_equal),
            reads=[("sfcx",), ("gsx",)], writes=[("tmpx",)])
        k.op("vector", lambda e: e.scalar_tensor_tensor(
            out=a["tmp"][:, :], in0=a["tmp"][:, :], scalar=-1e30, in1=a["sfc"][:, :],
            op0=ALU.mult, op1=ALU.add),
            reads=[("tmpx",), ("sfcx",)], writes=[("tmpx",)])
        k.op("vector", lambda e, tmp_g=tmp_g: e.reduce_max(
            out=a["gs2"][:, :], in_=tmp_g, axis=AX.X),
            reads=[("tmpx",)], writes=[("gs2x",)])
        k.op("vector", lambda e: e.tensor_add(out=a["gs"][:, :], in0=a["gs"][:, :],
                                              in1=a["gs2"][:, :]),
             reads=[("gsx",), ("gs2x",)], writes=[("gsx",)])
        k.op("vector", lambda e: e.max(out=a["gs2"][:, :], in_=a["gs"][:, :]),
             reads=[("gsx",)], writes=[("gs2x",)])
        k.op("vector", lambda e: e.tensor_scalar(
            out=a["gs2"][:, :], in0=a["gs"][:, :], scalar1=a["gs2"][:, 3:4], scalar2=None,
            op0=ALU.is_ge),
            reads=[("gsx",), ("gs2x",)], writes=[("gs2x",)])
        smbc = a["gs2"][:, :, None].broadcast_to((128, 8, 8))
        k.op("vector", lambda e, smbc=smbc, sfc_g=sfc_g, tmp_g=tmp_g: e.tensor_tensor(
            out=tmp_g, in0=sfc_g, in1=smbc, op=ALU.mult),
            reads=[("gs2x",), ("sfcx",)], writes=[("tmpx",)])
        k.op("vector", lambda e, smbc=smbc: e.tensor_scalar(
            out=a["tmp2"][:, 0:8], in0=a["gs2"][:, :], scalar1=1e4, scalar2=-1e4,
            op0=ALU.mult, op1=ALU.add),
            reads=[("gs2x",)], writes=[("tmp2x",)])
        t2bc = a["tmp2"][:, 0:8][:, :, None].broadcast_to((128, 8, 8))
        k.op("vector", lambda e, t2bc=t2bc, tmp_g=tmp_g: e.tensor_tensor(
            out=tmp_g, in0=tmp_g, in1=t2bc, op=ALU.add),
            reads=[("tmpx",), ("tmp2x",)], writes=[("tmpx",)])
        k.op("vector", lambda e: e.max(out=a["top8"][:, :], in_=a["tmp"][:, :]),
             reads=[("tmpx",)], writes=[("top8x",)])
        k.op("vector", lambda e: e.max_index(out=a["idxu"][:, :], in_max=a["top8"][:, :],
                                             in_values=a["tmp"][:, :]),
             reads=[("tmpx",), ("top8x",)], writes=[("idxux",)])
        ov = a["idx8"][:, tc, :]
        k.op("vector", lambda e, ov=ov: e.tensor_copy(out=ov, in_=a["idxu"][:, :]),
             reads=[("idxux",)], writes=[(cid, "idx8", tc)])
        k.op("vector", lambda e: e.tensor_scalar(
            out=a["iu1"][:, :], in0=a["idxu"][:, :], scalar1=7, scalar2=None,
            op0=ALU.bitwise_and),
            reads=[("idxux",)], writes=[("iu1x",)])
        k.op("vector", lambda e: e.tensor_scalar(
            out=a["iu2"][:, :], in0=a["idxu"][:, :], scalar1=3, scalar2=None,
            op0=ALU.arith_shift_right),
            reads=[("idxux",)], writes=[("iu2x",)])
        emt = a["emaj"][:, tc, :]
        k.op("vector", lambda e, emt=emt: e.tensor_scalar(
            out=emt, in0=a["iu1"][:, :], scalar1=float(8 * CAP), scalar2=None,
            op0=ALU.mult),
            reads=[("iu1x",)], writes=[(cid, "emaj", tc)])
        k.op("vector", lambda e, emt=emt: e.scalar_tensor_tensor(
            out=emt, in0=a["iu2"][:, :], scalar=float(CAP), in1=emt,
            op0=ALU.mult, op1=ALU.add),
            reads=[("iu2x",), (cid, "emaj", tc)], writes=[(cid, "emaj", tc)])
        k.op("vector", lambda e, tc=tc: e.memset(a["wa"][:, tc, :], 0.0),
             writes=[(cid, "wa", tc)])
        for kk_ in range(K_TOP):
            idc = a["idx8"][:, tc, kk_:kk_ + 1]
            k.op("vector", lambda e, idc=idc, sc_t=sc_t: e.scalar_tensor_tensor(
                out=a["tmp2"][:, :], in0=iota[:, :], scalar=idc, in1=sc_t,
                op0=ALU.is_equal, op1=ALU.mult),
                reads=[("iota64",), (cid, "idx8", tc), (cid, "sc", tc)],
                writes=[("tmp2x",)])
            ov = a["srow8"][:, kk_:kk_ + 1]
            k.op("vector", lambda e, ov=ov: e.reduce_sum(
                out=ov, in_=a["tmp2"][:, :], axis=AX.X),
                reads=[("tmp2x",)], writes=[("srow8x",)])
            k.op("vector", lambda e, idc=idc: e.tensor_scalar(
                out=a["tmp2"][:, :], in0=iota[:, :], scalar1=idc, scalar2=None,
                op0=ALU.is_equal),
                reads=[("iota64",), (cid, "idx8", tc)], writes=[("tmp2x",)])
            k.op("vector", lambda e, tc=tc: e.tensor_add(
                out=a["wa"][:, tc, :], in0=a["wa"][:, tc, :], in1=a["tmp2"][:, :]),
                reads=[(cid, "wa", tc), ("tmp2x",)], writes=[(cid, "wa", tc)])
        k.op("vector", lambda e: e.reduce_sum(out=a["wsum"][:, :], in_=a["srow8"][:, :],
                                              axis=AX.X),
             reads=[("srow8x",)], writes=[("wsumx",)])
        k.op("vector", lambda e: e.tensor_scalar(
            out=a["wsum"][:, :], in0=a["wsum"][:, :], scalar1=1e-20, scalar2=None,
            op0=ALU.add),
            reads=[("wsumx",)], writes=[("wsumx",)])
        k.op("vector", lambda e: e.reciprocal(out=a["wsum"][:, :], in_=a["wsum"][:, :]),
             reads=[("wsumx",)], writes=[("wsumx",)])
        ov = a["wn"][:, tc, :]
        k.op("vector", lambda e, ov=ov: e.tensor_scalar(
            out=ov, in0=a["srow8"][:, :], scalar1=a["wsum"][:, 0:1], scalar2=MOE_SCALE,
            op0=ALU.mult, op1=ALU.mult),
            reads=[("srow8x",), ("wsumx",)], writes=[(cid, "wn", tc)])

    for tc in range(16):
        src = a["wa"][:, tc, :]
        k.op("tensor", lambda e, src=src: e.transpose(
            a["pstf"][0:64, :], src, ctx.identf[:, :]),
            reads=[(cid, "wa", tc), ctx.identf_key], writes=[("mpstf",)])
        ov = a["aot"][:, tc * 128:(tc + 1) * 128]
        k.op("vector", lambda e, ov=ov: e.tensor_copy(out=ov, in_=a["pstf"][0:64, :]),
             reads=[("mpstf",)], writes=[("aotx",)])
    k.op("vector", lambda e: e.reduce_sum(out=a["cnt"][:, :], in_=a["aot"][:, :],
                                          axis=AX.X),
         reads=[("aotx",)], writes=[("cntx",)])
    st1, k1 = a["stage"].next()
    k.op("vector", lambda e, st1=st1: e.tensor_copy(out=st1[0:64, 0:1], in_=a["cnt"][:, :]),
         reads=[("cntx",)], writes=[k1])
    k.op("sync", lambda e, st1=st1: e.dma_start(
        out=dram["cnt_loc"].rearrange("a b -> b a"), in_=st1[0:64, 0:1]),
        reads=[k1], writes=[(cid, "cntloc")], dma=True)
    k.op("gpsimd", lambda e: e.collective_compute(
        "AllGather", ALU.bypass, replica_groups=[list(range(8))],
        ins=[dram["cnt_loc"]], outs=[dram["cnt_all"]]),
        reads=[(cid, "cntloc")], writes=[(cid, "cntall")])
    k.op("sync", lambda e: e.dma_start(out=a["call"][:, :], in_=dram["cnt_all"]),
         reads=[(cid, "cntall")], writes=[("callx",)], dma=True)
    ps, pkey = pp.next()
    k.op("tensor", lambda e, ps=ps: e.matmul(
        ps[0:64, 0:1], a["call"][:, :], a["cmask"][:, :], start=True, stop=True),
        reads=[("callx",), ("cmask",)], writes=[pkey])
    k.op("vector", lambda e, ps=ps: e.tensor_copy(out=a["offc"][:, :], in_=ps[0:64, 0:1]),
         reads=[pkey], writes=[("offcx",)])

    aot2 = a["aot2"]
    k.op("vector", lambda e: e.memset(aot2[:, :], 0.0), writes=[("aot2x",)])
    k.op("vector", lambda e: e.tensor_tensor_scan(
        out=aot2[:, :], data0=aot2[:, :], data1=a["aot"][:, :],
        initial=a["offc"][:, 0:1], op0=ALU.add, op1=ALU.add),
        reads=[("aotx",), ("aot2x",), ("offcx",)], writes=[("aot2x",)])
    k.op("vector", lambda e: e.tensor_sub(out=aot2[:, :], in0=aot2[:, :], in1=a["aot"][:, :]),
         reads=[("aot2x",), ("aotx",)], writes=[("aot2x",)])
    for tc in range(16):
        src = aot2[:, tc * 128:(tc + 1) * 128]
        k.op("tensor", lambda e, src=src: e.transpose(
            a["pstf"][:, 0:64], src, ctx.identf[0:64, 0:64]),
            reads=[("aot2x",), ctx.identf_key], writes=[("mpstf",)])
        ov = a["ct"][:, tc, :]
        k.op("vector", lambda e, ov=ov: e.tensor_copy(out=ov, in_=a["pstf"][:, 0:64]),
             reads=[("mpstf",)], writes=[(cid, "ct", tc)])

    c_neg1 = ctx.const.dram("Lneg1", np.full((128, LROWS), -1.0, np.float32))
    k.op("sync", lambda e: e.dma_start(out=dram["Lloc"], in_=c_neg1),
         writes=[(cid, "Lloc")], dma=True)
    Lflat = dram["Lloc"].rearrange("a b -> (a b) ()")
    scat_keys = []
    for tc in range(16):
        for kk_ in range(K_TOP):
            idc = a["idx8"][:, tc, kk_:kk_ + 1]
            k.op("vector", lambda e, idc=idc, tc=tc: e.scalar_tensor_tensor(
                out=a["tmp2"][:, :], in0=iota[:, :], scalar=idc, in1=a["ct"][:, tc, :],
                op0=ALU.is_equal, op1=ALU.mult),
                reads=[("iota64",), (cid, "idx8", tc), (cid, "ct", tc)],
                writes=[("tmp2x",)])
            k.op("vector", lambda e: e.reduce_sum(out=a["wsum"][:, :], in_=a["tmp2"][:, :],
                                                  axis=AX.X),
                 reads=[("tmp2x",)], writes=[("wsumx",)])
            k.op("vector", lambda e: e.tensor_scalar(
                out=a["tmp2"][:, 0:1], in0=a["wsum"][:, :], scalar1=float(CAP),
                scalar2=None, op0=ALU.is_ge),
                reads=[("wsumx",)], writes=[("tmp2x",)])
            k.op("vector", lambda e: e.tensor_scalar(
                out=a["tmp2"][:, 1:2], in0=a["tmp2"][:, 0:1], scalar1=-1.0, scalar2=1.0,
                op0=ALU.mult, op1=ALU.add),
                reads=[("tmp2x",)], writes=[("tmp2x",)])
            wv = a["wn"][:, tc, kk_:kk_ + 1]
            k.op("vector", lambda e, wv=wv: e.tensor_mul(
                out=wv, in0=wv, in1=a["tmp2"][:, 1:2]),
                reads=[(cid, "wn", tc), ("tmp2x",)], writes=[(cid, "wn", tc)])
            lf = a["lflat"][:, tc, kk_:kk_ + 1]
            k.op("vector", lambda e, lf=lf, idc=idc: e.scalar_tensor_tensor(
                out=lf, in0=idc, scalar=float(CAP), in1=a["wsum"][:, :],
                op0=ALU.mult, op1=ALU.add),
                reads=[(cid, "idx8", tc), ("wsumx",)], writes=[(cid, "lflat", tc)])
            k.op("vector", lambda e, lf=lf: e.tensor_mul(
                out=a["tmp2"][:, 3:4], in0=lf, in1=a["tmp2"][:, 1:2]),
                reads=[(cid, "lflat", tc), ("tmp2x",)], writes=[("tmp2x",)])
            k.op("vector", lambda e, tc=tc, kk_=kk_: e.scalar_tensor_tensor(
                out=a["tmp2"][:, 4:5], in0=a["tmp2"][:, 0:1],
                scalar=float(JUNK + tc * K_TOP + kk_),
                in1=a["tmp2"][:, 3:4], op0=ALU.mult, op1=ALU.add),
                reads=[("tmp2x",)], writes=[("tmp2x",)])
            dd = a["lidx"][:, tc, kk_:kk_ + 1]
            k.op("vector", lambda e, dd=dd: e.tensor_copy(out=dd, in_=a["tmp2"][:, 4:5]),
                 reads=[("tmp2x",)], writes=[(cid, "lidx", tc, kk_)])
            k.op("vector", lambda e, tc=tc, kk_=kk_: e.tensor_add(
                out=a["tmp2"][:, 5:6], in0=a["emaj"][:, tc, kk_:kk_ + 1],
                in1=a["wsum"][:, :]),
                reads=[(cid, "emaj", tc), ("wsumx",), ("tmp2x",)], writes=[("tmp2x",)])
            oo = a["oidx"][:, tc, kk_:kk_ + 1]
            k.op("vector", lambda e, oo=oo: e.tensor_mul(
                out=oo, in0=a["tmp2"][:, 5:6], in1=a["tmp2"][:, 1:2]),
                reads=[("tmp2x",)], writes=[(cid, "oidx", tc)])
    for tc in range(16):
        tks = a["tokid"][:, tc:tc + 1]
        for kk_ in range(K_TOP):
            dd = a["lidx"][:, tc, kk_:kk_ + 1]
            sk = (cid, "scat", tc, kk_)
            scat_keys.append(sk)
            k.op("gpsimd", lambda e, dd=dd, tks=tks: e.indirect_dma_start(
                Lflat, IndirectOffsetOnAxis(dd, 0), tks, None),
                reads=[(cid, "lidx", tc, kk_), ("tokid",), (cid, "Lloc")],
                writes=[sk], dma=True)
    k.op("gpsimd", lambda e: e.collective_compute(
        "AllGather", ALU.bypass, replica_groups=[list(range(8))],
        ins=[dram["Lloc"]], outs=[dram["Lall"]]),
        reads=scat_keys + [(cid, "Lloc")], writes=[(cid, "Lall")])
    k.op("sync", lambda e: e.dma_start(
        out=a["Lmrg"][:, :], in_=dram["Lall"][0:128, :]),
        reads=[(cid, "Lall")], writes=[("Lmrg",)], dma=True)
    for s in range(1, 8):
        buf = a["Lin"][:, s % 2, :]
        k.op("sync", lambda e, buf=buf, s=s: e.dma_start(
            out=buf, in_=dram["Lall"][s * 128:(s + 1) * 128, :]),
            reads=[(cid, "Lall")], writes=[("Lin", s % 2)], dma=True)
        k.op("vector", lambda e, buf=buf: e.tensor_max(
            out=a["Lmrg"][:, :], in0=a["Lmrg"][:, :], in1=buf),
            reads=[("Lmrg",), ("Lin", s % 2)], writes=[("Lmrg",)])
    k.op("sync", lambda e: e.dma_start(out=dram["Lfin"], in_=a["Lmrg"][:, :]),
         reads=[("Lmrg",)], writes=[(cid, "Lfin")], dma=True)

    if barrier is not None:
        barrier()

    Lfin_flat = dram["Lfin"].rearrange("a b -> (a b) ()")
    for j in range(64):
        sic = a["slabidx"][:, j:j + 1]
        ov = a["gidx"][:, j:j + 1]
        k.op("gpsimd", lambda e, sic=sic, ov=ov: e.indirect_dma_start(
            ov, None, Lfin_flat, IndirectOffsetOnAxis(sic, 0)),
            reads=[("slabidx",), (cid, "Lfin")], writes=[("gidx", j)], dma=True)
    k.op("vector", lambda e: e.tensor_relu(out=a["gidx"][:, :], in_=a["gidx"][:, :]),
         reads=[("gidx", j2) for j2 in range(64)], writes=[("gidxF",)])
    k.op("vector", lambda e: e.tensor_copy(out=a["gidxi"][:, :], in_=a["gidx"][:, :]),
         reads=[("gidxF",)], writes=[("gidxF",)])

    wg_bf, wg_key = wp["wg_bf"]
    wu_bf, wu_key = wp["wu_bf"]
    wd_bf, wd_key = wp["wd_bf"]
    yslab_keys = []
    for el in range(EPG):
        btm = a["buf_tm"][0]
        for j in range(8):
            gc = a["gidxi"][:, el * 8 + j:el * 8 + j + 1]
            ov = btm[:, j, :]
            k.op("gpsimd", lambda e, gc=gc, ov=ov: e.indirect_dma_start(
                ov, None, dram["xall"], IndirectOffsetOnAxis(gc, 0)),
                reads=[("gidxF",), (cid, "xall")], writes=[("btm", j)], dma=True)
        for j in range(8):
            for dc in range(NC):
                src = btm[:, j, dc * 128:(dc + 1) * 128]
                k.op("tensor", lambda e, src=src: e.transpose(
                    a["pst"][:, :], src, ctx.identb[:, :]),
                    reads=[("btm", j), ctx.identb_key], writes=[("mpst",)])
                ov = a["buf_fm"][:, dc, j * 128:(j + 1) * 128]
                if (j + dc) % 2 == 0:
                    k.op("vector", lambda e, ov=ov: e.tensor_copy(out=ov, in_=a["pst"][:, :]),
                         reads=[("mpst",)], writes=[("bfm", dc, j)])
                else:
                    k.op("scalar", lambda e, ov=ov: e.copy(out=ov, in_=a["pst"][:, :]),
                         reads=[("mpst",)], writes=[("bfm", dc, j)])
        bfm_all = [("bfm", dc, j) for dc in range(NC) for j in range(8)]
        for oi in range(4):
            for dc in range(NC):
                k.op("sync", lambda e, el=el, oi=oi, dc=dc: e.dma_start(
                    out=a["wg_st"][:, dc, :],
                    in_=wg_bf[el, dc * 128:(dc + 1) * 128, oi * 128:(oi + 1) * 128]),
                    reads=[wg_key], writes=[("wgst", dc)], dma=True)
                k.op("sync", lambda e, el=el, oi=oi, dc=dc: e.dma_start(
                    out=a["wu_st"][:, dc, :],
                    in_=wu_bf[el, dc * 128:(dc + 1) * 128, oi * 128:(oi + 1) * 128]),
                    reads=[wu_key], writes=[("wust", dc)], dma=True)
            for st in range(2):
                sl = slice(st * 512, (st + 1) * 512)
                for dc in range(NC):
                    rr = a["buf_fm"][:, dc, sl]
                    k.op("tensor", lambda e, dc=dc, rr=rr: e.matmul(
                        a["psG"][:, :], a["wg_st"][:, dc, :], rr,
                        start=(dc == 0), stop=(dc == NC - 1)),
                        reads=[("wgst", dc), ("bfm", dc, 4 * st), ("bfm", dc, 4 * st + 1),
                               ("bfm", dc, 4 * st + 2), ("bfm", dc, 4 * st + 3)],
                        writes=[("psGx",)])
                k.op("scalar", lambda e: e.activation(
                    out=a["sg"][:, :], in_=a["psG"][:, :], func=AF.Silu),
                    reads=[("psGx",)], writes=[("sgx",)])
                for dc in range(NC):
                    rr = a["buf_fm"][:, dc, sl]
                    k.op("tensor", lambda e, dc=dc, rr=rr: e.matmul(
                        a["psU"][:, :], a["wu_st"][:, dc, :], rr,
                        start=(dc == 0), stop=(dc == NC - 1)),
                        reads=[("wust", dc), ("bfm", dc, 4 * st), ("bfm", dc, 4 * st + 1),
                               ("bfm", dc, 4 * st + 2), ("bfm", dc, 4 * st + 3)],
                        writes=[("psUx",)])
                ov = a["h_fm"][:, oi, sl]
                k.op("vector", lambda e, ov=ov: e.tensor_mul(
                    out=ov, in0=a["psU"][:, :], in1=a["sg"][:, :]),
                    reads=[("psUx",), ("sgx",)], writes=[("hfm", oi, st)])
        for dh in range(2):
            for ic in range(4):
                k.op("sync", lambda e, el=el, dh=dh, ic=ic: e.dma_start(
                    out=a["wd_st"][:, ic, :],
                    in_=wd_bf[el, ic * 128:(ic + 1) * 128, dh * 512:(dh + 1) * 512]),
                    reads=[wd_key], writes=[("wdst", ic)], dma=True)
            for pc in range(8):
                for ic in range(4):
                    lhs = a["h_fm"][:, ic, pc * 128:(pc + 1) * 128]
                    k.op("tensor", lambda e, lhs=lhs, ic=ic: e.matmul(
                        a["psY"][:, :], lhs, a["wd_st"][:, ic, :],
                        start=(ic == 0), stop=(ic == 3)),
                        reads=[("wdst", ic), ("hfm", ic, pc // 4)], writes=[("psYx",)])
                ov = a["ytm"][:, dh * 512:(dh + 1) * 512]
                k.op("vector", lambda e, ov=ov: e.tensor_copy(out=ov, in_=a["psY"][:, :]),
                     reads=[("psYx",)], writes=[("ytm2", dh)])
                yk = (cid, "yslab", el, pc, dh)
                yslab_keys.append(yk)
                k.op("sync", lambda e, el=el, pc=pc, dh=dh: e.dma_start(
                    out=dram["y_slab"][el * 1024 + pc * 128:el * 1024 + (pc + 1) * 128,
                                       dh * 512:(dh + 1) * 512],
                    in_=a["ytm"][:, dh * 512:(dh + 1) * 512]),
                    reads=[("ytm2", dh)], writes=[yk], dma=True)
        el_keys = [kk3 for kk3 in yslab_keys if kk3[2] == el]
        y_all_v = dram["y_all"][el * 8 * CAP:(el + 1) * 8 * CAP, :]
        k.op("gpsimd", lambda e, el=el, y_all_v=y_all_v: e.collective_compute(
            "AllGather", ALU.bypass, replica_groups=[list(range(8))],
            ins=[dram["y_slab"][el * CAP:(el + 1) * CAP, :]], outs=[y_all_v]),
            reads=el_keys, writes=[(cid, "yall", el)])
    yall_keys = [(cid, "yall", el2) for el2 in range(EPG)]

    if barrier is not None:
        barrier()

    for tc in range(16):
        k.op("vector", lambda e: e.memset(a["acc"][:, :], 0.0), writes=[("maccx",)])
        for kk_ in range(K_TOP):
            yd = a["ydst"][kk_ % 3]
            ydk = ("yd", kk_ % 3)
            oo = a["oidx"][:, tc, kk_:kk_ + 1]
            k.op("gpsimd", lambda e, yd=yd, oo=oo: e.indirect_dma_start(
                yd[:, :], None, dram["y_all"], IndirectOffsetOnAxis(oo, 0)),
                reads=[(cid, "oidx", tc)] + yall_keys, writes=[ydk], dma=True)
            wv = a["wn"][:, tc, kk_:kk_ + 1]
            eng = "vector"
            k.op(eng, lambda e, yd=yd, wv=wv: e.scalar_tensor_tensor(
                out=a["acc"][:, :], in0=yd[:, :], scalar=wv, in1=a["acc"][:, :],
                op0=ALU.mult, op1=ALU.add),
                reads=[ydk, (cid, "wn", tc), ("maccx",)], writes=[("maccx",)])
        for dc in range(NC):
            src = a["acc"][:, dc * 128:(dc + 1) * 128]
            k.op("tensor", lambda e, src=src: e.transpose(
                a["pstf"][:, :], src, ctx.identf[:, :]),
                reads=[("maccx",), ctx.identf_key], writes=[("mpstf",)])
            xout_writer(dc, tc, a["pstf"], ("mpstf",))

"""Fused single-launch megakernel: all 8 sublayers in ONE Bass program.

x ping-pongs between DRAM buffers xA/xB (streamed via a shared XDram slot
pool); per-phase SBUF scratch is hand-placed in an overlapping arena
(alloc_sbuf_tensor_at), with K.barrier() fencing phase transitions. PSUM is
scoped per phase via `with` guards. MoE's routing/expert/consumer scratch
groups alias each other inside the moe phases (step_moe's internal barriers).
"""
import contextlib
import numpy as np
import concourse.bass as bass
import concourse.mybir as mybir
from concourse.bass_utils import run_bass_kernel_spmd

LAT = 256
NCORE = 8

_DT_SIZE = {F32: 4, BF16: 2, I32: 4, I16: 2}


class Arena:

    def __init__(self, nc, reserve_tail=1024):
        size = nc.sbuf_bytes_remaining - reserve_tail
        b, e = nc.bump_sbuf(size)
        self.nc, self.base, self.end = nc, b, e
        self.cur = b
        self.hi = b

    def mark(self):
        return self.cur

    def reset(self, m):
        self.cur = m

    def alloc(self, name, shape, dtype):
        per = 1
        for s in shape[1:]:
            per *= s
        per *= _DT_SIZE.get(dtype, 4)
        off = (self.cur + 31) // 32 * 32
        assert off + per <= self.end, (
            f"arena overflow: {name} needs {per}B at {off}, end {self.end}")
        t = self.nc.alloc_sbuf_tensor_at(name, shape, dtype, offset=off)
        self.cur = off + per
        self.hi = max(self.hi, self.cur)
        return t


def mk_tabs(ctx, base):
    half = 32
    theta = (1.0 / base) ** (np.arange(half, dtype=np.float64) / half)
    ang = np.arange(T, dtype=np.float64)[:, None] * theta[None, :]
    cos = np.tile(np.cos(ang).T, (4, 1))
    sin_ = np.sin(ang).T
    sinp = np.concatenate([-sin_, sin_, -sin_, sin_], axis=0)
    cd = ctx.const.dram(f"rc{int(base)}", cos.astype(BF))
    sd = ctx.const.dram(f"rs{int(base)}", sinp.astype(BF))
    return cd, sd


def build_full():
    nc = bass.Bass("TRN2", target_bir_lowering=False, debug=False)
    k = K(nc)
    ctx = Ctx(nc, k)

    xT = nc.dram_tensor("xT", [D, T], F32, kind="ExternalInput").ap()
    yT = nc.dram_tensor("yT", [D, T], F32, kind="ExternalOutput").ap()
    E_ = {}

    def ein(name, shape, dt=F32):
        E_[name] = nc.dram_tensor(name, shape, dt, kind="ExternalInput").ap()
        return E_[name]

    for p in ("a0", "b0"):
        for nm in ("wq", "wk", "wv", "wo"):
            ein(f"{p}_{nm}", [D, D])
        for nm in ("bq", "bk", "bo"):
            ein(f"{p}_{nm}", [128, NC])
        ein(f"{p}_bv", [1, D])
    ein("b0_gb", [128, 2 * NC])
    ein("ffn_wg_i", [D, HID], I32)
    ein("ffn_wu_i", [D, HID], I32)
    ein("ffn_wd_i", [HID, D], I32)
    ein("a1_w2", [2 * D, D])
    ein("a1_w1", [1, D])
    ein("a1_b1", [128, NC])
    B1SH = {"g_wq": (D, D), "g_kv_down": (D, LAT), "g_k_up": (LAT, LAT),
            "g_v_up": (LAT, LAT), "g_wo": (D, D), "l_wq": (D, D),
            "l_wk": (D, LAT), "l_wv": (D, LAT), "l_wo": (D, D)}
    for nm, sh in B1SH.items():
        ein("b1_" + nm, list(sh))
    ein("b1_n1w", [128, NC])
    ein("b1_n2w", [128, NC])
    ein("moe_gate_w", [64, D])
    ein("moe_gate_b", [1, 64])
    ein("moe_wg_i", [EPG, D, HID_I])
    ein("moe_wu_i", [EPG, D, HID_I])
    ein("moe_wd_i", [EPG, HID_I, D])
    ein("moe_tokid", [128, 16])
    ein("moe_slab", [128, 64], I32)
    ein("moe_cmask", [8, 1])

    xA = nc.dram_tensor("xA", [D, T], F32, kind="Internal").ap()
    xB = nc.dram_tensor("xB", [D, T], F32, kind="Internal").ap()
    bf = {}

    def mkbf(name, shape):
        bf[name] = nc.dram_tensor(name + "_bf", shape, BF16, kind="Internal").ap()
        return bf[name]

    for p in ("a0", "b0"):
        for nm in ("wq", "wk", "wv", "wo"):
            mkbf(f"{p}_{nm}", [D, D])
    mkbf("ffn_wg", [D, HID])
    mkbf("ffn_wu", [D, HID])
    mkbf("ffn_wd", [HID, D])
    mkbf("a1_w2", [2 * D, D])
    for nm, sh in B1SH.items():
        mkbf("b1_" + nm, list(sh))
    mkbf("moe_gate", [128, D])
    mkbf("moe_gate_lo", [128, D])
    mkbf("moe_wg", [EPG, D, HID_I])
    mkbf("moe_wu", [EPG, D, HID_I])
    mkbf("moe_wd", [EPG, HID_I, D])
    mdram = {}
    for inst in ("m0", "m1"):
        mdram[inst] = {
            "ag_in": nc.dram_tensor(f"{inst}_agin", [T, D], BF16, kind="Internal").ap(),
            "xall": nc.dram_tensor(f"{inst}_xall", [NCORE * T, D], BF16, kind="Internal",
                                   addr_space="Shared").ap(),
            "cnt_loc": nc.dram_tensor(f"{inst}_cntl", [1, 64], F32, kind="Internal").ap(),
            "cnt_all": nc.dram_tensor(f"{inst}_cnta", [8, 64], F32, kind="Internal").ap(),
            "Lloc": nc.dram_tensor(f"{inst}_Lloc", [128, LROWS], F32, kind="Internal").ap(),
            "Lall": nc.dram_tensor(f"{inst}_Lall", [8 * 128, LROWS], F32, kind="Internal",
                                   addr_space="Shared").ap(),
            "Lfin": nc.dram_tensor(f"{inst}_Lfin", [128, LROWS], F32, kind="Internal").ap(),
            "y_slab": nc.dram_tensor(f"{inst}_yslab", [EPG * CAP, D], BF16, kind="Internal").ap(),
            "y_all": nc.dram_tensor(f"{inst}_yall", [64 * CAP, D], BF16, kind="Internal",
                                    addr_space="Shared").ap(),
            "tokid": E_["moe_tokid"], "slabidx": E_["moe_slab"],
            "cmask": E_["moe_cmask"],
        }

    with nc.Block() as block:
        setup_common(ctx)
        murow = nc.alloc_sbuf_tensor("murow", [1, T], F32)
        varrow = nc.alloc_sbuf_tensor("varrow", [1, T], F32)
        sq = nc.alloc_sbuf_tensor("lnsq", [128, TT], F32)
        mubc = nc.alloc_sbuf_tensor("mubc", [128, T], BF16)
        rbc = nc.alloc_sbuf_tensor("rbc", [128, T], F32)
        srow = nc.alloc_sbuf_tensor("gsrow", [1, T], F32)
        xln = nc.alloc_sbuf_tensor("xln", [128, NC, T], BF16)
        xlnkeys = [("xln", ci) for ci in range(NC)]
        pp = PsumPool(ctx, 2, [128, TT], F32, "gpp")
        stage = SbufPool(ctx, 2, [128, TT], F32, "gstage")
        xdpool = SbufPool(ctx, 2, [128, TT], F32, "gxd")
        lnsc = {"murow": (murow, "murow"), "varrow": (varrow, "varrow"),
                "sq": (sq, "lnsq"), "mubc": (mubc, "mubc"), "rbc": (rbc, "rbc"),
                "pp": pp}
        bias1 = nc.alloc_sbuf_tensor("gbias1", [128, NC], F32)
        bias2 = nc.alloc_sbuf_tensor("gbias2", [128, NC], F32)
        bias3 = nc.alloc_sbuf_tensor("gbias3", [128, NC], F32)
        nrm = nc.alloc_sbuf_tensor("gnrm", [1, 2 * NH, 4], F32)
        mrow = nc.alloc_sbuf_tensor("gmrow", [1, NH], F32)
        mcols = nc.alloc_sbuf_tensor("gmcols", [128, NH], F32)
        gbt = nc.alloc_sbuf_tensor("ggb", [128, 2 * NC], F32)
        triu_t, triu_k = load_const_tile(
            ctx, np.triu(np.ones((128, 128), dtype=BF)), "triu", BF16)
        wm = np.zeros((128, 384), dtype=BF)
        for kl in range(128):
            wm[kl, kl:min(kl + WIN, 384)] = 1
        wmask_t, wmask_k = load_const_tile(ctx, wm, "wmask", BF16)
        P128 = perm128_const(ctx)
        for v in (EPS_LN, 1.0, 1e-6, 1e-12):
            ctx.sconst(v)
        mb = {}
        mb["wn"] = nc.alloc_sbuf_tensor("moe_wn", [128, 16, 8], F32)
        mb["oidx"] = nc.alloc_sbuf_tensor("moe_oidx", [128, 16, 8], I32)
        mb["lidx"] = nc.alloc_sbuf_tensor("moe_lidx", [128, 16, 8], I32)
        mb["lflat"] = nc.alloc_sbuf_tensor("moe_lflat", [128, 16, 8], F32)
        mb["tokid"] = nc.alloc_sbuf_tensor("ms_tokid", [128, 16], F32)
        mb["slabidx"] = nc.alloc_sbuf_tensor("ms_slabidx", [128, 64], I32)
        mb["gidx"] = nc.alloc_sbuf_tensor("ms_gidx", [128, 64], F32)
        mb["gidxi"] = nc.alloc_sbuf_tensor("ms_gidxi", [128, 64], I32)
        mb["iota64"] = nc.alloc_sbuf_tensor("ms_iota64", [128, 64], F32)
        mb["ytm"] = nc.alloc_sbuf_tensor("moe_ytm", [128, 1024], BF16)
        mb["srow8"] = nc.alloc_sbuf_tensor("moe_srow8", [128, 8], F32)
        mb["emaj"] = nc.alloc_sbuf_tensor("moe_emaj", [128, 16, 8], F32)
        mb["iu1"] = nc.alloc_sbuf_tensor("moe_iu1", [128, 8], mybir.dt.uint32)
        mb["iu2"] = nc.alloc_sbuf_tensor("moe_iu2", [128, 8], mybir.dt.uint32)
        mb["gs"] = nc.alloc_sbuf_tensor("moe_gs", [128, 8], F32)
        mb["gs2"] = nc.alloc_sbuf_tensor("moe_gs2", [128, 8], F32)
        mb["top8"] = nc.alloc_sbuf_tensor("moe_top8", [128, 8], F32)
        mb["idxu"] = nc.alloc_sbuf_tensor("moe_idxu", [128, 8], mybir.dt.uint32)
        mb["wsum"] = nc.alloc_sbuf_tensor("moe_wsum", [128, 1], F32)
        mb["cnt"] = nc.alloc_sbuf_tensor("moe_cnt", [64, 1], F32)
        mb["call"] = nc.alloc_sbuf_tensor("moe_call", [8, 64], F32)
        mb["cmask"] = nc.alloc_sbuf_tensor("ms_cmask", [8, 1], F32)
        mb["offc"] = nc.alloc_sbuf_tensor("moe_offc", [64, 1], F32)

        ar = Arena(nc)
        m0_ = ar.mark()
        rct = ar.alloc("grct", [128, T], BF16)
        rst = ar.alloc("grst", [128, T], BF16)
        m1_ = ar.mark()
        MH = {}
        MH["q"] = ar.alloc("mha_q", [128, NC, T], BF16)
        MH["kk"] = ar.alloc("mha_k", [128, NC, T], BF16)
        MH["v2"] = ar.alloc("mha_v2", [128, 16, 65 * 8], BF16)
        MH["E"] = ar.alloc("mha_E", [128, T], BF16)
        MH["wst"] = ar.alloc("mha_wst", [128, NC, 128], BF16)
        MH["wstv"] = ar.alloc("mha_wstv", [128, NC, 512], BF16)
        MH["ropeA"] = ar.alloc("mha_ropeA", [128, 1, TT], F32)
        MH["ropeB"] = ar.alloc("mha_ropeB", [128, 1, TT], F32)
        MH["brow"] = ar.alloc("mha_brow", [1, D], BF16)
        MH["bbc"] = ar.alloc("mha_bbc", [128, D], BF16)
        ar.reset(m0_)
        FF = {}
        FF["h"] = ar.alloc("ffn_h", [128, NHC, HALF], BF16)
        FF["wg"] = ar.alloc("ffn_wg", [128, 2, NC, 128], BF16)
        FF["wu"] = ar.alloc("ffn_wu", [128, 2, NC, 128], BF16)
        FF["wd"] = ar.alloc("ffn_wd", [128, 2, NHC, 128], BF16)
        FF["sg"] = ar.alloc("ffn_sg", [128, 2, FT], F32)
        ar.reset(m0_)
        A1 = {}
        A1["q"] = ar.alloc("a1_nrm", [128, NC, T], BF16)
        A1["kk"] = ar.alloc("a1_cnt", [128, NC, T], BF16)
        A1["E"] = ar.alloc("a1_E", [128, T], BF16)
        A1["c1"] = ar.alloc("a1_c1", [128, T], BF16)
        A1["wst"] = ar.alloc("a1_wst", [128, NC, 128], BF16)
        A1["wstv"] = ar.alloc("a1_wstv", [128, NC, 128], BF16)
        A1["brow"] = ar.alloc("a1_brow", [1, D], F32)
        ar.reset(m1_)
        B1 = {}
        B1["xi"] = ar.alloc("b1_xi", [128, NC, T], BF16)
        B1["xr"] = ar.alloc("b1_xr", [128, NC, T], BF16)
        B1["kk"] = ar.alloc("b1_k", [128, 2, T], BF16)
        B1["lat"] = ar.alloc("b1_lat", [128, 2, T], BF16)
        B1["v2g"] = ar.alloc("b1_v2", [128, 16, 65 * KVH], BF16)
        B1["E"] = ar.alloc("b1_E", [128, T], BF16)
        B1["c1"] = ar.alloc("b1_c1", [128, T], BF16)
        B1["c2"] = ar.alloc("b1_c2", [128, T], BF16)
        B1["kstage"] = ar.alloc("b1_kst", [128, T], BF16)
        B1["wst"] = ar.alloc("b1_wst", [128, NC, 128], BF16)
        B1["wstv"] = ar.alloc("b1_wstv", [128, NC, 256], BF16)
        ar.reset(m0_)
        MR = {}
        MR["f32c"] = ar.alloc("lnf32c", [128, TT], F32)
        MR["xlo"] = ar.alloc("moe_xlo", [128, NC, T], BF16)
        MR["gwst"] = ar.alloc("moe_gwst", [128, NC, 128], BF16)
        MR["gwst_lo"] = ar.alloc("moe_gwstlo", [128, NC, 128], BF16)
        MR["lgt"] = ar.alloc("moe_lgt", [128, 16, 64], F32)
        MR["sc"] = ar.alloc("moe_sc", [128, 16, 64], F32)
        MR["wa"] = ar.alloc("moe_wa", [128, 16, 64], F32)
        MR["ct"] = ar.alloc("moe_ct", [128, 16, 64], F32)
        MR["sfc"] = ar.alloc("moe_sfc", [128, 64], F32)
        MR["tmp"] = ar.alloc("moe_tmp", [128, 64], F32)
        MR["tmp2"] = ar.alloc("moe_tmp2", [128, 64], F32)
        MR["bbc"] = ar.alloc("moe_bbc", [128, 64], F32)
        MR["idx8"] = ar.alloc("moe_idx8", [128, 16, 8], F32)
        MR["aot"] = ar.alloc("moe_aot", [64, T], F32)
        MR["aot2"] = ar.alloc("moe_aot2", [64, T], F32)
        MR["Lmrg"] = ar.alloc("moe_Lmrg", [128, LROWS], F32)
        MR["Lin"] = ar.alloc("moe_Lin", [128, 2, LROWS], F32)
        m_moe = ar.mark()
        ar.reset(m0_)
        ME = {}
        ME["buf_tm"] = ar.alloc("moe_btm0", [128, 8, 1024], BF16)
        ME["buf_fm"] = ar.alloc("moe_bfm", [128, 8, 1024], BF16)
        ME["h_fm"] = ar.alloc("moe_hfm", [128, 4, 1024], BF16)
        ME["sg"] = ar.alloc("moe_sg", [128, 512], BF16)
        ME["wg_st"] = ar.alloc("moe_wgst", [128, 8, 128], BF16)
        ME["wu_st"] = ar.alloc("moe_wust", [128, 8, 128], BF16)
        ME["wd_st"] = ar.alloc("moe_wdst", [128, 4, 512], BF16)
        ar.reset(m0_)
        MC = {}
        MC["ydst"] = [ar.alloc(f"moe_yd{i}", [128, 1024], BF16) for i in range(3)]
        MC["acc"] = ar.alloc("moe_acc", [128, 1024], F32)
        ar.reset(m0_)

        tabs = {"g": mk_tabs(ctx, 1000000.0), "l": mk_tabs(ctx, 10000.0)}

        def rt_load(which):
            cd, sd = tabs[which]
            k.op("sync", lambda e, cd=cd: e.dma_start(out=rct[:, :], in_=cd),
                 writes=[("rtc",)], dma=True)
            k.op("sync", lambda e, sd=sd: e.dma_start(out=rst[:, :], in_=sd),
                 writes=[("rts",)], dma=True)

        for p in ("a0", "b0"):
            for nm in ("wq", "wk", "wv", "wo"):
                src, dst = E_[f"{p}_{nm}"], bf[f"{p}_{nm}"]
                key = f"{p}_{nm}bf"
                if p == "a0" and nm in ("wq", "wk"):
                    inr = src.rearrange("(h j par) d -> h par j d", par=2, j=32)
                    outr = dst.rearrange("(h par j) d -> h par j d", par=2, j=32)
                    for par in range(2):
                        ia, oa = inr[:, par, :, :], outr[:, par, :, :]
                        k.op("gpsimd", lambda e, ia=ia, oa=oa: e.dma_start(out=oa, in_=ia),
                             writes=[key], dma=True)
                else:
                    k.op("gpsimd", lambda e, src=src, dst=dst: e.dma_start(out=dst, in_=src),
                         writes=[key], dma=True)
        for nm, key in [("ffn_wg", "wgbf"), ("ffn_wu", "wubf"), ("ffn_wd", "wdbf")]:
            src, dst = E_[nm + "_i"], bf[nm]
            k.op("gpsimd", lambda e, src=src, dst=dst: e.dma_start(out=dst, in_=src),
                 writes=[key], dma=True)
        k.op("gpsimd", lambda e: e.dma_start(out=bf["a1_w2"], in_=E_["a1_w2"]),
             writes=["w2bf"], dma=True)
        for nm in B1SH:
            k.op("gpsimd", lambda e, s=E_["b1_" + nm], d=bf["b1_" + nm]: e.dma_start(
                out=d, in_=s), writes=[nm + "bf"], dma=True)
        wpm = {
            "gate_i": (E_["moe_gate_w"], None), "gate_bf": (bf["moe_gate"], "gatebf"),
            "gate_lo_bf": (bf["moe_gate_lo"], "gatelobf"),
            "gate_bias": (E_["moe_gate_b"], None),
            "wg_i": (E_["moe_wg_i"], None), "wg_bf": (bf["moe_wg"], "wgbf_m"),
            "wu_i": (E_["moe_wu_i"], None), "wu_bf": (bf["moe_wu"], "wubf_m"),
            "wd_i": (E_["moe_wd_i"], None), "wd_bf": (bf["moe_wd"], "wdbf_m"),
        }
        moe_prep_weights(ctx, wpm)

        def mk_mha_a(ph):
            a = dict(MH)
            a["bias"], a["bias2"], a["bias3"] = bias1, bias2, bias3
            a["nrm"], a["mrow"], a["mcols"] = nrm, mrow, mcols
            a["srow"], a["gb"] = srow, gbt
            a["stage"] = stage
            a["triu"], a["triu_key"] = triu_t, triu_k
            return a

        def phase_mha(ph, src, dst, variant):
            xsrc = XDram(ctx, src, ("xdram", ph), pool=xdpool)
            a = mk_mha_a(ph)
            with contextlib.ExitStack() as st_:
                a["psS"] = st_.enter_context(nc.psum_tensor(f"{ph}_psS", [128, 1024], F32))
                a["psO"] = [st_.enter_context(nc.psum_tensor(f"{ph}_psO{i}", [65, TT], F32))
                            for i in range(4)]
                if variant == "a0":
                    rt_load("l")
                    W = {"q": (bf["a0_wq"], True, "a0_wqbf"),
                         "k": (bf["a0_wk"], True, "a0_wkbf"),
                         "v": (bf["a0_wv"], True, "a0_wvbf"),
                         "o": (bf["a0_wo"], True, "a0_wobf"),
                         "bq": (E_["a0_bq"], None), "bk": (E_["a0_bk"], None),
                         "bv": (E_["a0_bv"], None), "bo": (E_["a0_bo"], None)}

                    def xout_writer(oc, t0, ps, pkey, bias_info):
                        bcol, bck, occ = bias_info
                        st1, k1 = stage.next()
                        bc = bcol[:, occ:occ + 1]
                        k.op("scalar", lambda e, ps=ps, st1=st1, bc=bc: e.activation(
                            out=st1[:, :], in_=ps[:, 0:TT], func=AF.Identity, bias=bc,
                            scale=1.0), reads=[pkey, bck], writes=[k1])
                        xap, xkey = xsrc.get(ctx, oc, t0, TT)
                        k.op("vector", lambda e, st1=st1, xap=xap: e.tensor_add(
                            out=st1[:, :], in0=st1[:, :], in1=xap),
                            reads=[k1, xkey], writes=[k1])
                        k.op("sync", lambda e, oc=oc, t0=t0, st1=st1: e.dma_start(
                            out=dst[oc * 128:(oc + 1) * 128, t0:t0 + TT], in_=st1[:, :]),
                            reads=[k1], writes=[("yo", ph, oc, t0)], dma=True)

                    step_mha(ctx, a, xsrc, xout_writer, xln, xlnkeys, W, lnsc, pp,
                             {"rope": (rct, ("rtc",), rst, ("rts",)), "post": "res"})
                else:
                    W = {"q": (bf["b0_wq"], False, "b0_wqbf"),
                         "k": (bf["b0_wk"], False, "b0_wkbf"),
                         "v": (bf["b0_wv"], False, "b0_wvbf"),
                         "o": (bf["b0_wo"], False, "b0_wobf"),
                         "bq": (E_["b0_bq"], None), "bk": (E_["b0_bk"], None),
                         "bv": (E_["b0_bv"], None), "bo": (E_["b0_bo"], None)}

                    def xout_writer(oc, t0, ps, pkey, info):
                        flag, zt, zkey = info
                        st1, k1 = stage.next()
                        xap, xkey = xsrc.get(ctx, oc, t0, TT)
                        zap = zt[:, oc, t0:t0 + TT]
                        k.op("vector", lambda e, st1=st1, xap=xap, zap=zap: e.tensor_add(
                            out=st1[:, :], in0=zap, in1=xap),
                            reads=[zkey, xkey], writes=[k1])
                        k.op("sync", lambda e, oc=oc, t0=t0, st1=st1: e.dma_start(
                            out=dst[oc * 128:(oc + 1) * 128, t0:t0 + TT], in_=st1[:, :]),
                            reads=[k1], writes=[("yo", ph, oc, t0)], dma=True)

                    step_mha(ctx, a, xsrc, xout_writer, xln, xlnkeys, W, lnsc, pp,
                             {"rope": None, "post": "lnaff",
                              "gamma_beta": (E_["b0_gb"], None)})

        def phase_ffn(ph, src, dst):
            xsrc = XDram(ctx, src, ("xdram", ph), pool=xdpool)
            a = dict(FF)
            with contextlib.ExitStack() as st_:
                a["psG"] = st_.enter_context(nc.psum_tensor(f"{ph}_psG", [128, 2, FT], F32))
                a["psU"] = st_.enter_context(nc.psum_tensor(f"{ph}_psU", [128, 2, FT], F32))
                a["psY"] = st_.enter_context(nc.psum_tensor(f"{ph}_psY", [128, 2, FT], F32))

                def xout_writer(oc, t0, psY, pkey):
                    st1, k1 = stage.next()
                    xap, xkey = xsrc.get(ctx, oc, t0, FT)
                    k.op("vector", lambda e, st1=st1, psY=psY, xap=xap: e.scalar_tensor_tensor(
                        out=st1[:, :], in0=psY, scalar=S_DQ * S_DQ, in1=xap,
                        op0=ALU.mult, op1=ALU.add),
                        reads=[pkey, xkey], writes=[k1])
                    k.op("sync", lambda e, oc=oc, t0=t0, st1=st1: e.dma_start(
                        out=dst[oc * 128:(oc + 1) * 128, t0:t0 + FT], in_=st1[:, :]),
                        reads=[k1], writes=[("yo", ph, oc, t0)], dma=True)

                step_ffn0x(ctx, xsrc, xout_writer, xln, xlnkeys, bf["ffn_wg"],
                           bf["ffn_wu"], bf["ffn_wd"], a, lnsc, tag=ph,
                           wdram_keys=["wgbf", "wubf", "wdbf"])

        def phase_a1(ph, src, dst):
            xsrc = XDram(ctx, src, ("xdram", ph), pool=xdpool)
            a = dict(A1)
            a["srow"], a["bias"], a["stage"] = srow, bias1, stage
            with contextlib.ExitStack() as st_:
                a["psS"] = st_.enter_context(nc.psum_tensor(f"{ph}_psS", [128, 1024], F32))
                psC = st_.enter_context(nc.psum_tensor(f"{ph}_psC", [1, T], F32))
                wp = {"w2": (bf["a1_w2"], "w2bf"), "w1row": (E_["a1_w1"], None),
                      "b1": (E_["a1_b1"], None)}

                def xout_writer(oc, t0, ps, pkey, info):
                    st1, k1 = stage.next()
                    xap, xkey = xsrc.get(ctx, oc, t0, TT)
                    k.op("vector", lambda e, st1=st1, xap=xap, ps=ps: e.tensor_add(
                        out=st1[:, :], in0=ps[:, 0:TT], in1=xap),
                        reads=[pkey, xkey], writes=[k1])
                    k.op("sync", lambda e, oc=oc, t0=t0, st1=st1: e.dma_start(
                        out=dst[oc * 128:(oc + 1) * 128, t0:t0 + TT], in_=st1[:, :]),
                        reads=[k1], writes=[("yo", ph, oc, t0)], dma=True)

                step_a1(ctx, a, xsrc, xout_writer, xln, xlnkeys, wp, lnsc, pp, psC)

        def phase_moe(ph, src, dst, inst, first):
            xsrc = XDram(ctx, src, ("xdram", ph), pool=xdpool)
            a = {}
            a.update(mb)
            a.update(MR)
            a.update(ME)
            a.update(MC)
            a["buf_tm"] = [ME["buf_tm"]]
            a["xln"] = xln
            a["xlnkeys"] = xlnkeys
            a["xlo"] = MR["xlo"]
            a["xlokeys"] = [("moexlo", ci) for ci in range(NC)]
            a["stage"] = stage
            with contextlib.ExitStack() as st_:
                a["psY"] = st_.enter_context(nc.psum_tensor(f"{ph}_psY", [128, 512], F32))
                a["psG"] = st_.enter_context(nc.psum_tensor(f"{ph}_psG", [128, 512], F32))
                a["psU"] = st_.enter_context(nc.psum_tensor(f"{ph}_psU", [128, 512], F32))
                a["pst"] = st_.enter_context(nc.psum_tensor(f"{ph}_pst", [128, 128], BF16))
                a["pstf"] = st_.enter_context(nc.psum_tensor(f"{ph}_pstf", [128, 128], F32))
                if first:
                    moe_prep(ctx, a, wpm, weights=False)

                def xout_writer(dc, tc, pstf, pkey):
                    st1, k1 = a["stage"].next()
                    xap, xkey = xsrc.get(ctx, dc, tc * 128, 128)
                    k.op("vector", lambda e, st1=st1, xap=xap, pstf=pstf: e.tensor_add(
                        out=st1[:, 0:128], in0=pstf[:, :], in1=xap),
                        reads=[pkey, xkey], writes=[k1])
                    k.op("sync", lambda e, dc=dc, tc=tc, st1=st1: e.dma_start(
                        out=dst[dc * 128:(dc + 1) * 128, tc * 128:(tc + 1) * 128],
                        in_=st1[:, 0:128]),
                        reads=[k1], writes=[("yo", ph, dc, tc)], dma=True)

                mlnsc = dict(lnsc)
                mlnsc["f32c"] = (MR["f32c"], "lnf32c")
                step_moe(ctx, a, xsrc, xout_writer, wpm, mlnsc, pp, mdram[inst], inst,
                         barrier=k.barrier)

        def phase_b1(ph, src, dst):
            xsrc = XDram(ctx, src, ("xdram", ph), pool=xdpool)
            a = dict(B1)
            a["xikeys"] = [("xi", ci) for ci in range(NC)]
            a["xrkeys"] = [("xr", ci) for ci in range(NC)]
            a["q"] = xln
            a["o"] = a["xr"]
            a["okeys"] = a["xrkeys"]
            a["srow"], a["bias"], a["bias2"] = srow, bias1, bias2
            a["wmask"], a["wmask_key"] = wmask_t, wmask_k
            a["triu"], a["triu_key"] = triu_t, triu_k
            a["P128"] = P128
            a["stage"] = stage
            a["rt_load"] = rt_load
            a["rt"] = (rct, ("rtc",), rst, ("rts",))
            b1lnsc = {"murow": (murow, "murow"), "varrow": (varrow, "varrow"),
                      "sq": (sq, "lnsq"), "mubc": (mubc, "mubc"), "rbc": (rbc, "rbc"),
                      "pp": pp}
            with contextlib.ExitStack() as st_:
                a["psS"] = st_.enter_context(nc.psum_tensor(f"{ph}_psS", [128, 1024], F32))
                a["psO"] = [st_.enter_context(nc.psum_tensor(f"{ph}_psO{i}", [65, TT], F32))
                            for i in range(4)]
                W = {nm: (bf["b1_" + nm], nm + "bf") for nm in B1SH}
                W["n1w"] = (E_["b1_n1w"], None)
                W["n2w"] = (E_["b1_n2w"], None)

                def xout_writer(oc, t0, xi_ap, xi_key, info):
                    st1, k1 = stage.next()
                    xap, xkey = xsrc.get(ctx, oc, t0, TT)
                    k.op("vector", lambda e, st1=st1, xap=xap, xi_ap=xi_ap: e.tensor_add(
                        out=st1[:, :], in0=xi_ap, in1=xap),
                        reads=[xi_key, xkey], writes=[k1])
                    k.op("sync", lambda e, oc=oc, t0=t0, st1=st1: e.dma_start(
                        out=dst[oc * 128:(oc + 1) * 128, t0:t0 + TT], in_=st1[:, :]),
                        reads=[k1], writes=[("yo", ph, oc, t0)], dma=True)

                step_b1(ctx, a, xsrc, xout_writer, W, b1lnsc, pp)

        phase_mha("pa0", xT, xA, "a0")
        k.barrier()
        phase_ffn("pf0", xA, xB)
        k.barrier()
        phase_mha("pb0", xB, xA, "b0")
        k.barrier()
        phase_ffn("pf1", xA, xB)
        k.barrier()
        phase_a1("pa1", xB, xA)
        k.barrier()
        phase_moe("pm0", xA, xB, "m0", True)
        k.barrier()
        phase_b1("pb1", xB, xA)
        k.barrier()
        phase_moe("pm1", xA, yT, "m1", False)
        k.emit(block)
    return nc



def _cols(v):
    return np.ascontiguousarray(np.asarray(v, np.float32).reshape(NC, 128).T)


def _rope_perm():
    p = np.zeros(D, np.int64)
    for h in range(16):
        for par in range(2):
            for j in range(32):
                p[64 * h + 32 * par + j] = 64 * h + 2 * j + par
    return p


def kernel(**inp):
    f = {kk2: np.asarray(vv) for kk2, vv in inp.items()}
    x = np.asarray(f["L0_attn_a_x"], np.float32)
    B = x.shape[0]
    perm = _rope_perm()
    com = {
        "a0_wq": np.ascontiguousarray(f["L0_attn_a_W_q"].astype(np.float32)),
        "a0_wk": np.ascontiguousarray(f["L0_attn_a_W_k"].astype(np.float32)),
        "a0_wv": np.ascontiguousarray(f["L0_attn_a_W_v"].astype(np.float32)),
        "a0_wo": np.ascontiguousarray(f["L0_attn_a_W_o"].astype(np.float32)),
        "a0_bq": _cols(np.asarray(f["L0_attn_a_b_q"], np.float32)[perm]),
        "a0_bk": _cols(np.asarray(f["L0_attn_a_b_k"], np.float32)[perm]),
        "a0_bv": np.ascontiguousarray(f["L0_attn_a_b_v"].astype(np.float32).reshape(1, D)),
        "a0_bo": _cols(f["L0_attn_a_b_o"]),
        "b0_wq": np.ascontiguousarray(f["L0_attn_b_Wq"].astype(np.float32)),
        "b0_wk": np.ascontiguousarray(f["L0_attn_b_Wk"].astype(np.float32)),
        "b0_wv": np.ascontiguousarray(f["L0_attn_b_Wv"].astype(np.float32)),
        "b0_wo": np.ascontiguousarray(f["L0_attn_b_Wo"].astype(np.float32)),
        "b0_bq": _cols(f["L0_attn_b_bq"]), "b0_bk": _cols(f["L0_attn_b_bk"]),
        "b0_bv": np.ascontiguousarray(f["L0_attn_b_bv"].astype(np.float32).reshape(1, D)),
        "b0_bo": _cols(f["L0_attn_b_bo"]),
        "b0_gb": np.ascontiguousarray(np.concatenate(
            [_cols(f["L0_attn_b_gamma"]), _cols(f["L0_attn_b_beta"])], axis=1)),
        "ffn_wg_i": np.ascontiguousarray(f["L0_ffn_w_gate"].astype(np.int32).T),
        "ffn_wu_i": np.ascontiguousarray(f["L0_ffn_w_up"].astype(np.int32).T),
        "ffn_wd_i": np.ascontiguousarray(f["L0_ffn_w_down"].astype(np.int32).T),
        "a1_w2": np.ascontiguousarray(f["L1_attn_a_W2"].astype(np.float32)),
        "a1_w1": np.ascontiguousarray(f["L1_attn_a_W1"].astype(np.float32).reshape(1, D)),
        "a1_b1": _cols(f["L1_attn_a_b1"]),
        "b1_n1w": _cols(f["L1_attn_b_norm1_w"]),
        "b1_n2w": _cols(f["L1_attn_b_norm2_w"]),
        "moe_gate_w": np.ascontiguousarray(f["L1_ffn_gate_w"].astype(np.float32)),
        "moe_gate_b": np.ascontiguousarray(
            f["L1_ffn_gate_bias"].astype(np.float32).reshape(1, 64)),
    }
    for nm in ("g_wq", "g_kv_down", "g_k_up", "g_v_up", "g_wo",
               "l_wq", "l_wk", "l_wv", "l_wo"):
        com["b1_" + nm] = np.ascontiguousarray(f["L1_attn_b_" + nm].astype(np.float32))
    wg_all = f["L1_ffn_w_gate"].astype(np.float32)
    wu_all = f["L1_ffn_w_up"].astype(np.float32)
    wd_all = f["L1_ffn_w_down"].astype(np.float32)
    p_ = np.arange(128)

    nc = build_full()
    in_maps = []
    for c in range(B):
        m = dict(com)
        m["xT"] = np.ascontiguousarray(x[c].T)
        m["moe_wg_i"] = np.ascontiguousarray(wg_all[c * EPG:(c + 1) * EPG])
        m["moe_wu_i"] = np.ascontiguousarray(wu_all[c * EPG:(c + 1) * EPG])
        m["moe_wd_i"] = np.ascontiguousarray(wd_all[c * EPG:(c + 1) * EPG])
        m["moe_tokid"] = (c * T + np.arange(16)[None, :] * 128 + p_[:, None]).astype(np.float32)
        m["moe_slab"] = (c * EPG * CAP + np.arange(64)[None, :] * 128 + p_[:, None]).astype(np.int32)
        m["moe_cmask"] = (np.arange(8) < c).astype(np.float32).reshape(8, 1)
        in_maps.append(m)
    res = run_bass_kernel_spmd(nc, in_maps, core_ids=list(range(B)))
    return np.ascontiguousarray(np.stack(
        [np.asarray(res.results[c]["yT"], np.float32).T for c in range(B)]))


# version 12
# speedup vs baseline: 1.0344x; 1.0344x over previous
from concourse.bass_utils import run_bass_kernel_spmd


"""Kernel-building framework for the mk_hybrid_d2 megakernel.

Op scheduler with buffer-granularity dependency tracking:
- Ops are recorded in global program order, each on one engine.
- Each engine has ONE semaphore; every instruction bumps it (+1 compute,
  +16 DMA). Cross-engine deps become wait_ge(producer_engine_sem, count).
- Same-engine deps: emitted as waits too (safe mode), except tensor->tensor
  (PE FIFO is safe; waits would kill matmul throughput).
- Buffers: named resources (tensor, key). Tracker adds RAW/WAR/WAW deps.
"""
import numpy as np
import ml_dtypes
import concourse.bass as bass
import concourse.mybir as mybir
from concourse.bass_core_rust import IndirectOffsetOnAxis

F32 = mybir.dt.float32
BF16 = mybir.dt.bfloat16
I32 = mybir.dt.int32
I16 = mybir.dt.int16
AF = mybir.ActivationFunctionType
ALU = mybir.AluOpType
AX = mybir.AxisListType
BF = ml_dtypes.bfloat16

ENGINES = ("sync", "scalar", "vector", "tensor", "gpsimd")


class Op:
    __slots__ = ("engine", "emit", "inc", "seq", "cum", "waits", "tag")

    def __init__(self, engine, emit, inc, tag=""):
        self.engine = engine
        self.emit = emit
        self.inc = inc
        self.seq = None
        self.cum = None
        self.waits = {}
        self.tag = tag


class K:

    def __init__(self, nc):
        self.nc = nc
        self.ops = []
        self.counts = {(e, kind): 0 for e in ENGINES for kind in ("c", "d")}
        self.bufs = {}
        self.trust_same_engine = {"tensor"}
        self.dma_bounds = {e: set() for e in ENGINES}
        self.barrier_waits = {}

    def barrier(self):
        for e in ENGINES:
            for kind in ("c", "d"):
                v = self.counts[(e, kind)]
                if v:
                    self.barrier_waits[(e, kind)] = v
            v = self.counts[(e, "d")]
            if v:
                self.dma_bounds[e].add(v)

    def op(self, engine, emit, reads=(), writes=(), dma=False, deps=(), tag=""):
        inc = 16 if dma else 1
        kind = "d" if dma else "c"
        o = Op(engine, emit, inc, tag)
        dep_ops = list(deps)
        for r in reads:
            ent = self.bufs.get(r)
            if ent and ent[0] is not None:
                dep_ops.append(ent[0])
        for w in writes:
            ent = self.bufs.get(w)
            if ent:
                if ent[0] is not None:
                    dep_ops.append(ent[0])
                dep_ops.extend(ent[1])
        for d in dep_ops:
            if d is None:
                continue
            if d.inc == 16:
                dk = (d.engine, "d")
                val = self.counts[dk]
                self.dma_bounds[d.engine].add(val)
            else:
                if d.engine == engine and engine in self.trust_same_engine:
                    continue
                dk = (d.engine, "c")
                val = d.cum
            cur = o.waits.get(dk, -1)
            if val > cur:
                o.waits[dk] = val
        for dk, val in self.barrier_waits.items():
            if dk == (engine, "c"):
                continue
            if val > o.waits.get(dk, -1):
                o.waits[dk] = val
        self.counts[(engine, kind)] += inc
        o.cum = self.counts[(engine, kind)]
        o.seq = len(self.ops)
        self.ops.append(o)
        for r in reads:
            ent = self.bufs.setdefault(r, [None, []])
            ent[1].append(o)
        for w in writes:
            self.bufs[w] = [o, []]
        return o

    def emit(self, block):
        nc = self.nc
        sems = {}
        for e in ENGINES:
            for kind in ("c", "d"):
                if self.counts[(e, kind)] or (e == "sync" and kind == "d"):
                    sems[(e, kind)] = nc.alloc_semaphore(f"ksem_{e}_{kind}")
        per_engine = {e: [] for e in ENGINES}
        for o in self.ops:
            per_engine[o.engine].append(o)

        def mk(engine):
            ops = per_engine[engine]
            bounds = sorted(self.dma_bounds[engine])

            def body(eng):
                waited = {sk: 0 for sk in sems}
                bi = 0
                for o in ops:
                    if o.inc == 16:
                        pre = o.cum - o.inc
                        dk = (engine, "d")
                        while bi < len(bounds) and bounds[bi] <= pre:
                            b = bounds[bi]
                            bi += 1
                            if b > waited[dk]:
                                eng.wait_ge(sems[dk], b)
                                waited[dk] = b
                    for sk, val in sorted(o.waits.items(), key=str):
                        if val > waited[sk]:
                            eng.wait_ge(sems[sk], val)
                            waited[sk] = val
                    inst = o.emit(eng)
                    inst.then_inc(sems[(engine, "d" if o.inc == 16 else "c")], o.inc)
            return body

        for e in ENGINES:
            if per_engine[e] or e == "sync":
                getattr(block, e)(mk(e))
        final = {sk: self.counts[sk] for sk in sems}

        def tail(eng):
            for sk, v in final.items():
                if v:
                    eng.wait_ge(sems[sk], v)
        block.sync(tail)
        self.sems = sems



def ceil_div(a, b):
    return (a + b - 1) // b


class Const:

    def __init__(self, nc, k):
        self.nc = nc
        self.k = k
        self.cache = {}

    def dram(self, name, arr):
        if name not in self.cache:
            self.cache[name] = self.nc.inline_tensor(np.ascontiguousarray(arr), f"c_{name}").ap()
        return self.cache[name]


def np_bir_dtype(dt):
    return {F32: np.float32, BF16: BF, I32: np.int32, I16: np.int16}[dt]

"""Sublayer emitters for the mk_hybrid megakernel. All activations are
feature-major (fm): SBUF tiles [128 dims, T tokens], d_model=1024 = 8 chunks.

Conventions:
- T = 2048 tokens per core, D = 1024, NC = 8 chunks, TT = 512 token tile.
- fm buffers are [128, nchunks, T] tensors; buffer key = (name, chunk).
- Weights pre-staged in DRAM as bf16 in lhsT orientation ([d_in, d_out],
  rows contiguous) unless noted; tiles loaded [128, 128].
"""
import numpy as np

T = 2048
D = 1024
NC = D // 128
TT = 512
NTT = T // TT
EPS_LN = 1e-5


class Ctx:
    def __init__(self, nc, k):
        self.nc = nc
        self.k = k
        self.const = Const(nc, k)
        self.uid = 0
        self._sc_tile = None
        self._sc_map = {}

    def name(self, p):
        self.uid += 1
        return f"{p}{self.uid}"

    def sconst(self, val):
        val = float(val)
        if self._sc_tile is None:
            self._sc_tile = self.nc.alloc_sbuf_tensor("scconsts", [128, 32], F32)
        if val not in self._sc_map:
            idx = len(self._sc_map)
            assert idx < 32
            key = ("sconst", val)
            t = self._sc_tile
            self.k.op("vector", lambda e, idx=idx, val=val: e.memset(t[:, idx:idx + 1], val),
                      writes=[key])
            self.nc.const_aps.register(F32, val, t[:, idx:idx + 1])
            self._sc_map[val] = key
        return self._sc_map[val]


class PsumPool:

    def __init__(self, ctx, n, shape, dtype=F32, prefix="pp"):
        self.slots = []
        for i in range(n):
            t = ctx.nc.alloc_psum_tensor(ctx.name(prefix), list(shape), dtype)
            self.slots.append((t, ctx.name(prefix + "k")))
        self.i = 0

    def next(self):
        s = self.slots[self.i % len(self.slots)]
        self.i += 1
        return s


class SbufPool:
    def __init__(self, ctx, n, shape, dtype=F32, prefix="sp"):
        self.slots = []
        for i in range(n):
            t = ctx.nc.alloc_sbuf_tensor(ctx.name(prefix), list(shape), dtype)
            self.slots.append((t, ctx.name(prefix + "k")))
        self.i = 0

    def next(self):
        s = self.slots[self.i % len(self.slots)]
        self.i += 1
        return s


def load_const_tile(ctx, arr, name, dtype=F32):
    nc, k = ctx.nc, ctx.k
    t = nc.alloc_sbuf_tensor(ctx.name(name), list(arr.shape), dtype)
    d = ctx.const.dram(ctx.name(name + "_d"), arr)
    key = ctx.name("cbuf")
    k.op("sync", lambda e: e.dma_start(out=t[...], in_=d), writes=[key], dma=True)
    return t, key


def setup_common(ctx):
    ctx.ones1_128, ctx.ones1_128_key = load_const_tile(
        ctx, np.ones((1, 128), np.float32), "ones1x128")
    ctx.ones1_128b, ctx.ones1_128b_key = load_const_tile(
        ctx, np.ones((1, 128), BF), "ones1x128b", BF16)
    ctx.onescol, ctx.onescol_key = load_const_tile(
        ctx, np.ones((128, 1), np.float32), "ones128x1")
    ctx.onescol_b, ctx.onescol_b_key = load_const_tile(
        ctx, np.ones((128, 1), BF), "ones128x1b", BF16)
    ctx.identb, ctx.identb_key = load_const_tile(
        ctx, np.eye(128, dtype=BF), "identb", BF16)
    ctx.identf, ctx.identf_key = load_const_tile(
        ctx, np.eye(128, dtype=np.float32), "identf", F32)


def bcast_rows(ctx, src_ap, src_key, out_sb, out_key, width, pp, nparts=128,
               lhsT=None, lhsT_key=None, nrows=1, out_dt_tiles=None):
    nc, k = ctx.nc, ctx.k
    if lhsT is None:
        sdt = src_ap.tensor.dtype if hasattr(src_ap, 'tensor') else src_ap.dtype
        if sdt == BF16:
            lhsT, lhsT_key = ctx.ones1_128b, ctx.ones1_128b_key
        else:
            lhsT, lhsT_key = ctx.ones1_128, ctx.ones1_128_key
    for t0 in range(0, width, TT):
        w = min(TT, width - t0)
        ps, pkey = pp.next()
        sk = src_key(t0) if callable(src_key) else src_key
        k.op("tensor", lambda e, t0=t0, w=w, ps=ps, sk=sk: e.matmul(
            ps[0:nparts, 0:w], lhsT[0:nrows, 0:nparts], src_ap[0:nrows, t0:t0 + w],
            start=True, stop=True),
            reads=[sk, lhsT_key], writes=[pkey])
        k.op("vector", lambda e, t0=t0, w=w, ps=ps: e.tensor_copy(
            out=out_sb[0:nparts, t0:t0 + w], in_=ps[0:nparts, 0:w]),
            reads=[pkey], writes=[(out_key, t0)])


class XRes:

    def __init__(self, t, keys):
        self.t = t
        self.keys = keys

    def get(self, ctx, ci, t0, w):
        return self.t[:, ci, t0:t0 + w], self.keys[ci]


class XDram:

    def __init__(self, ctx, dram_ap, dram_key, nslots=3, width=TT, pool=None):
        self.dram = dram_ap
        self.dram_key = dram_key
        self.pool = pool if pool is not None else SbufPool(
            ctx, nslots, [128, width], F32, "xd")

    def get(self, ctx, ci, t0, w):
        sl, key = self.pool.next()
        ctx.k.op("sync", lambda e, ci=ci, t0=t0, w=w, sl=sl: e.dma_start(
            out=sl[:, 0:w], in_=self.dram[ci * 128:(ci + 1) * 128, t0:t0 + w]),
            reads=[self.dram_key], writes=[key], dma=True)
        return sl[:, 0:w], key


def reduce_feat(ctx, xsrc, nchunks, outrow, outrow_key, pp, square=None,
                scale=None):
    nc, k = ctx.nc, ctx.k
    for t0 in range(0, T, TT):
        ps, pkey = pp.next()
        for ci in range(nchunks):
            xap, xkey = xsrc.get(ctx, ci, t0, TT)
            if square is not None:
                sq, sqk = square
                k.op("vector", lambda e, xap=xap: e.tensor_mul(
                    out=sq[:, 0:TT], in0=xap, in1=xap),
                    reads=[xkey], writes=[sqk])
                rhs, rk = sq[:, 0:TT], sqk
            else:
                rhs, rk = xap, xkey
            rdt = rhs.tensor.dtype if hasattr(rhs, "tensor") else rhs.dtype
            if rdt == BF16:
                oc_t, oc_k = ctx.onescol_b, ctx.onescol_b_key
            else:
                oc_t, oc_k = ctx.onescol, ctx.onescol_key
            k.op("tensor", lambda e, rhs=rhs, ci=ci, ps=ps, oc_t=oc_t: e.matmul(
                ps[0:1, 0:TT], oc_t[:, :], rhs,
                start=(ci == 0), stop=(ci == nchunks - 1)),
                reads=[rk, oc_k], writes=[pkey])
        if scale is None:
            k.op("vector", lambda e, t0=t0, ps=ps: e.tensor_copy(
                out=outrow[0:1, t0:t0 + TT], in_=ps[0:1, 0:TT]),
                reads=[pkey], writes=[(outrow_key, t0)])
        else:
            k.op("vector", lambda e, t0=t0, ps=ps: e.tensor_scalar(
                out=outrow[0:1, t0:t0 + TT], in0=ps[0:1, 0:TT], scalar1=scale,
                scalar2=None, op0=ALU.mult),
                reads=[pkey], writes=[(outrow_key, t0)])


def layernorm_fm(ctx, xsrc, out, outkeys, sc, lo=None, lokeys=None):
    nc, k = ctx.nc, ctx.k
    murow, murow_k = sc["murow"]
    varrow, varrow_k = sc["varrow"]
    mubc, mubc_k = sc["mubc"]
    rbc, rbc_k = sc["rbc"]
    sq, sqk = sc["sq"]
    reduce_feat(ctx, xsrc, NC, murow, murow_k, sc["pp"], scale=1.0 / D)
    reduce_feat(ctx, xsrc, NC, varrow, varrow_k, sc["pp"], square=sc["sq"],
                scale=1.0 / D)
    mu_reads = [(murow_k, t0) for t0 in range(0, T, TT)]
    ss_reads = [(varrow_k, t0) for t0 in range(0, T, TT)]
    for t0 in range(0, T, TT):
        k.op("vector", lambda e, t0=t0: e.tensor_mul(
            out=sq[0:1, 0:TT], in0=murow[0:1, t0:t0 + TT], in1=murow[0:1, t0:t0 + TT]),
            reads=[(murow_k, t0)], writes=[sqk])
        k.op("vector", lambda e, t0=t0: e.tensor_sub(
            out=varrow[0:1, t0:t0 + TT], in0=varrow[0:1, t0:t0 + TT], in1=sq[0:1, 0:TT]),
            reads=[(varrow_k, t0), sqk], writes=[(varrow_k, t0)])
    k.op("scalar", lambda e: e.activation(out=varrow[:, :], in_=varrow[:, :], func=AF.Sqrt,
                                          bias=EPS_LN, scale=1.0),
         reads=ss_reads + [ctx.sconst(EPS_LN)], writes=ss_reads)
    k.op("vector", lambda e: e.reciprocal(out=varrow[:, :], in_=varrow[:, :]),
         reads=ss_reads, writes=ss_reads)
    bcast_rows(ctx, murow, lambda t0: (murow_k, t0), mubc, mubc_k, T, sc["pp"])
    bcast_rows(ctx, varrow, lambda t0: (varrow_k, t0), rbc, rbc_k, T, sc["pp"])
    for ci in range(NC):
        for t0 in range(0, T, TT):
            xap, xkey = xsrc.get(ctx, ci, t0, TT)
            ov = out[:, ci, t0:t0 + TT]
            if lo is None:
                k.op("vector", lambda e, xap=xap, t0=t0: e.tensor_sub(
                    out=sq[:, 0:TT], in0=xap, in1=mubc[:, t0:t0 + TT]),
                    reads=[xkey, (mubc_k, t0)], writes=[sqk])
                k.op("vector", lambda e, ov=ov, t0=t0: e.tensor_mul(
                    out=ov, in0=sq[:, 0:TT], in1=rbc[:, t0:t0 + TT]),
                    reads=[sqk, (rbc_k, t0)],
                    writes=[outkeys[ci]])
            else:
                f32c, f32k = sc["f32c"]
                lv = lo[:, ci, t0:t0 + TT]
                k.op("vector", lambda e, f32c=f32c, xap=xap, t0=t0: e.tensor_sub(
                    out=f32c[:, 0:TT], in0=xap, in1=mubc[:, t0:t0 + TT]),
                    reads=[xkey, (mubc_k, t0)], writes=[f32k])
                k.op("vector", lambda e, f32c=f32c, t0=t0: e.tensor_mul(
                    out=f32c[:, 0:TT], in0=f32c[:, 0:TT], in1=rbc[:, t0:t0 + TT]),
                    reads=[f32k, (rbc_k, t0)], writes=[f32k])
                k.op("vector", lambda e, ov=ov, f32c=f32c: e.tensor_copy(
                    out=ov, in_=f32c[:, 0:TT]),
                    reads=[f32k], writes=[outkeys[ci]])
                k.op("vector", lambda e, lv=lv, ov=ov, f32c=f32c: e.tensor_sub(
                    out=lv, in0=f32c[:, 0:TT], in1=ov),
                    reads=[f32k, outkeys[ci]], writes=[lokeys[ci]])


def proj_fm(ctx, in_t, in_keys, n_in, w_dram, wstage, wstage_keyb, n_out, pp,
            evac, w_T=False, tok_range=None, rhs_sel=None, n_free=TT):
    nc, k = ctx.nc, ctx.k
    t_lo, t_hi = tok_range or (0, T)
    for oc in range(n_out):
        for ic in range(n_in):
            if w_T:
                k.op("scalar", lambda e, oc=oc, ic=ic: e.dma_start_transpose(
                    wstage[:, ic, :],
                    w_dram[oc * 128:(oc + 1) * 128, ic * 128:(ic + 1) * 128]),
                    writes=[(wstage_keyb, ic)], dma=True)
            else:
                k.op("sync", lambda e, oc=oc, ic=ic: e.dma_start(
                    out=wstage[:, ic, :],
                    in_=w_dram[ic * 128:(ic + 1) * 128, oc * 128:(oc + 1) * 128]),
                    writes=[(wstage_keyb, ic)], dma=True)
        for t0 in range(t_lo, t_hi, n_free):
            w = min(n_free, t_hi - t0)
            ps, pkey = pp.next()
            for ic in range(n_in):
                if rhs_sel is not None:
                    rhs, rk = rhs_sel(ic, t0, w)
                else:
                    rhs, rk = in_t[:, ic, t0:t0 + w], in_keys[ic]
                k.op("tensor", lambda e, oc=oc, ic=ic, rhs=rhs, ps=ps, w=w: e.matmul(
                    ps[:, 0:w], wstage[:, ic, :], rhs,
                    start=(ic == 0), stop=(ic == n_in - 1)),
                    reads=[(wstage_keyb, ic), rk], writes=[pkey])
            evac(oc, t0, w, ps, pkey)


def residual_add(ctx, x, xkeys, src, srckeys, nchunks=NC):
    nc, k = ctx.nc, ctx.k
    for ci in range(nchunks):
        k.op("vector", lambda e, ci=ci: e.tensor_add(
            out=x[:, ci, :], in0=x[:, ci, :], in1=src[:, ci, :]),
            reads=[xkeys[ci], srckeys[ci]], writes=[xkeys[ci]])

"""ffn0: x += swiglu_int8(LN(x)).  HIDDEN=3072 (24 chunks), dequant 1/127.

Weights (prepped bf16 DRAM):
  wg_bf, wu_bf: [3072, 1024]  (lhsT via transpose-load: [d 128, h 128])
  wd_bf:       [1024, 3072]  (lhsT via transpose-load: [h 128, o 128])
Scale folding: silu(s*G) via activation scale; final evac multiplies s^2.
"""
import numpy as np

HID = 3072
NHC = HID // 128
S_DQ = 1.0 / 127.0
HALF = 1024
FT = 512


def alloc_ffn0(ctx):
    nc = ctx.nc
    a = {}
    a["h"] = nc.alloc_sbuf_tensor("ffn_h", [128, NHC, HALF], BF16)
    a["wg"] = nc.alloc_sbuf_tensor("ffn_wg", [128, NC, 128], BF16)
    a["wu"] = nc.alloc_sbuf_tensor("ffn_wu", [128, NC, 128], BF16)
    a["wd"] = nc.alloc_sbuf_tensor("ffn_wd", [128, NHC, 128], BF16)
    a["sg"] = nc.alloc_sbuf_tensor("ffn_sg", [128, FT], F32)
    a["psG"] = nc.alloc_psum_tensor("ffn_psG", [128, FT], F32)
    a["psU"] = nc.alloc_psum_tensor("ffn_psU", [128, FT], F32)
    a["psY"] = nc.alloc_psum_tensor("ffn_psY", [128, FT], F32)
    return a


def step_ffn0x(ctx, xsrc, xout_writer, xln, xlnkeys, wgT_bf, wuT_bf, wdT_bf, a, lnsc,
               tag="f0", wdram_keys=()):
    nc, k = ctx.nc, ctx.k
    layernorm_fm(ctx, xsrc, xln, xlnkeys, lnsc)
    h = a["h"]
    wgT3 = wgT_bf.rearrange("(ic p) h -> p ic h", p=128)
    wuT3 = wuT_bf.rearrange("(ic p) h -> p ic h", p=128)
    wdT3 = wdT_bf.rearrange("(hc p) d -> p hc d", p=128)
    for half in range(2):
        th0 = half * HALF
        hk = (tag, half, "h")
        for hc in range(NHC):
            b = hc % 2
            k.op("sync", lambda e, hc=hc, b=b: e.dma_start(
                out=a["wg"][:, b, :, :], in_=wgT3[:, :, hc * 128:(hc + 1) * 128]),
                reads=list(wdram_keys), writes=[(tag, "wg", b)], dma=True)
            k.op("sync", lambda e, hc=hc, b=b: e.dma_start(
                out=a["wu"][:, b, :, :], in_=wuT3[:, :, hc * 128:(hc + 1) * 128]),
                reads=list(wdram_keys), writes=[(tag, "wu", b)], dma=True)
            for tt in range(0, HALF, FT):
                t0 = th0 + tt
                p = (tt // FT) % 2
                psG = a["psG"][:, p, :]
                psU = a["psU"][:, p, :]
                for ic in range(NC):
                    k.op("tensor", lambda e, b=b, ic=ic, t0=t0, psG=psG: e.matmul(
                        psG, a["wg"][:, b, ic, :], xln[:, ic, t0:t0 + FT],
                        start=(ic == 0), stop=(ic == NC - 1)),
                        reads=[(tag, "wg", b), xlnkeys[ic]], writes=[(tag, "psG", p)])
                sgp = a["sg"][:, p, :]
                k.op("scalar", lambda e, sgp=sgp, psG=psG: e.activation(
                    out=sgp, in_=psG, func=AF.Silu, scale=S_DQ),
                    reads=[(tag, "psG", p)], writes=[(tag, "sg", p)])
                for ic in range(NC):
                    k.op("tensor", lambda e, b=b, ic=ic, t0=t0, psU=psU: e.matmul(
                        psU, a["wu"][:, b, ic, :], xln[:, ic, t0:t0 + FT],
                        start=(ic == 0), stop=(ic == NC - 1)),
                        reads=[(tag, "wu", b), xlnkeys[ic]], writes=[(tag, "psU", p)])
                k.op("vector", lambda e, hc=hc, tt=tt, psU=psU, sgp=sgp: e.tensor_mul(
                    out=h[:, hc, tt:tt + FT], in0=psU, in1=sgp),
                    reads=[(tag, "psU", p), (tag, "sg", p)], writes=[(hk, hc)])
        for oc in range(NC):
            b = oc % 2
            k.op("sync", lambda e, oc=oc, b=b: e.dma_start(
                out=a["wd"][:, b, :, :], in_=wdT3[:, :, oc * 128:(oc + 1) * 128]),
                reads=list(wdram_keys), writes=[(tag, "wd", b)], dma=True)
            for tt in range(0, HALF, FT):
                t0 = th0 + tt
                p = (tt // FT) % 2
                psY = a["psY"][:, p, :]
                for hc in range(NHC):
                    k.op("tensor", lambda e, b=b, hc=hc, tt=tt, psY=psY: e.matmul(
                        psY, a["wd"][:, b, hc, :], h[:, hc, tt:tt + FT],
                        start=(hc == 0), stop=(hc == NHC - 1)),
                        reads=[(tag, "wd", b), (hk, hc)], writes=[(tag, "psY", p)])
                xout_writer(oc, t0, psY, (tag, "psY", p))


def step_ffn0(ctx, x, xkeys, xln, xlnkeys, wg_bf, wu_bf, wd_bf, a, lnsc, tag="f0",
              wdram_keys=()):
    nc, k = ctx.nc, ctx.k
    layernorm_fm(ctx, XRes(x, xkeys), xln, xlnkeys, lnsc)
    h = a["h"]
    for half in range(2):
        th0 = half * HALF
        hk = (tag, half, "h")
        for hc in range(NHC):
            for ic in range(NC):
                k.op("scalar", lambda e, hc=hc, ic=ic: e.dma_start_transpose(
                    a["wg"][:, ic, :], wg_bf[hc * 128:(hc + 1) * 128, ic * 128:(ic + 1) * 128]),
                    reads=list(wdram_keys), writes=[(tag, "wg", ic)], dma=True)
                k.op("scalar", lambda e, hc=hc, ic=ic: e.dma_start_transpose(
                    a["wu"][:, ic, :], wu_bf[hc * 128:(hc + 1) * 128, ic * 128:(ic + 1) * 128]),
                    reads=list(wdram_keys), writes=[(tag, "wu", ic)], dma=True)
            for tt in range(0, HALF, FT):
                t0 = th0 + tt
                for ic in range(NC):
                    k.op("tensor", lambda e, hc=hc, ic=ic, t0=t0: e.matmul(
                        a["psG"][:, :], a["wg"][:, ic, :], xln[:, ic, t0:t0 + FT],
                        start=(ic == 0), stop=(ic == NC - 1)),
                        reads=[(tag, "wg", ic), xlnkeys[ic]], writes=[(tag, "psG")])
                k.op("scalar", lambda e: e.activation(
                    out=a["sg"][:, :], in_=a["psG"][:, :], func=AF.Silu, scale=S_DQ),
                    reads=[(tag, "psG")], writes=[(tag, "sg")])
                for ic in range(NC):
                    k.op("tensor", lambda e, hc=hc, ic=ic, t0=t0: e.matmul(
                        a["psU"][:, :], a["wu"][:, ic, :], xln[:, ic, t0:t0 + FT],
                        start=(ic == 0), stop=(ic == NC - 1)),
                        reads=[(tag, "wu", ic), xlnkeys[ic]], writes=[(tag, "psU")])
                k.op("vector", lambda e, hc=hc, tt=tt: e.tensor_mul(
                    out=h[:, hc, tt:tt + FT], in0=a["psU"][:, :], in1=a["sg"][:, :]),
                    reads=[(tag, "psU"), (tag, "sg")], writes=[(hk, hc)])
        for oc in range(NC):
            for hc in range(NHC):
                k.op("scalar", lambda e, oc=oc, hc=hc: e.dma_start_transpose(
                    a["wd"][:, hc, :], wd_bf[oc * 128:(oc + 1) * 128, hc * 128:(hc + 1) * 128]),
                    reads=list(wdram_keys), writes=[(tag, "wd", hc)], dma=True)
            for tt in range(0, HALF, FT):
                t0 = th0 + tt
                for hc in range(NHC):
                    k.op("tensor", lambda e, oc=oc, hc=hc, tt=tt: e.matmul(
                        a["psY"][:, :], a["wd"][:, hc, :], h[:, hc, tt:tt + FT],
                        start=(hc == 0), stop=(hc == NHC - 1)),
                        reads=[(tag, "wd", hc), (hk, hc)], writes=[(tag, "psY")])
                k.op("vector", lambda e, oc=oc, t0=t0: e.scalar_tensor_tensor(
                    out=x[:, oc, t0:t0 + FT], in0=a["psY"][:, :], scalar=S_DQ * S_DQ,
                    in1=x[:, oc, t0:t0 + FT], op0=ALU.mult, op1=ALU.add),
                    reads=[(tag, "psY"), xkeys[oc]], writes=[xkeys[oc]])

"""Causal MHA sublayer (attn_a0 / attn_b0 variants), feature-major.

attn_a0: x += rope_mha(LN(x), W.T) + bo            (rope; weights applied x@W.T)
attn_b0: x += LNaff(mha(LN(x)) @ Wo + bo)          (no rope; gamma/beta post-LN)

Scores: S^T layout [k-part, q-free]; exp(S/8 - M_h) with per-head bound
M_h = (max|q|^2 + max|k|^2)/16; softmax normalizer via ones-column in V2.
xln tensor is reused as O_fm after projections; q/kk reused for b0's y/z.
"""
import numpy as np

NH = 16
HD = 64


def alloc_mha(ctx):
    nc = ctx.nc
    a = {}
    a["q"] = nc.alloc_sbuf_tensor("mha_q", [128, NC, T], BF16)
    a["kk"] = nc.alloc_sbuf_tensor("mha_k", [128, NC, T], BF16)
    a["v2"] = nc.alloc_sbuf_tensor("mha_v2", [128, 16, 65 * (NH // 2)], BF16)
    a["E"] = nc.alloc_sbuf_tensor("mha_E", [128, T], BF16)
    a["wst"] = nc.alloc_sbuf_tensor("mha_wst", [128, NC, 128], BF16)
    a["wstv"] = nc.alloc_sbuf_tensor("mha_wstv", [128, NC, 512], BF16)
    a["bias"] = nc.alloc_sbuf_tensor("mha_bias", [128, NC], F32)
    a["bias2"] = nc.alloc_sbuf_tensor("mha_bias2", [128, NC], F32)
    a["bias3"] = nc.alloc_sbuf_tensor("mha_bias3", [128, NC], F32)
    a["brow"] = nc.alloc_sbuf_tensor("mha_brow", [1, D], BF16)
    a["bbc"] = nc.alloc_sbuf_tensor("mha_bbc", [128, D], BF16)
    a["nrm"] = nc.alloc_sbuf_tensor("mha_nrm", [1, 2 * NH, 4], F32)
    a["mrow"] = nc.alloc_sbuf_tensor("mha_mrow", [1, NH], F32)
    a["mcols"] = nc.alloc_sbuf_tensor("mha_mcols", [128, NH], F32)
    a["srow"] = nc.alloc_sbuf_tensor("mha_srow", [1, T], F32)
    a["gb"] = nc.alloc_sbuf_tensor("mha_gb", [128, 2 * NC], F32)
    a["stage"] = SbufPool(ctx, 2, [128, TT], F32, "mha_stage")
    a["ropeA"] = nc.alloc_sbuf_tensor("mha_ropeA", [128, 1, TT], F32)
    a["ropeB"] = nc.alloc_sbuf_tensor("mha_ropeB", [128, 1, TT], F32)
    a["psS"] = nc.alloc_psum_tensor("mha_psS", [128, 1024], F32)
    a["psO"] = [nc.alloc_psum_tensor(f"mha_psO{i}", [65, TT], F32) for i in range(4)]
    a["triu"], a["triu_key"] = load_const_tile(
        ctx, np.triu(np.ones((128, 128), dtype=BF)), "triu", BF16)
    return a


def rope_tables(ctx, base, f32=False):
    half = HD // 2
    theta = (1.0 / base) ** (np.arange(half, dtype=np.float64) / half)
    ang = np.arange(T, dtype=np.float64)[:, None] * theta[None, :]
    cos = np.cos(ang).T.astype(np.float32)
    sin = np.sin(ang).T.astype(np.float32)
    cos128 = np.tile(cos, (4, 1))
    sinp = np.concatenate([-sin, sin, -sin, sin], axis=0)
    if f32:
        ct, ck = load_const_tile(ctx, cos128, f"ropecf{int(base)}", F32)
        st, sk = load_const_tile(ctx, sinp, f"ropesf{int(base)}", F32)
    else:
        ct, ck = load_const_tile(ctx, cos128.astype(BF), f"ropec{int(base)}", BF16)
        st, sk = load_const_tile(ctx, sinp.astype(BF), f"ropes{int(base)}", BF16)
    return (ct, ck, st, sk)


def _rope_chunk(ctx, a, qt, qkeys, ci, rt, tag):
    nc, k = ctx.nc, ctx.k
    ct, ck, st, sk = rt
    c1, c2 = a["c1"], a["c2"]
    k.op("vector", lambda e, ci=ci: e.tensor_mul(out=c1[:, :], in0=qt[:, ci, :], in1=ct[:, :]),
         reads=[qkeys[ci], ck], writes=[("mc1",)])
    for j in range(4):
        so, do = 32 * (j ^ 1), 32 * j
        k.op("vector", lambda e, ci=ci, so=so, do=do: e.tensor_copy(
            out=c2[do:do + 32, :], in_=qt[so:so + 32, ci, :]),
            reads=[qkeys[ci]], writes=[("mc2",)])
    k.op("vector", lambda e: e.tensor_mul(out=c2[:, :], in0=c2[:, :], in1=st[:, :]),
         reads=[("mc2",), sk], writes=[("mc2",)])
    k.op("vector", lambda e, ci=ci: e.tensor_add(
        out=qt[:, ci, :], in0=c1[:, :], in1=c2[:, :]),
        reads=[("mc1",), ("mc2",)], writes=[qkeys[ci]])


def _proj_to_fm(ctx, a, rhs_t, rhs_keys, w_bf, wdep, out, outkeys, bias_col, bias_key,
                pp, tag, w_T, evac_override=None):
    nc, k = ctx.nc, ctx.k
    for oc in range(NC):
        for ic in range(NC):
            if w_T:
                k.op("scalar", lambda e, oc=oc, ic=ic: e.dma_start_transpose(
                    a["wst"][:, ic, :],
                    w_bf[oc * 128:(oc + 1) * 128, ic * 128:(ic + 1) * 128]),
                    reads=[wdep], writes=[("mwst", ic)], dma=True)
            else:
                k.op("sync", lambda e, oc=oc, ic=ic: e.dma_start(
                    out=a["wst"][:, ic, :],
                    in_=w_bf[ic * 128:(ic + 1) * 128, oc * 128:(oc + 1) * 128]),
                    reads=[wdep], writes=[("mwst", ic)], dma=True)
        for t0 in range(0, T, TT):
            ps, pkey = pp.next()
            for ic in range(NC):
                rr = rhs_t[:, ic, t0:t0 + TT]
                k.op("tensor", lambda e, oc=oc, ic=ic, rr=rr, ps=ps: e.matmul(
                    ps[:, 0:TT], a["wst"][:, ic, :], rr,
                    start=(ic == 0), stop=(ic == NC - 1)),
                    reads=[("mwst", ic), rhs_keys[ic]], writes=[pkey])
            if evac_override is not None:
                evac_override(oc, t0, ps, pkey)
            elif bias_col is not None:
                bc = bias_col[:, oc:oc + 1]
                k.op("scalar", lambda e, oc=oc, t0=t0, ps=ps, bc=bc: e.activation(
                    out=out[:, oc, t0:t0 + TT], in_=ps[:, 0:TT], func=AF.Identity,
                    bias=bc, scale=1.0),
                    reads=[pkey, bias_key], writes=[outkeys[oc]])
            else:
                k.op("scalar", lambda e, oc=oc, t0=t0, ps=ps: e.copy(
                    out=out[:, oc, t0:t0 + TT], in_=ps[:, 0:TT]),
                    reads=[pkey], writes=[outkeys[oc]])


def _maxsq_rows(ctx, a, src, srckeys, off, tag, pp):
    nc, k = ctx.nc, ctx.k
    c1 = a["ropeA"][:, 0, :]
    c1k = ("ropeA",)
    for ci in range(NC):
        for qt in range(4):
            k.op("vector", lambda e, ci=ci, qt=qt: e.tensor_mul(
                out=c1[:, :], in0=src[:, ci, qt * TT:(qt + 1) * TT],
                in1=src[:, ci, qt * TT:(qt + 1) * TT]),
                reads=[srckeys[ci]], writes=[c1k])
            for blk in range(2):
                h = 2 * ci + blk
                lhs = ctx.onescol[64 * blk:64 * blk + 64, :]
                ps, pkey = pp.next()
                rr = c1[64 * blk:64 * blk + 64, :]
                k.op("tensor", lambda e, rr=rr, lhs=lhs, ps=ps: e.matmul(
                    ps[0:1, 0:TT], lhs, rr, start=True, stop=True),
                    reads=[c1k, ctx.onescol_key], writes=[pkey])
                ov = a["nrm"][0:1, off + h, qt:qt + 1]
                k.op("vector", lambda e, ov=ov, ps=ps: e.reduce_max(
                    out=ov, in_=ps[0:1, 0:TT], axis=AX.X),
                    reads=[pkey], writes=[("mnrm",)])


def step_mha(ctx, a, xsrc, xout_writer, xln, xlnkeys, W, lnsc, pp, cfg):
    nc, k = ctx.nc, ctx.k
    tag = ctx.name("mha")
    q, kk, v2, E = a["q"], a["kk"], a["v2"], a["E"]
    qkeys = [("mq", ci) for ci in range(NC)]
    kkeys = [("mk", ci) for ci in range(NC)]
    Ekey = ("mE",)

    layernorm_fm(ctx, xsrc, xln, xlnkeys, lnsc)

    def load_bias_col(name, tile):
        if W.get(name) is None:
            return None, None
        ap, dep = W[name]
        key = ("mbias", id(tile) % 7)
        k.op("sync", lambda e, ap=ap, tile=tile: e.dma_start(out=tile[:, :], in_=ap),
             reads=[dep], writes=[key], dma=True)
        return tile, key

    def mk_rope_evac(out_t, outkeys_t, bcol, bck):
        ct, ck2, st_, sk2 = cfg["rope"]

        def evac(oc, t0, ps, pkey):
            ra = a["ropeA"][:, 0, :]
            rb = a["ropeB"][:, 0, :]
            rak, rbk = ("ropeA",), ("ropeB",)
            reads = [pkey] + ([bck] if bck is not None else [])
            if bcol is not None:
                bc = bcol[:, oc:oc + 1]
                k.op("scalar", lambda e, ps=ps, ra=ra, bc=bc: e.activation(
                    out=ra[:, :], in_=ps[:, 0:TT], func=AF.Identity, bias=bc, scale=1.0),
                    reads=reads, writes=[rak])
            else:
                k.op("scalar", lambda e, ps=ps, ra=ra: e.copy(out=ra[:, :], in_=ps[:, 0:TT]),
                     reads=reads, writes=[rak])
            for j in range(4):
                so, do = 32 * (j ^ 1), 32 * j
                k.op("vector", lambda e, ra=ra, rb=rb, so=so, do=do: e.tensor_copy(
                    out=rb[do:do + 32, :], in_=ra[so:so + 32, :]),
                    reads=[rak], writes=[rbk])
            k.op("vector", lambda e, ra=ra, t0=t0: e.tensor_mul(
                out=ra[:, :], in0=ra[:, :], in1=ct[:, t0:t0 + TT]),
                reads=[rak, ck2], writes=[rak])
            k.op("vector", lambda e, rb=rb, t0=t0: e.tensor_mul(
                out=rb[:, :], in0=rb[:, :], in1=st_[:, t0:t0 + TT]),
                reads=[rbk, sk2], writes=[rbk])
            k.op("vector", lambda e, oc=oc, t0=t0, ra=ra, rb=rb: e.tensor_add(
                out=out_t[:, oc, t0:t0 + TT], in0=ra[:, :], in1=rb[:, :]),
                reads=[rak, rbk], writes=[outkeys_t[oc]])
        return evac

    bq, bqk = load_bias_col("bq", a["bias"])
    if cfg.get("rope") is not None:
        _proj_to_fm(ctx, a, xln, xlnkeys, W["q"][0], W["q"][2], None, None, None, None,
                    pp, tag + "q", W["q"][1], evac_override=mk_rope_evac(q, qkeys, bq, bqk))
        bk, bkk = load_bias_col("bk", a["bias2"])
        _proj_to_fm(ctx, a, xln, xlnkeys, W["k"][0], W["k"][2], None, None, None, None,
                    pp, tag + "k", W["k"][1], evac_override=mk_rope_evac(kk, kkeys, bk, bkk))
    else:
        _proj_to_fm(ctx, a, xln, xlnkeys, W["q"][0], W["q"][2], q, qkeys, bq, bqk,
                    pp, tag + "q", W["q"][1])
        bk, bkk = load_bias_col("bk", a["bias2"])
        _proj_to_fm(ctx, a, xln, xlnkeys, W["k"][0], W["k"][2], kk, kkeys, bk, bkk,
                    pp, tag + "k", W["k"][1])

    _maxsq_rows(ctx, a, q, qkeys, 0, tag + "nq", pp)
    _maxsq_rows(ctx, a, kk, kkeys, NH, tag + "nk", pp)
    nrm_rd = [("mnrm",)]
    k.op("vector", lambda e: e.reduce_max(
        out=a["nrm"][0:1, :, 0:1], in_=a["nrm"][0:1, :, :], axis=AX.X),
        reads=nrm_rd, writes=[("mnrm",)])
    k.op("vector", lambda e: e.scalar_tensor_tensor(
        out=a["mrow"][0:1, :], in0=a["nrm"][0:1, 0:NH, 0], scalar=0.0,
        in1=a["nrm"][0:1, NH:2 * NH, 0], op0=ALU.add, op1=ALU.add),
        reads=[("mnrm",)], writes=[("mrow0",)])
    k.op("vector", lambda e: e.tensor_scalar(
        out=a["mrow"][0:1, :], in0=a["mrow"][0:1, :], scalar1=-1.0 / 16.0, scalar2=None,
        op0=ALU.mult),
        reads=[("mrow0",)], writes=[("mrowF",)])
    for h in range(NH):
        ps, pkey = pp.next()
        rr = a["mrow"][0:1, h:h + 1]
        k.op("tensor", lambda e, rr=rr, ps=ps: e.matmul(
            ps[0:128, 0:1], ctx.ones1_128[:, :], rr, start=True, stop=True),
            reads=[("mrowF",), ctx.ones1_128_key], writes=[pkey])
        ov = a["mcols"][:, h:h + 1]
        k.op("vector", lambda e, ov=ov, ps=ps: e.tensor_copy(out=ov, in_=ps[0:128, 0:1]),
             reads=[pkey], writes=[("mcol", h)])

    if W.get("bv") is not None:
        ap, dep = W["bv"]
        k.op("gpsimd", lambda e, ap=ap: e.dma_start(out=a["brow"][:, :], in_=ap),
             reads=[dep], writes=[("mbrow",)], dma=True)
        bcast_rows(ctx, a["brow"], ("mbrow",), a["bbc"], ("mbbc"), D, pp)
        bbc_rd = [(("mbbc"), t0) for t0 in range(0, D, TT)]
    else:
        bbc_rd = None
    wv_bf, wv_T, wv_dep = W["v"]
    O_keys = xlnkeys
    psO = a["psO"]
    for halfo in range(2):
        o0 = halfo * 512
        k.op("vector", lambda e: e.memset(
            v2.rearrange("p c (h x) -> p c h x", x=65)[:, :, :, 64:65], 1.0),
            writes=[("mv2o",)])
        for ic in range(NC):
            if wv_T:
                for j in range(4):
                    k.op("scalar", lambda e, ic=ic, j=j, o0=o0: e.dma_start_transpose(
                        a["wstv"][:, ic, j * 128:(j + 1) * 128],
                        wv_bf[o0 + j * 128:o0 + (j + 1) * 128, ic * 128:(ic + 1) * 128]),
                        reads=[wv_dep], writes=[("mwstv", ic)], dma=True)
            else:
                k.op("sync", lambda e, ic=ic, o0=o0: e.dma_start(
                    out=a["wstv"][:, ic, :], in_=wv_bf[ic * 128:(ic + 1) * 128, o0:o0 + 512]),
                    reads=[wv_dep], writes=[("mwstv", ic)], dma=True)
        for tc in range(16):
            ps, pkey = pp.next()
            for ic in range(NC):
                rr = xln[:, ic, tc * 128:(tc + 1) * 128]
                k.op("tensor", lambda e, ic=ic, rr=rr, ps=ps: e.matmul(
                    ps[:, 0:512], rr, a["wstv"][:, ic, :],
                    start=(ic == 0), stop=(ic == NC - 1)),
                    reads=[("mwstv", ic), xlnkeys[ic]], writes=[pkey])
            outv = v2[:, tc, :].rearrange("p (h x) -> p h x", x=65)[:, :, 0:64]
            psv = ps[:, 0:512].rearrange("p (h x) -> p h x", x=64)
            if bbc_rd is not None:
                bv_sl = a["bbc"][:, o0:o0 + 512].rearrange("p (h x) -> p h x", x=64)
                k.op("vector", lambda e, outv=outv, psv=psv, bv_sl=bv_sl: e.tensor_add(
                    out=outv, in0=psv, in1=bv_sl),
                    reads=[pkey] + bbc_rd, writes=[("mv2", tc)])
            else:
                k.op("vector", lambda e, outv=outv, psv=psv: e.tensor_copy(
                    out=outv, in_=psv), reads=[pkey], writes=[("mv2", tc)])
        _mha_heads(ctx, a, cfg, tag, q, qkeys, kk, kkeys, v2, E, Ekey, qkeys,
                   psO, pp, range(8 * halfo, 8 * halfo + 8), halfo)

    _mha_post(ctx, a, cfg, tag, q, qkeys, kk, kkeys, xln, xlnkeys, W, lnsc, pp,
              xout_writer, load_bias_col)


def _mha_heads(ctx, a, cfg, tag, q, qkeys, kk, kkeys, v2, E, Ekey, O_keys,
               psO, pp, heads, halfo):
    nc, k = ctx.nc, ctx.k
    for h in heads:
        hh = h - 8 * halfo
        hc, blk = h // 2, h % 2
        b0 = 64 * blk
        for kc in range(16):
            qlo = kc * 128
            qlo_f = (qlo // TT) * TT
            for q0 in range(qlo_f, T, 1024):
                wq = min(1024, T - q0)
                for qq in range(q0, q0 + wq, TT):
                    lhs = kk[b0:b0 + 64, hc, kc * 128:(kc + 1) * 128]
                    rr = q[b0:b0 + 64, hc, qq:qq + TT]
                    k.op("tensor", lambda e, lhs=lhs, rr=rr, qq=qq, q0=q0: e.matmul(
                        a["psS"][:, qq - q0:qq - q0 + TT], lhs, rr, start=True, stop=True),
                        reads=[kkeys[hc], qkeys[hc]], writes=[("mpsS",)])
                mc = a["mcols"][:, h:h + 1]
                k.op("scalar", lambda e, q0=q0, wq=wq, mc=mc: e.activation(
                    out=E[:, q0:q0 + wq], in_=a["psS"][:, 0:wq], func=AF.Exp,
                    scale=0.125, bias=mc),
                    reads=[("mpsS",), ("mcol", h)], writes=[Ekey])
            if qlo > qlo_f:
                k.op("vector", lambda e, qlo_f=qlo_f, qlo=qlo: e.memset(
                    E[:, qlo_f:qlo], 0.0), writes=[Ekey])
            k.op("vector", lambda e, qlo=qlo: e.tensor_mul(
                out=E[:, qlo:qlo + 128], in0=E[:, qlo:qlo + 128], in1=a["triu"][:, :]),
                reads=[Ekey, a["triu_key"]], writes=[Ekey])
            lhs_v = v2[:, kc, 65 * hh:65 * hh + 65]
            for qt in range(qlo_f // TT, 4):
                rr = E[:, qt * TT:(qt + 1) * TT]
                k.op("tensor", lambda e, lhs_v=lhs_v, rr=rr, qt=qt, kc=kc: e.matmul(
                    psO[qt][:, :], lhs_v, rr,
                    start=(kc == 0), stop=(kc == min(qt * 4 + 3, 15))),
                    reads=[("mv2", kc), ("mv2o",), Ekey],
                    writes=[("mpsO", qt)])
        for qt in range(4):
            t0 = qt * TT
            sv = a["srow"][0:1, t0:t0 + TT]
            k.op("vector", lambda e, sv=sv, qt=qt: e.tensor_copy(
                out=sv, in_=psO[qt][64:65, :]),
                reads=[("mpsO", qt)], writes=[("msrow",)])
            k.op("vector", lambda e, sv=sv: e.reciprocal(out=sv, in_=sv),
                 reads=[("msrow",)], writes=[("msrow",)])
            ps, pkey = pp.next()
            outp = ps[b0:b0 + 64, 0:TT]
            k.op("tensor", lambda e, sv=sv, outp=outp, b0=b0: e.matmul(
                outp, ctx.ones1_128[:, 0:64], sv, start=True, stop=True,
                tile_position=(0, b0)),
                reads=[("msrow",), ctx.ones1_128_key], writes=[pkey])
            st1, k1 = a["stage"].next()
            k.op("scalar", lambda e, st1=st1, qt=qt, b0=b0: e.copy(
                out=st1[b0:b0 + 64, :], in_=psO[qt][0:64, :]),
                reads=[("mpsO", qt)], writes=[k1])
            ov = q[b0:b0 + 64, hc, t0:t0 + TT]
            k.op("vector", lambda e, ov=ov, st1=st1, ps=ps, b0=b0: e.tensor_mul(
                out=ov, in0=st1[b0:b0 + 64, :], in1=ps[b0:b0 + 64, 0:TT]),
                reads=[k1, pkey], writes=[O_keys[hc]])

def _mha_post(ctx, a, cfg, tag, q, qkeys, kk, kkeys, xln, xlnkeys, W, lnsc, pp,
              xout_writer, load_bias_col):
    nc, k = ctx.nc, ctx.k
    if cfg["post"] == "res":
        bo, bok = load_bias_col("bo", a["bias3"])

        def evac(oc, t0, ps, pkey):
            xout_writer(oc, t0, ps, pkey, (bo, bok, oc))
        _proj_to_fm(ctx, a, q, qkeys, W["o"][0], W["o"][2], None, None, None, None,
                    pp, tag + "o", W["o"][1], evac_override=evac)
    else:
        bo, bok = load_bias_col("bo", a["bias3"])
        ykeys = []
        for oc in range(NC):
            yk = ("my", oc)
            ykeys.append(yk)

        def evacy(oc, t0, ps, pkey):
            bc = bo[:, oc:oc + 1]
            k.op("scalar", lambda e, oc=oc, t0=t0, ps=ps, bc=bc: e.activation(
                out=xln[:, oc, t0:t0 + TT], in_=ps[:, 0:TT], func=AF.Identity,
                bias=bc, scale=1.0),
                reads=[pkey, bok], writes=[xlnkeys[oc], ykeys[oc]])
        _proj_to_fm(ctx, a, q, qkeys, W["o"][0], W["o"][2], None, None, None, None,
                    pp, tag + "o", W["o"][1], evac_override=evacy)
        zkeys = [("mz", ci) for ci in range(NC)]
        for ci in range(NC):
            k.op("vector", lambda e, ci=ci: e.memset(kk[0:1, ci, 0:1], 0.0),
                 writes=[kkeys[ci], zkeys[ci]])
        layernorm_fm(ctx, XRes(xln, ykeys), kk, zkeys, lnsc)
        gbap, gbdep = cfg["gamma_beta"]
        k.op("sync", lambda e: e.dma_start(out=a["gb"][:, :], in_=gbap),
             reads=[gbdep], writes=[("mgb",)], dma=True)
        for oc in range(NC):
            g = a["gb"][:, oc:oc + 1]
            b = a["gb"][:, NC + oc:NC + oc + 1]
            k.op("vector", lambda e, oc=oc, g=g, b=b: e.tensor_scalar(
                out=kk[:, oc, :], in0=kk[:, oc, :], scalar1=g, scalar2=b,
                op0=ALU.mult, op1=ALU.add),
                reads=[zkeys[oc], ("mgb",)], writes=[zkeys[oc], kkeys[oc]])
        for oc in range(NC):
            for t0 in range(0, T, TT):
                xout_writer(oc, t0, None, None, ("z", kk, zkeys[oc]))

"""attn_a1 (cosine-sim counter block): x += concat([d, counter]) @ W2 where
d = LN(x), normed = d/|d|, sim = normed @ normed.T,
counter = softplus(rowsum(relu(sim)) @ W1 + b1).

Needs from caller: xln tensor (+keys) for d; a dict with scratch reused from
alloc_mha (q -> normed storage, kk -> counter storage, E, psS, srow/c1...).
W2 stored [2048, 1024] (lhsT as-is); W1 supplied as [1, 1024] row; b1 as
[128, 8] cols.
"""
import numpy as np


def step_a1(ctx, a, xsrc, xout_writer, xln, xlnkeys, wp, lnsc, pp, psC):
    nc, k = ctx.nc, ctx.k
    nrm = a["q"]
    nkeys = [("mq", ci) for ci in range(NC)]
    cnt = a["kk"]
    ckeys = [("mk", ci) for ci in range(NC)]
    E, Ekey = a["E"], ("mE",)
    psS = a["psS"]

    layernorm_fm(ctx, xsrc, xln, xlnkeys, lnsc)

    srow = a["srow"]
    reduce_feat(ctx, XRes(xln, xlnkeys), NC, srow, "a1sq", pp, square=(a["c1"][0:128, 0:TT], ("mc1",)))
    sq_rd = [("a1sq", t0) for t0 in range(0, T, TT)]
    k.op("vector", lambda e: e.tensor_scalar(
        out=srow[0:1, :], in0=srow[0:1, :], scalar1=1e-12, scalar2=None, op0=ALU.max),
        reads=sq_rd, writes=sq_rd)
    k.op("scalar", lambda e: e.activation(out=srow[0:1, :], in_=srow[0:1, :], func=AF.Sqrt),
         reads=sq_rd, writes=sq_rd)
    k.op("vector", lambda e: e.reciprocal(out=srow[0:1, :], in_=srow[0:1, :]),
         reads=sq_rd, writes=sq_rd)
    rbc, rbc_k = lnsc["rbc"]
    bcast_rows(ctx, srow, sq_rd[0], rbc, "a1rbc", T, pp)
    rbc_rd = [("a1rbc", t0) for t0 in range(0, T, TT)]
    for ci in range(NC):
        k.op("vector", lambda e, ci=ci: e.tensor_mul(
            out=nrm[:, ci, :], in0=xln[:, ci, :], in1=rbc[:, :]),
            reads=[xlnkeys[ci]] + rbc_rd, writes=[nkeys[ci]])

    for kc in range(16):
        for q0 in range(0, T, 1024):
            for qq in range(q0, q0 + 1024, TT):
                for ic in range(NC):
                    lhs = nrm[:, ic, kc * 128:(kc + 1) * 128]
                    rr = nrm[:, ic, qq:qq + TT]
                    k.op("tensor", lambda e, lhs=lhs, rr=rr, qq=qq, q0=q0, ic=ic: e.matmul(
                        psS[:, qq - q0:qq - q0 + TT], lhs, rr,
                        start=(ic == 0), stop=(ic == NC - 1)),
                        reads=[nkeys[ic]], writes=[("mpsS",)])
            k.op("scalar", lambda e, q0=q0: e.activation(
                out=E[:, q0:q0 + 1024], in_=psS[:, 0:1024], func=AF.Relu),
                reads=[("mpsS",)], writes=[Ekey])
        for t0 in range(0, T, TT):
            rr = E[:, t0:t0 + TT]
            k.op("tensor", lambda e, rr=rr, kc=kc, t0=t0: e.matmul(
                psC[0:1, t0:t0 + TT], ctx.onescol_b[:, :], rr,
                start=(kc == 0), stop=(kc == 15)),
                reads=[Ekey, ctx.onescol_b_key], writes=[("a1psC", t0)])
    for t0 in range(0, T, TT):
        k.op("vector", lambda e, t0=t0: e.tensor_copy(
            out=srow[0:1, t0:t0 + TT], in_=psC[0:1, t0:t0 + TT]),
            reads=[("a1psC", t0)], writes=[("a1cr", t0)])
    cr_rd = [("a1cr", t0) for t0 in range(0, T, TT)]

    w1ap, w1dep = wp["w1row"]
    b1ap, b1dep = wp["b1"]
    k.op("gpsimd", lambda e: e.dma_start(out=a["brow"][:, :], in_=w1ap),
         reads=[w1dep], writes=[("mbrow",)], dma=True)
    k.op("sync", lambda e: e.dma_start(out=a["bias"][:, :], in_=b1ap),
         reads=[b1dep], writes=[("mbias1",)], dma=True)
    c1 = a["c1"]
    for dc in range(NC):
        for t0 in range(0, T, TT):
            ps, pkey = pp.next()
            lhs = a["brow"][0:1, dc * 128:(dc + 1) * 128]
            rr = srow[0:1, t0:t0 + TT]
            k.op("tensor", lambda e, lhs=lhs, rr=rr, ps=ps: e.matmul(
                ps[:, 0:TT], lhs, rr, start=True, stop=True),
                reads=[("mbrow",), ("a1cr", t0)], writes=[pkey])
            st1, k1 = a["stage"].next()
            st2, k2 = a["stage"].next()
            bc = a["bias"][:, dc:dc + 1]
            k.op("scalar", lambda e, ps=ps, st1=st1, bc=bc: e.activation(
                out=st1[:, :], in_=ps[:, 0:TT], func=AF.Identity, bias=bc, scale=1.0),
                reads=[pkey, ("mbias1",)], writes=[k1])
            k.op("scalar", lambda e, st1=st1, st2=st2: e.activation(
                out=st2[:, :], in_=st1[:, :], func=AF.Abs),
                reads=[k1], writes=[k2])
            k.op("scalar", lambda e, st2=st2: e.activation(
                out=st2[:, :], in_=st2[:, :], func=AF.Exp, scale=-1.0),
                reads=[k2], writes=[k2])
            k.op("scalar", lambda e, st2=st2: e.activation(
                out=st2[:, :], in_=st2[:, :], func=AF.Ln, bias=1.0, scale=1.0),
                reads=[k2, ctx.sconst(1.0)], writes=[k2])
            k.op("vector", lambda e, st1=st1: e.tensor_relu(out=st1[:, :], in_=st1[:, :]),
                 reads=[k1], writes=[k1])
            ov = cnt[:, dc, t0:t0 + TT]
            k.op("vector", lambda e, ov=ov, st1=st1, st2=st2: e.tensor_add(
                out=ov, in0=st1[:, :], in1=st2[:, :]),
                reads=[k1, k2], writes=[ckeys[dc]])

    w2ap, w2dep = wp["w2"]
    wst = a["wst"]
    for oc in range(NC):
        for ic in range(NC):
            k.op("sync", lambda e, oc=oc, ic=ic: e.dma_start(
                out=wst[:, ic, :], in_=w2ap[ic * 128:(ic + 1) * 128, oc * 128:(oc + 1) * 128]),
                reads=[w2dep], writes=[("mwst", ic)], dma=True)
            k.op("sync", lambda e, oc=oc, ic=ic: e.dma_start(
                out=a["wstv"][:, ic, 0:128],
                in_=w2ap[1024 + ic * 128:1024 + (ic + 1) * 128, oc * 128:(oc + 1) * 128]),
                reads=[w2dep], writes=[("mwstv", ic)], dma=True)
        for t0 in range(0, T, TT):
            ps, pkey = pp.next()
            for ic in range(NC):
                rr = xln[:, ic, t0:t0 + TT]
                k.op("tensor", lambda e, ic=ic, rr=rr, ps=ps: e.matmul(
                    ps[:, 0:TT], wst[:, ic, :], rr, start=(ic == 0), stop=False),
                    reads=[("mwst", ic), xlnkeys[ic]], writes=[pkey])
            for ic in range(NC):
                rr = cnt[:, ic, t0:t0 + TT]
                k.op("tensor", lambda e, ic=ic, rr=rr, ps=ps: e.matmul(
                    ps[:, 0:TT], a["wstv"][:, ic, 0:128], rr,
                    start=False, stop=(ic == NC - 1)),
                    reads=[("mwstv", ic), ckeys[ic]], writes=[pkey])
            xout_writer(oc, t0, ps, pkey, None)

"""attn_b1: x += xi_final  where
  xi0 = LN(x); xi1 = xi0 + G(rms(xi0, n1w)); xi2 = xi1 + L(rms(xi1, n2w))
G = global latent attention (16 q-heads, 4 kv-heads via latent 256, rope base
1e6, qk_norm, causal); L = local GQA (window 256, rope base 1e4, qk_norm).

Permutation to [e|o] rope basis applied post-projection via a constant
block-diag permutation matmul (P128).
"""
import numpy as np

NH = 16
KVH = 4
HD = 64
EPS_B1 = 1e-6
WIN = 256


def perm128_const(ctx):
    P = np.zeros((128, 128), dtype=BF)
    for h in range(2):
        for j in range(32):
            P[64 * h + 2 * j, 64 * h + j] = 1
            P[64 * h + 2 * j + 1, 64 * h + 32 + j] = 1
    t, key = load_const_tile(ctx, P, "P128", BF16)
    return t, key


def _permute_chunks(ctx, a, tens, keys, chunks, pp, Pt, Pk):
    nc, k = ctx.nc, ctx.k
    for ci in chunks:
        for t0 in range(0, T, TT):
            ps, pkey = pp.next()
            rr = tens[:, ci, t0:t0 + TT]
            k.op("tensor", lambda e, rr=rr, ps=ps: e.matmul(
                ps[:, 0:TT], Pt[:, :], rr, start=True, stop=True),
                reads=[keys[ci], Pk], writes=[pkey])
            k.op("vector", lambda e, rr=rr, ps=ps: e.tensor_copy(out=rr, in_=ps[:, 0:TT]),
                 reads=[pkey], writes=[keys[ci]])


def _qknorm(ctx, a, tens, keys, chunks, pp, tag):
    nc, k = ctx.nc, ctx.k
    srow = a["srow"]
    for ci in chunks:
        k.op("vector", lambda e, ci=ci: e.tensor_mul(
            out=a["c1"][:, :], in0=tens[:, ci, :], in1=tens[:, ci, :]),
            reads=[keys[ci]], writes=[("mc1",)])
        for blk in range(2):
            b0 = 64 * blk
            lhs = ctx.onescol_b[b0:b0 + 64, :]
            for t0 in range(0, T, TT):
                ps, pkey = pp.next()
                rr = a["c1"][b0:b0 + 64, t0:t0 + TT]
                k.op("tensor", lambda e, rr=rr, lhs=lhs, ps=ps: e.matmul(
                    ps[0:1, 0:TT], lhs, rr, start=True, stop=True),
                    reads=[("mc1",), ctx.onescol_b_key], writes=[pkey])
                sv = srow[0:1, t0:t0 + TT]
                k.op("vector", lambda e, sv=sv, ps=ps: e.tensor_scalar(
                    out=sv, in0=ps[0:1, 0:TT], scalar1=1.0 / HD, scalar2=None, op0=ALU.mult),
                    reads=[pkey], writes=[("msrow",)])
            k.op("scalar", lambda e: e.activation(out=srow[0:1, :], in_=srow[0:1, :],
                                                  func=AF.Sqrt, bias=EPS_B1, scale=1.0),
                 reads=[("msrow",), ctx.sconst(EPS_B1)], writes=[("msrow",)])
            k.op("vector", lambda e: e.reciprocal(out=srow[0:1, :], in_=srow[0:1, :]),
                 reads=[("msrow",)], writes=[("msrow",)])
            for t0 in range(0, T, TT):
                ps, pkey = pp.next()
                rr = srow[0:1, t0:t0 + TT]
                outp = ps[b0:b0 + 64, 0:TT]
                k.op("tensor", lambda e, rr=rr, outp=outp, b0=b0: e.matmul(
                    outp, ctx.ones1_128[:, 0:64], rr, start=True, stop=True,
                    tile_position=(0, b0)),
                    reads=[("msrow",), ctx.ones1_128_key], writes=[pkey])
                ov = tens[b0:b0 + 64, ci, t0:t0 + TT]
                k.op("vector", lambda e, ov=ov, ps=ps, b0=b0: e.tensor_mul(
                    out=ov, in0=ov, in1=ps[b0:b0 + 64, 0:TT]),
                    reads=[pkey, keys[ci]], writes=[keys[ci]])


def _rmsnorm_w(ctx, xi, xikeys, out, outkeys, wcol, wcol_key, lnsc, pp):
    nc, k = ctx.nc, ctx.k
    murow, murow_k = lnsc["murow"]
    rbc, rbc_k = lnsc["rbc"]
    reduce_feat(ctx, XRes(xi, xikeys), NC, murow, murow_k, pp,
                square=lnsc["sq"], scale=1.0 / D)
    mr = [(murow_k, t0) for t0 in range(0, T, TT)]
    k.op("scalar", lambda e: e.activation(out=murow[0:1, :], in_=murow[0:1, :],
                                          func=AF.Sqrt, bias=EPS_B1, scale=1.0),
         reads=mr + [ctx.sconst(EPS_B1)], writes=mr)
    k.op("vector", lambda e: e.reciprocal(out=murow[0:1, :], in_=murow[0:1, :]),
         reads=mr, writes=mr)
    bcast_rows(ctx, murow, lambda t0: (murow_k, t0), rbc, rbc_k, T, pp)
    for ci in range(NC):
        for t0 in range(0, T, TT):
            ov = out[:, ci, t0:t0 + TT]
            xa = xi[:, ci, t0:t0 + TT]
            wc = wcol[:, ci:ci + 1]
            k.op("vector", lambda e, ov=ov, xa=xa, wc=wc, t0=t0: e.scalar_tensor_tensor(
                out=ov, in0=xa, scalar=wc, in1=rbc[:, t0:t0 + TT],
                op0=ALU.mult, op1=ALU.mult),
                reads=[xikeys[ci], (rbc_k, t0), wcol_key], writes=[outkeys[ci]])


def _attn_heads_b1(ctx, a, q, qkeys, kk, kkeys, v2, kv_of_h, E, Ekey, pp, psS, psO,
                   window=None):
    nc, k = ctx.nc, ctx.k
    O, O_keys = a["o"], a["okeys"]
    ks = a["kstage"]
    for h in range(NH):
        hc, blk = h // 2, h % 2
        b0 = 64 * blk
        g = h // (NH // KVH)
        gc, gblk = g // 2, g % 2
        gb0 = 64 * gblk
        if h % (NH // KVH) == 0:
            for dst0 in range(0, 128, 64):
                src = kk[gb0:gb0 + 64, gc, :]
                k.op("vector", lambda e, src=src, dst0=dst0: e.tensor_copy(
                    out=ks[dst0:dst0 + 64, :], in_=src),
                    reads=[kkeys[gc]], writes=[("mkstage",)])
        for kc in range(16):
            qlo = kc * 128
            if window is None:
                qhi = T
            else:
                qhi = min(qlo + 128 + window - 1, T)
            qlo_f = (qlo // TT) * TT
            qhi_f = min(ceil_div(qhi, TT) * TT, T)
            for q0 in range(qlo_f, qhi_f, 1024):
                wq = min(1024, qhi - q0)
                if wq <= 0:
                    continue
                for qq in range(q0, min(q0 + 1024, qhi), TT):
                    wqq = min(TT, qhi - qq)
                    lhs = ks[b0:b0 + 64, kc * 128:(kc + 1) * 128]
                    rr = q[b0:b0 + 64, hc, qq:qq + wqq]
                    k.op("tensor", lambda e, lhs=lhs, rr=rr, qq=qq, q0=q0, wqq=wqq: e.matmul(
                        psS[:, qq - q0:qq - q0 + wqq], lhs, rr, start=True, stop=True),
                        reads=[("mkstage",), qkeys[hc]], writes=[("mpsS",)])
                k.op("scalar", lambda e, q0=q0, wq=wq: e.activation(
                    out=E[:, q0:q0 + wq], in_=psS[:, 0:wq], func=AF.Exp, scale=0.125),
                    reads=[("mpsS",)], writes=[Ekey])
            if qlo > qlo_f:
                k.op("vector", lambda e, qlo_f=qlo_f, qlo=qlo: e.memset(
                    E[:, qlo_f:qlo], 0.0), writes=[Ekey])
            if window is not None and qhi < qhi_f:
                k.op("vector", lambda e, qhi=qhi, qhi_f=qhi_f: e.memset(
                    E[:, qhi:qhi_f], 0.0), writes=[Ekey])
            if window is None:
                k.op("vector", lambda e, qlo=qlo: e.tensor_mul(
                    out=E[:, qlo:qlo + 128], in0=E[:, qlo:qlo + 128], in1=a["triu"][:, :]),
                    reads=[Ekey, a["triu_key"]], writes=[Ekey])
            else:
                wm = min(qhi - qlo, 128 + window - 1)
                k.op("vector", lambda e, qlo=qlo, wm=wm: e.tensor_mul(
                    out=E[:, qlo:qlo + wm], in0=E[:, qlo:qlo + wm],
                    in1=a["wmask"][:, 0:wm]),
                    reads=[Ekey, a["wmask_key"]], writes=[Ekey])
            lhs_v = v2[:, kc, 65 * g:65 * g + 65]
            for qt in range(qlo_f // TT, ceil_div(qhi_f, TT)):
                if window is None:
                    first_kc, last_kc = 0, min(qt * 4 + 3, 15)
                else:
                    kcs = [c for c in range(16)
                           if c * 128 < (qt + 1) * TT and min(c * 128 + 128 + window - 1, T) > qt * TT]
                    first_kc, last_kc = kcs[0], kcs[-1]
                rr = E[:, qt * TT:(qt + 1) * TT]
                k.op("tensor", lambda e, lhs_v=lhs_v, rr=rr, qt=qt, kc=kc,
                     first_kc=first_kc, last_kc=last_kc: e.matmul(
                    psO[qt][:, :], lhs_v, rr,
                    start=(kc == first_kc), stop=(kc == last_kc)),
                    reads=[("mv2", kc), ("mv2o",), Ekey],
                    writes=[("mpsO", qt)])
        srow = a["srow"]
        for qt in range(4):
            ov = O[b0:b0 + 64, hc, qt * TT:(qt + 1) * TT]
            k.op("scalar", lambda e, ov=ov, qt=qt: e.copy(out=ov, in_=psO[qt][0:64, :]),
                 reads=[("mpsO", qt)], writes=[O_keys[hc]])
            sv = srow[0:1, qt * TT:(qt + 1) * TT]
            k.op("vector", lambda e, sv=sv, qt=qt: e.tensor_copy(
                out=sv, in_=psO[qt][64:65, :]),
                reads=[("mpsO", qt)], writes=[("msrow",)])
        k.op("vector", lambda e: e.reciprocal(out=srow[0:1, :], in_=srow[0:1, :]),
             reads=[("msrow",)], writes=[("msrow",)])
        for t0 in range(0, T, TT):
            ps, pkey = pp.next()
            rr = srow[0:1, t0:t0 + TT]
            outp = ps[b0:b0 + 64, 0:TT]
            k.op("tensor", lambda e, rr=rr, outp=outp, b0=b0: e.matmul(
                outp, ctx.ones1_128[:, 0:64], rr, start=True, stop=True,
                tile_position=(0, b0)),
                reads=[("msrow",), ctx.ones1_128_key], writes=[pkey])
            ov = O[b0:b0 + 64, hc, t0:t0 + TT]
            k.op("vector", lambda e, ov=ov, ps=ps, b0=b0: e.tensor_mul(
                out=ov, in0=ov, in1=ps[b0:b0 + 64, 0:TT]),
                reads=[pkey, O_keys[hc]], writes=[O_keys[hc]])


def _proj_generic(ctx, a, rhs_t, rhs_keys, n_in, w_bf, wdep, out, outkeys, n_out,
                  pp, accum_into_x=None):
    nc, k = ctx.nc, ctx.k
    wst = a["wst"]
    for oc in range(n_out):
        for ic in range(n_in):
            k.op("sync", lambda e, oc=oc, ic=ic: e.dma_start(
                out=wst[:, ic, :], in_=w_bf[ic * 128:(ic + 1) * 128, oc * 128:(oc + 1) * 128]),
                reads=[wdep], writes=[("mwst", ic)], dma=True)
        for t0 in range(0, T, TT):
            ps, pkey = pp.next()
            for ic in range(n_in):
                rr = rhs_t[:, ic, t0:t0 + TT]
                k.op("tensor", lambda e, ic=ic, rr=rr, ps=ps: e.matmul(
                    ps[:, 0:TT], wst[:, ic, :], rr, start=(ic == 0), stop=(ic == n_in - 1)),
                    reads=[("mwst", ic), rhs_keys[ic]], writes=[pkey])
            if accum_into_x is not None:
                x, xkeys = accum_into_x
                ov = x[:, oc, t0:t0 + TT]
                k.op("vector", lambda e, ov=ov, ps=ps: e.tensor_add(
                    out=ov, in0=ps[:, 0:TT], in1=ov),
                    reads=[pkey, xkeys[oc]], writes=[xkeys[oc]])
            else:
                ov = out[:, oc, t0:t0 + TT]
                k.op("scalar", lambda e, ov=ov, ps=ps: e.copy(out=ov, in_=ps[:, 0:TT]),
                     reads=[pkey], writes=[outkeys[oc]])


def step_b1(ctx, a, xsrc, xout_writer, W, lnsc, pp):
    nc, k = ctx.nc, ctx.k
    xi, xikeys = a["xi"], a["xikeys"]
    xr, xrkeys = a["xr"], a["xrkeys"]
    q, qkeys = a["q"], [("mq", ci) for ci in range(NC)]
    kk, kkeys = a["kk"], [("mk", ci) for ci in range(NC)]
    lat, latkeys = a["lat"], [("mlat", ci) for ci in range(2)]
    v2, E, Ekey = a["v2g"], a["E"], ("mE",)
    psS, psO = a["psS"], a["psO"]
    Pt, Pk = a["P128"]

    layernorm_fm(ctx, xsrc, xi, xikeys, lnsc)
    mubc_k = lnsc["mubc"][1]
    k.op("vector", lambda e: e.memset(a["E"][0:1, 0:1], 0.0),
         writes=[("mE",)] + [(mubc_k, t0) for t0 in range(0, T, TT)])
    varrow_k = lnsc["varrow"][1]
    k.op("vector", lambda e: e.memset(a["srow"][0:1, 0:1], 0.0),
         writes=[("msrow",)] + [(varrow_k, t0) for t0 in range(0, T, TT)])

    for nm, tile, key in [("n1w", a["bias"], ("mbias1",)), ("n2w", a["bias2"], ("mbias2",))]:
        ap, dep = W[nm]
        k.op("sync", lambda e, ap=ap, tile=tile: e.dma_start(out=tile[:, :], in_=ap),
             reads=[dep], writes=[key], dma=True)

    _rmsnorm_w(ctx, xi, xikeys, xr, xrkeys, a["bias"], ("mbias1",), lnsc, pp)
    _proj_generic(ctx, a, xr, xrkeys, NC, W["g_wq"][0], W["g_wq"][1], q, qkeys, NC, pp)
    _proj_generic(ctx, a, xr, xrkeys, NC, W["g_kv_down"][0], W["g_kv_down"][1],
                  lat, latkeys, 2, pp)
    _proj_generic(ctx, a, lat, latkeys, 2, W["g_k_up"][0], W["g_k_up"][1], kk, kkeys, 2, pp)
    k.op("vector", lambda e: e.memset(
        v2.rearrange("p c (h x) -> p c h x", x=65)[:, :, :, 64:65], 1.0),
        writes=[("mv2o",)])
    wstv = a["wstv"]
    for ic in range(2):
        k.op("sync", lambda e, ic=ic: e.dma_start(
            out=wstv[:, ic, 0:256], in_=W["g_v_up"][0][ic * 128:(ic + 1) * 128, :]),
            reads=[W["g_v_up"][1]], writes=[("mwstv", ic)], dma=True)
    for tc in range(16):
        ps, pkey = pp.next()
        for ic in range(2):
            rr = lat[:, ic, tc * 128:(tc + 1) * 128]
            k.op("tensor", lambda e, ic=ic, rr=rr, ps=ps: e.matmul(
                ps[:, 0:256], rr, wstv[:, ic, 0:256], start=(ic == 0), stop=(ic == 1)),
                reads=[("mwstv", ic), latkeys[ic]], writes=[pkey])
        outv = v2[:, tc, :].rearrange("p (h x) -> p h x", x=65)[:, :, 0:64]
        psv = ps[:, 0:256].rearrange("p (h x) -> p h x", x=64)
        k.op("vector", lambda e, outv=outv, psv=psv: e.tensor_copy(out=outv, in_=psv),
             reads=[pkey], writes=[("mv2", tc)])
    _permute_chunks(ctx, a, q, qkeys, range(NC), pp, Pt, Pk)
    _permute_chunks(ctx, a, kk, kkeys, range(2), pp, Pt, Pk)
    a["rt_load"]("g")
    for ci in range(NC):
        _rope_chunk(ctx, a, q, qkeys, ci, a["rt"], "b1g")
    for ci in range(2):
        _rope_chunk(ctx, a, kk, kkeys, ci, a["rt"], "b1g")
    _qknorm(ctx, a, q, qkeys, range(NC), pp, "b1gq")
    _qknorm(ctx, a, kk, kkeys, range(2), pp, "b1gk")
    _attn_heads_b1(ctx, a, q, qkeys, kk, kkeys, v2, None, E, Ekey, pp, psS, psO,
                   window=None)
    _proj_generic(ctx, a, a["o"], a["okeys"], NC, W["g_wo"][0], W["g_wo"][1],
                  None, None, NC, pp, accum_into_x=(xi, xikeys))

    _rmsnorm_w(ctx, xi, xikeys, xr, xrkeys, a["bias2"], ("mbias2",), lnsc, pp)
    _proj_generic(ctx, a, xr, xrkeys, NC, W["l_wq"][0], W["l_wq"][1], q, qkeys, NC, pp)
    _proj_generic(ctx, a, xr, xrkeys, NC, W["l_wk"][0], W["l_wk"][1], kk, kkeys, 2, pp)
    k.op("vector", lambda e: e.memset(
        v2.rearrange("p c (h x) -> p c h x", x=65)[:, :, :, 64:65], 1.0),
        writes=[("mv2o",)])
    for ic in range(NC):
        k.op("sync", lambda e, ic=ic: e.dma_start(
            out=wstv[:, ic, 0:256], in_=W["l_wv"][0][ic * 128:(ic + 1) * 128, :]),
            reads=[W["l_wv"][1]], writes=[("mwstv", ic)], dma=True)
    for tc in range(16):
        ps, pkey = pp.next()
        for ic in range(NC):
            rr = xr[:, ic, tc * 128:(tc + 1) * 128]
            k.op("tensor", lambda e, ic=ic, rr=rr, ps=ps: e.matmul(
                ps[:, 0:256], rr, wstv[:, ic, 0:256], start=(ic == 0), stop=(ic == NC - 1)),
                reads=[("mwstv", ic), xrkeys[ic]], writes=[pkey])
        outv = v2[:, tc, :].rearrange("p (h x) -> p h x", x=65)[:, :, 0:64]
        psv = ps[:, 0:256].rearrange("p (h x) -> p h x", x=64)
        k.op("vector", lambda e, outv=outv, psv=psv: e.tensor_copy(out=outv, in_=psv),
             reads=[pkey], writes=[("mv2", tc)])
    _permute_chunks(ctx, a, q, qkeys, range(NC), pp, Pt, Pk)
    _permute_chunks(ctx, a, kk, kkeys, range(2), pp, Pt, Pk)
    a["rt_load"]("l")
    for ci in range(NC):
        _rope_chunk(ctx, a, q, qkeys, ci, a["rt"], "b1l")
    for ci in range(2):
        _rope_chunk(ctx, a, kk, kkeys, ci, a["rt"], "b1l")
    _qknorm(ctx, a, q, qkeys, range(NC), pp, "b1lq")
    _qknorm(ctx, a, kk, kkeys, range(2), pp, "b1lk")
    _attn_heads_b1(ctx, a, q, qkeys, kk, kkeys, v2, None, E, Ekey, pp, psS, psO,
                   window=WIN)
    _proj_generic(ctx, a, a["o"], a["okeys"], NC, W["l_wo"][0], W["l_wo"][1],
                  None, None, NC, pp, accum_into_x=(xi, xikeys))

    for oc in range(NC):
        for t0 in range(0, T, TT):
            xout_writer(oc, t0, xi[:, oc, t0:t0 + TT], xikeys[oc], None)

"""MoE sublayer (L1_ffn): x += moe(LN(x)). 64 experts, top-8 w/ group-top4,
CAP=1024 global token-order capacity, SCALE=2.5. Expert-parallel: core c owns
experts 8c..8c+7 and tokens [2048c, 2048c+2048).

Position logic: pos[t,e] = offset_c[e] + local_exclusive_count (local scan);
offset via AllGather of per-core counts + coremask (input) matmul.
Slot table: consumers scatter token-ids at flat slot e*1024+pos into a local
[-1]-initialized table; AllReduce-max merges. Producers gather their slots'
token rows from AllGathered xln; dropped slots get weight 0 consumer-side.
"""
import numpy as np

E_TOT = 64
EPG = 8
K_TOP = 8
CAP = 1024
MOE_SCALE = 2.5
HID_I = 512
LROWS = 514
JUNK = 65540


def alloc_moe(ctx):
    nc = ctx.nc
    a = {}
    a["xln"] = nc.alloc_sbuf_tensor("moe_xln", [128, NC, T], BF16)
    a["xlnkeys"] = [("moexln", ci) for ci in range(NC)]
    a["lgt"] = nc.alloc_sbuf_tensor("moe_lgt", [128, 16, 64], F32)
    a["sc"] = nc.alloc_sbuf_tensor("moe_sc", [128, 16, 64], F32)
    a["wa"] = nc.alloc_sbuf_tensor("moe_wa", [128, 16, 64], F32)
    a["sfc"] = nc.alloc_sbuf_tensor("moe_sfc", [128, 64], F32)
    a["tmp"] = nc.alloc_sbuf_tensor("moe_tmp", [128, 64], F32)
    a["tmp2"] = nc.alloc_sbuf_tensor("moe_tmp2", [128, 64], F32)
    a["gs"] = nc.alloc_sbuf_tensor("moe_gs", [128, 8], F32)
    a["gs2"] = nc.alloc_sbuf_tensor("moe_gs2", [128, 8], F32)
    a["top8"] = nc.alloc_sbuf_tensor("moe_top8", [128, 8], F32)
    a["idxu"] = nc.alloc_sbuf_tensor("moe_idxu", [128, 8], mybir.dt.uint32)
    a["idx8"] = nc.alloc_sbuf_tensor("moe_idx8", [128, 16, 8], F32)
    a["emaj"] = nc.alloc_sbuf_tensor("moe_emaj", [128, 16, 8], F32)
    a["iu1"] = nc.alloc_sbuf_tensor("moe_iu1", [128, 8], mybir.dt.uint32)
    a["iu2"] = nc.alloc_sbuf_tensor("moe_iu2", [128, 8], mybir.dt.uint32)
    a["wn"] = nc.alloc_sbuf_tensor("moe_wn", [128, 16, 8], F32)
    a["srow8"] = nc.alloc_sbuf_tensor("moe_srow8", [128, 8], F32)
    a["wsum"] = nc.alloc_sbuf_tensor("moe_wsum", [128, 1], F32)
    a["bbc"] = nc.alloc_sbuf_tensor("moe_bbc", [128, 64], F32)
    a["gwst"] = nc.alloc_sbuf_tensor("moe_gwst", [128, NC, 128], BF16)
    a["gwst_lo"] = nc.alloc_sbuf_tensor("moe_gwstlo", [128, NC, 128], BF16)
    a["xlo"] = nc.alloc_sbuf_tensor("moe_xlo", [128, NC, T], BF16)
    a["xlokeys"] = [("moexlo", ci) for ci in range(NC)]

    a["ct"] = nc.alloc_sbuf_tensor("moe_ct", [128, 16, 64], F32)
    a["aot"] = nc.alloc_sbuf_tensor("moe_aot", [64, T], F32)
    a["aot2"] = nc.alloc_sbuf_tensor("moe_aot2", [64, T], F32)
    a["cnt"] = nc.alloc_sbuf_tensor("moe_cnt", [64, 1], F32)
    a["call"] = nc.alloc_sbuf_tensor("moe_call", [8, 64], F32)
    a["cmask"] = nc.alloc_sbuf_tensor("moe_cmask", [8, 1], F32)
    a["offc"] = nc.alloc_sbuf_tensor("moe_offc", [64, 1], F32)
    a["lidx"] = nc.alloc_sbuf_tensor("moe_lidx", [128, 16, 8], I32)
    a["lflat"] = nc.alloc_sbuf_tensor("moe_lflat", [128, 16, 8], F32)
    a["tokid"] = nc.alloc_sbuf_tensor("moe_tokid", [128, 16], F32)
    a["slabidx"] = nc.alloc_sbuf_tensor("moe_slabidx", [128, 64], I32)
    a["gidx"] = nc.alloc_sbuf_tensor("moe_gidx", [128, 64], F32)
    a["gidxi"] = nc.alloc_sbuf_tensor("moe_gidxi", [128, 64], I32)
    a["oidx"] = nc.alloc_sbuf_tensor("moe_oidx", [128, 16, 8], I32)
    a["buf_tm"] = [nc.alloc_sbuf_tensor("moe_btm0", [128, 8, 1024], BF16)]
    a["buf_fm"] = nc.alloc_sbuf_tensor("moe_bfm", [128, 8, 1024], BF16)
    a["h_fm"] = nc.alloc_sbuf_tensor("moe_hfm", [128, 4, 1024], BF16)
    a["sg"] = nc.alloc_sbuf_tensor("moe_sg", [128, 512], BF16)
    a["wg_st"] = nc.alloc_sbuf_tensor("moe_wgst", [128, 8, 128], BF16)
    a["wu_st"] = nc.alloc_sbuf_tensor("moe_wust", [128, 8, 128], BF16)
    a["wd_st"] = nc.alloc_sbuf_tensor("moe_wdst", [128, 4, 512], BF16)
    a["ytm"] = nc.alloc_sbuf_tensor("moe_ytm", [128, 1024], BF16)
    a["ydst"] = [nc.alloc_sbuf_tensor(f"moe_yd{i}", [128, 1024], BF16) for i in range(3)]
    a["acc"] = nc.alloc_sbuf_tensor("moe_acc", [128, 1024], F32)
    a["iota64"] = nc.alloc_sbuf_tensor("moe_iota64", [128, 64], F32)
    a["stage"] = SbufPool(ctx, 2, [128, 128], F32, "moe_stage")
    a["psY"] = nc.alloc_psum_tensor("moe_psY", [128, 512], F32)
    a["psG"] = nc.alloc_psum_tensor("moe_psG", [128, 512], F32)
    a["psU"] = nc.alloc_psum_tensor("moe_psU", [128, 512], F32)
    a["pst"] = nc.alloc_psum_tensor("moe_pst", [128, 128], BF16)
    a["pstf"] = nc.alloc_psum_tensor("moe_pstf", [128, 128], F32)
    return a


def moe_prep_weights(ctx, wp):
    k = ctx.k
    for nm in ("wg", "wu", "wd"):
        src, dep = wp[nm + "_i"]
        dst, dkey = wp[nm + "_bf"]
        k.op("gpsimd", lambda e, src=src, dst=dst: e.dma_start(out=dst, in_=src),
             reads=[dep] if dep else [], writes=[dkey], dma=True)


def moe_prep(ctx, a, wp, weights=True):
    nc, k = ctx.nc, ctx.k
    c_iota64 = ctx.const.dram(
        "iota64", np.tile(np.arange(64, dtype=np.float32)[None, :], (128, 1)))
    k.op("sync", lambda e: e.dma_start(out=a["iota64"][:, :], in_=c_iota64),
         writes=[("iota64",)], dma=True)
    if weights:
        moe_prep_weights(ctx, wp)
    src, dep = wp["gate_i"]
    dst, dkey = wp["gate_bf"]
    dstlo, dlokey = wp["gate_lo_bf"]
    g32 = a["aot"][:, 0:D]
    ghib = a["ytm"][0:64, :]
    glob = a["ytm"][64:128, :]
    ytmks = [("mytm", d2) for d2 in range(NC)]
    k.op("sync", lambda e, src=src, g32=g32: e.dma_start(out=g32, in_=src),
         reads=[dep] if dep else [], writes=[("g32x",), ("aotx",)], dma=True)
    k.op("vector", lambda e, ghib=ghib, g32=g32: e.tensor_copy(out=ghib, in_=g32),
         reads=[("g32x",)], writes=[("ghibx",)] + ytmks)
    k.op("vector", lambda e, glob=glob, g32=g32, ghib=ghib: e.tensor_sub(
        out=glob, in0=g32, in1=ghib),
        reads=[("g32x",), ("ghibx",)], writes=[("globx",)] + ytmks)
    k.op("sync", lambda e, dst=dst, ghib=ghib: e.dma_start(out=dst[0:64, :], in_=ghib),
         reads=[("ghibx",)], writes=[dkey], dma=True)
    k.op("sync", lambda e, dstlo=dstlo, glob=glob: e.dma_start(out=dstlo[0:64, :], in_=glob),
         reads=[("globx",)], writes=[dlokey], dma=True)
    k.op("vector", lambda e: e.memset(a["ytm"][0:64, :], 0.0),
         reads=[dkey, dlokey], writes=[("ytmz",)] + ytmks)
    k.op("sync", lambda e, dst=dst: e.dma_start(out=dst[64:128, :], in_=a["ytm"][0:64, :]),
         reads=[("ytmz",)], writes=[dkey], dma=True)
    k.op("sync", lambda e, dstlo=dstlo: e.dma_start(out=dstlo[64:128, :], in_=a["ytm"][0:64, :]),
         reads=[("ytmz",)], writes=[dlokey], dma=True)


def step_moe(ctx, a, xsrc, xout_writer, wp, lnsc, pp, dram, cid, barrier=None):
    nc, k = ctx.nc, ctx.k
    xln, xlnkeys = a["xln"], a["xlnkeys"]
    iota = a["iota64"]
    ag_in = dram["ag_in"]
    layernorm_fm(ctx, xsrc, xln, xlnkeys, lnsc, lo=a["xlo"], lokeys=a["xlokeys"])

    agin_keys = []
    for tc in range(16):
        for dc in range(NC):
            src = xln[:, dc, tc * 128:(tc + 1) * 128]
            k.op("tensor", lambda e, src=src: e.transpose(
                a["pst"][:, :], src, ctx.identb[:, :]),
                reads=[xlnkeys[dc], ctx.identb_key], writes=[("mpst",)])
            k.op("vector", lambda e, dc=dc: e.tensor_copy(
                out=a["ytm"][:, dc * 128:(dc + 1) * 128], in_=a["pst"][:, :]),
                reads=[("mpst",)], writes=[("mytm", dc)])
        kk2 = (cid, "agin", tc)
        agin_keys.append(kk2)
        k.op("sync", lambda e, tc=tc: e.dma_start(
            out=ag_in[tc * 128:(tc + 1) * 128, :], in_=a["ytm"][:, :]),
            reads=[("mytm", d2) for d2 in range(NC)], writes=[kk2], dma=True)
    k.op("gpsimd", lambda e: e.collective_compute(
        "AllGather", ALU.bypass, replica_groups=[list(range(8))],
        ins=[ag_in], outs=[dram["xall"]]),
        reads=agin_keys, writes=[(cid, "xall")])

    gate_bf, gate_dep = wp["gate_bf"]
    gate_lo_bf, gate_lo_dep = wp["gate_lo_bf"]
    for dc in range(NC):
        k.op("scalar", lambda e, dc=dc: e.dma_start_transpose(
            a["gwst"][:, dc, :], gate_bf[:, dc * 128:(dc + 1) * 128]),
            reads=[gate_dep], writes=[("gwst", dc)], dma=True)
        k.op("scalar", lambda e, dc=dc: e.dma_start_transpose(
            a["gwst_lo"][:, dc, :], gate_lo_bf[:, dc * 128:(dc + 1) * 128]),
            reads=[gate_lo_dep], writes=[("gwstlo", dc)], dma=True)
    gb_ap, gb_dep = wp["gate_bias"]
    k.op("sync", lambda e: e.dma_start(out=a["sfc"][0:1, :], in_=gb_ap),
         reads=[gb_dep] if gb_dep else [], writes=[("sfcx",)], dma=True)
    ps, pkey = pp.next()
    k.op("tensor", lambda e, ps=ps: e.matmul(
        ps[0:128, 0:64], ctx.ones1_128[:, :], a["sfc"][0:1, :], start=True, stop=True),
        reads=[("sfcx",), ctx.ones1_128_key], writes=[pkey])
    k.op("vector", lambda e, ps=ps: e.tensor_copy(out=a["bbc"][:, :], in_=ps[0:128, 0:64]),
         reads=[pkey], writes=[("gbbc",)])
    k.op("sync", lambda e: e.dma_start(out=a["tokid"][:, :], in_=dram["tokid"]),
         writes=[("tokid",)], dma=True)
    k.op("sync", lambda e: e.dma_start(out=a["slabidx"][:, :], in_=dram["slabidx"]),
         writes=[("slabidx",)], dma=True)
    k.op("sync", lambda e: e.dma_start(out=a["cmask"][:, :], in_=dram["cmask"]),
         writes=[("cmask",)], dma=True)

    for tc in range(16):
        ps, pkey = pp.next()
        nterm = 4 * NC
        ti = 0
        for ic in range(NC):
            for (lhs_t, lkeys, wst, wkey) in (
                    (xln, xlnkeys, a["gwst"], ("gwst", ic)),
                    (xln, xlnkeys, a["gwst_lo"], ("gwstlo", ic)),
                    (a["xlo"], a["xlokeys"], a["gwst"], ("gwst", ic)),
                    (a["xlo"], a["xlokeys"], a["gwst_lo"], ("gwstlo", ic))):
                lhs = lhs_t[:, ic, tc * 128:(tc + 1) * 128]
                rr = wst[:, ic, 0:64]
                k.op("tensor", lambda e, lhs=lhs, rr=rr, ps=ps, ti=ti: e.matmul(
                    ps[:, 0:64], lhs, rr, start=(ti == 0), stop=(ti == nterm - 1)),
                    reads=[wkey, lkeys[ic]], writes=[pkey])
                ti += 1
        ov = a["lgt"][:, tc, :]
        k.op("vector", lambda e, ov=ov, ps=ps: e.tensor_copy(out=ov, in_=ps[:, 0:64]),
             reads=[pkey], writes=[(cid, "lgt", tc)])

    for tc in range(16):
        sc_t = a["sc"][:, tc, :]
        k.op("scalar", lambda e, tc=tc, sc_t=sc_t: e.activation(
            out=sc_t, in_=a["lgt"][:, tc, :], func=AF.Exp, scale=-1.0),
            reads=[(cid, "lgt", tc)], writes=[(cid, "sc", tc)])
        k.op("vector", lambda e, sc_t=sc_t: e.tensor_scalar(
            out=sc_t, in0=sc_t, scalar1=1.0, scalar2=None, op0=ALU.add),
            reads=[(cid, "sc", tc)], writes=[(cid, "sc", tc)])
        k.op("vector", lambda e, sc_t=sc_t: e.reciprocal(out=sc_t, in_=sc_t),
            reads=[(cid, "sc", tc)], writes=[(cid, "sc", tc)])
        k.op("vector", lambda e, sc_t=sc_t: e.tensor_add(
            out=a["sfc"][:, :], in0=sc_t, in1=a["bbc"][:, :]),
            reads=[(cid, "sc", tc), ("gbbc",)], writes=[("sfcx",)])
        sfc_g = a["sfc"].rearrange("p (g e) -> p g e", e=8)
        tmp_g = a["tmp"].rearrange("p (g e) -> p g e", e=8)
        k.op("vector", lambda e, sfc_g=sfc_g: e.reduce_max(
            out=a["gs"][:, :], in_=sfc_g, axis=AX.X),
            reads=[("sfcx",)], writes=[("gsx",)])
        gbc = a["gs"][:, :, None].broadcast_to((128, 8, 8))
        k.op("vector", lambda e, sfc_g=sfc_g, gbc=gbc, tmp_g=tmp_g: e.tensor_tensor(
            out=tmp_g, in0=sfc_g, in1=gbc, op=ALU.is_equal),
            reads=[("sfcx",), ("gsx",)], writes=[("tmpx",)])
        k.op("vector", lambda e: e.scalar_tensor_tensor(
            out=a["tmp"][:, :], in0=a["tmp"][:, :], scalar=-1e30, in1=a["sfc"][:, :],
            op0=ALU.mult, op1=ALU.add),
            reads=[("tmpx",), ("sfcx",)], writes=[("tmpx",)])
        k.op("vector", lambda e, tmp_g=tmp_g: e.reduce_max(
            out=a["gs2"][:, :], in_=tmp_g, axis=AX.X),
            reads=[("tmpx",)], writes=[("gs2x",)])
        k.op("vector", lambda e: e.tensor_add(out=a["gs"][:, :], in0=a["gs"][:, :],
                                              in1=a["gs2"][:, :]),
             reads=[("gsx",), ("gs2x",)], writes=[("gsx",)])
        k.op("vector", lambda e: e.max(out=a["gs2"][:, :], in_=a["gs"][:, :]),
             reads=[("gsx",)], writes=[("gs2x",)])
        k.op("vector", lambda e: e.tensor_scalar(
            out=a["gs2"][:, :], in0=a["gs"][:, :], scalar1=a["gs2"][:, 3:4], scalar2=None,
            op0=ALU.is_ge),
            reads=[("gsx",), ("gs2x",)], writes=[("gs2x",)])
        smbc = a["gs2"][:, :, None].broadcast_to((128, 8, 8))
        k.op("vector", lambda e, smbc=smbc, sfc_g=sfc_g, tmp_g=tmp_g: e.tensor_tensor(
            out=tmp_g, in0=sfc_g, in1=smbc, op=ALU.mult),
            reads=[("gs2x",), ("sfcx",)], writes=[("tmpx",)])
        k.op("vector", lambda e, smbc=smbc: e.tensor_scalar(
            out=a["tmp2"][:, 0:8], in0=a["gs2"][:, :], scalar1=1e4, scalar2=-1e4,
            op0=ALU.mult, op1=ALU.add),
            reads=[("gs2x",)], writes=[("tmp2x",)])
        t2bc = a["tmp2"][:, 0:8][:, :, None].broadcast_to((128, 8, 8))
        k.op("vector", lambda e, t2bc=t2bc, tmp_g=tmp_g: e.tensor_tensor(
            out=tmp_g, in0=tmp_g, in1=t2bc, op=ALU.add),
            reads=[("tmpx",), ("tmp2x",)], writes=[("tmpx",)])
        k.op("vector", lambda e: e.max(out=a["top8"][:, :], in_=a["tmp"][:, :]),
             reads=[("tmpx",)], writes=[("top8x",)])
        k.op("vector", lambda e: e.max_index(out=a["idxu"][:, :], in_max=a["top8"][:, :],
                                             in_values=a["tmp"][:, :]),
             reads=[("tmpx",), ("top8x",)], writes=[("idxux",)])
        ov = a["idx8"][:, tc, :]
        k.op("vector", lambda e, ov=ov: e.tensor_copy(out=ov, in_=a["idxu"][:, :]),
             reads=[("idxux",)], writes=[(cid, "idx8", tc)])
        k.op("vector", lambda e: e.tensor_scalar(
            out=a["iu1"][:, :], in0=a["idxu"][:, :], scalar1=7, scalar2=None,
            op0=ALU.bitwise_and),
            reads=[("idxux",)], writes=[("iu1x",)])
        k.op("vector", lambda e: e.tensor_scalar(
            out=a["iu2"][:, :], in0=a["idxu"][:, :], scalar1=3, scalar2=None,
            op0=ALU.arith_shift_right),
            reads=[("idxux",)], writes=[("iu2x",)])
        emt = a["emaj"][:, tc, :]
        k.op("vector", lambda e, emt=emt: e.tensor_scalar(
            out=emt, in0=a["iu1"][:, :], scalar1=float(8 * CAP), scalar2=None,
            op0=ALU.mult),
            reads=[("iu1x",)], writes=[(cid, "emaj", tc)])
        k.op("vector", lambda e, emt=emt: e.scalar_tensor_tensor(
            out=emt, in0=a["iu2"][:, :], scalar=float(CAP), in1=emt,
            op0=ALU.mult, op1=ALU.add),
            reads=[("iu2x",), (cid, "emaj", tc)], writes=[(cid, "emaj", tc)])
        k.op("vector", lambda e, tc=tc: e.memset(a["wa"][:, tc, :], 0.0),
             writes=[(cid, "wa", tc)])
        for kk_ in range(K_TOP):
            idc = a["idx8"][:, tc, kk_:kk_ + 1]
            k.op("vector", lambda e, idc=idc, sc_t=sc_t: e.scalar_tensor_tensor(
                out=a["tmp2"][:, :], in0=iota[:, :], scalar=idc, in1=sc_t,
                op0=ALU.is_equal, op1=ALU.mult),
                reads=[("iota64",), (cid, "idx8", tc), (cid, "sc", tc)],
                writes=[("tmp2x",)])
            ov = a["srow8"][:, kk_:kk_ + 1]
            k.op("vector", lambda e, ov=ov: e.reduce_sum(
                out=ov, in_=a["tmp2"][:, :], axis=AX.X),
                reads=[("tmp2x",)], writes=[("srow8x",)])
            k.op("vector", lambda e, idc=idc: e.tensor_scalar(
                out=a["tmp2"][:, :], in0=iota[:, :], scalar1=idc, scalar2=None,
                op0=ALU.is_equal),
                reads=[("iota64",), (cid, "idx8", tc)], writes=[("tmp2x",)])
            k.op("vector", lambda e, tc=tc: e.tensor_add(
                out=a["wa"][:, tc, :], in0=a["wa"][:, tc, :], in1=a["tmp2"][:, :]),
                reads=[(cid, "wa", tc), ("tmp2x",)], writes=[(cid, "wa", tc)])
        k.op("vector", lambda e: e.reduce_sum(out=a["wsum"][:, :], in_=a["srow8"][:, :],
                                              axis=AX.X),
             reads=[("srow8x",)], writes=[("wsumx",)])
        k.op("vector", lambda e: e.tensor_scalar(
            out=a["wsum"][:, :], in0=a["wsum"][:, :], scalar1=1e-20, scalar2=None,
            op0=ALU.add),
            reads=[("wsumx",)], writes=[("wsumx",)])
        k.op("vector", lambda e: e.reciprocal(out=a["wsum"][:, :], in_=a["wsum"][:, :]),
             reads=[("wsumx",)], writes=[("wsumx",)])
        ov = a["wn"][:, tc, :]
        k.op("vector", lambda e, ov=ov: e.tensor_scalar(
            out=ov, in0=a["srow8"][:, :], scalar1=a["wsum"][:, 0:1], scalar2=MOE_SCALE,
            op0=ALU.mult, op1=ALU.mult),
            reads=[("srow8x",), ("wsumx",)], writes=[(cid, "wn", tc)])

    for tc in range(16):
        src = a["wa"][:, tc, :]
        k.op("tensor", lambda e, src=src: e.transpose(
            a["pstf"][0:64, :], src, ctx.identf[:, :]),
            reads=[(cid, "wa", tc), ctx.identf_key], writes=[("mpstf",)])
        ov = a["aot"][:, tc * 128:(tc + 1) * 128]
        k.op("vector", lambda e, ov=ov: e.tensor_copy(out=ov, in_=a["pstf"][0:64, :]),
             reads=[("mpstf",)], writes=[("aotx",)])
    k.op("vector", lambda e: e.reduce_sum(out=a["cnt"][:, :], in_=a["aot"][:, :],
                                          axis=AX.X),
         reads=[("aotx",)], writes=[("cntx",)])
    st1, k1 = a["stage"].next()
    k.op("vector", lambda e, st1=st1: e.tensor_copy(out=st1[0:64, 0:1], in_=a["cnt"][:, :]),
         reads=[("cntx",)], writes=[k1])
    k.op("sync", lambda e, st1=st1: e.dma_start(
        out=dram["cnt_loc"].rearrange("a b -> b a"), in_=st1[0:64, 0:1]),
        reads=[k1], writes=[(cid, "cntloc")], dma=True)
    k.op("gpsimd", lambda e: e.collective_compute(
        "AllGather", ALU.bypass, replica_groups=[list(range(8))],
        ins=[dram["cnt_loc"]], outs=[dram["cnt_all"]]),
        reads=[(cid, "cntloc")], writes=[(cid, "cntall")])
    k.op("sync", lambda e: e.dma_start(out=a["call"][:, :], in_=dram["cnt_all"]),
         reads=[(cid, "cntall")], writes=[("callx",)], dma=True)
    ps, pkey = pp.next()
    k.op("tensor", lambda e, ps=ps: e.matmul(
        ps[0:64, 0:1], a["call"][:, :], a["cmask"][:, :], start=True, stop=True),
        reads=[("callx",), ("cmask",)], writes=[pkey])
    k.op("vector", lambda e, ps=ps: e.tensor_copy(out=a["offc"][:, :], in_=ps[0:64, 0:1]),
         reads=[pkey], writes=[("offcx",)])

    aot2 = a["aot2"]
    k.op("vector", lambda e: e.memset(aot2[:, :], 0.0), writes=[("aot2x",)])
    k.op("vector", lambda e: e.tensor_tensor_scan(
        out=aot2[:, :], data0=aot2[:, :], data1=a["aot"][:, :],
        initial=a["offc"][:, 0:1], op0=ALU.add, op1=ALU.add),
        reads=[("aotx",), ("aot2x",), ("offcx",)], writes=[("aot2x",)])
    k.op("vector", lambda e: e.tensor_sub(out=aot2[:, :], in0=aot2[:, :], in1=a["aot"][:, :]),
         reads=[("aot2x",), ("aotx",)], writes=[("aot2x",)])
    for tc in range(16):
        src = aot2[:, tc * 128:(tc + 1) * 128]
        k.op("tensor", lambda e, src=src: e.transpose(
            a["pstf"][:, 0:64], src, ctx.identf[0:64, 0:64]),
            reads=[("aot2x",), ctx.identf_key], writes=[("mpstf",)])
        ov = a["ct"][:, tc, :]
        k.op("vector", lambda e, ov=ov: e.tensor_copy(out=ov, in_=a["pstf"][:, 0:64]),
             reads=[("mpstf",)], writes=[(cid, "ct", tc)])

    c_neg1 = ctx.const.dram("Lneg1", np.full((128, LROWS), -1.0, np.float32))
    k.op("sync", lambda e: e.dma_start(out=dram["Lloc"], in_=c_neg1),
         writes=[(cid, "Lloc")], dma=True)
    Lflat = dram["Lloc"].rearrange("a b -> (a b) ()")
    scat_keys = []
    for tc in range(16):
        for kk_ in range(K_TOP):
            idc = a["idx8"][:, tc, kk_:kk_ + 1]
            k.op("vector", lambda e, idc=idc, tc=tc: e.scalar_tensor_tensor(
                out=a["tmp2"][:, :], in0=iota[:, :], scalar=idc, in1=a["ct"][:, tc, :],
                op0=ALU.is_equal, op1=ALU.mult),
                reads=[("iota64",), (cid, "idx8", tc), (cid, "ct", tc)],
                writes=[("tmp2x",)])
            k.op("vector", lambda e: e.reduce_sum(out=a["wsum"][:, :], in_=a["tmp2"][:, :],
                                                  axis=AX.X),
                 reads=[("tmp2x",)], writes=[("wsumx",)])
            k.op("vector", lambda e: e.tensor_scalar(
                out=a["tmp2"][:, 0:1], in0=a["wsum"][:, :], scalar1=float(CAP),
                scalar2=None, op0=ALU.is_ge),
                reads=[("wsumx",)], writes=[("tmp2x",)])
            k.op("vector", lambda e: e.tensor_scalar(
                out=a["tmp2"][:, 1:2], in0=a["tmp2"][:, 0:1], scalar1=-1.0, scalar2=1.0,
                op0=ALU.mult, op1=ALU.add),
                reads=[("tmp2x",)], writes=[("tmp2x",)])
            wv = a["wn"][:, tc, kk_:kk_ + 1]
            k.op("vector", lambda e, wv=wv: e.tensor_mul(
                out=wv, in0=wv, in1=a["tmp2"][:, 1:2]),
                reads=[(cid, "wn", tc), ("tmp2x",)], writes=[(cid, "wn", tc)])
            lf = a["lflat"][:, tc, kk_:kk_ + 1]
            k.op("vector", lambda e, lf=lf, idc=idc: e.scalar_tensor_tensor(
                out=lf, in0=idc, scalar=float(CAP), in1=a["wsum"][:, :],
                op0=ALU.mult, op1=ALU.add),
                reads=[(cid, "idx8", tc), ("wsumx",)], writes=[(cid, "lflat", tc)])
            k.op("vector", lambda e, lf=lf: e.tensor_mul(
                out=a["tmp2"][:, 3:4], in0=lf, in1=a["tmp2"][:, 1:2]),
                reads=[(cid, "lflat", tc), ("tmp2x",)], writes=[("tmp2x",)])
            k.op("vector", lambda e, tc=tc, kk_=kk_: e.scalar_tensor_tensor(
                out=a["tmp2"][:, 4:5], in0=a["tmp2"][:, 0:1],
                scalar=float(JUNK + tc * K_TOP + kk_),
                in1=a["tmp2"][:, 3:4], op0=ALU.mult, op1=ALU.add),
                reads=[("tmp2x",)], writes=[("tmp2x",)])
            dd = a["lidx"][:, tc, kk_:kk_ + 1]
            k.op("vector", lambda e, dd=dd: e.tensor_copy(out=dd, in_=a["tmp2"][:, 4:5]),
                 reads=[("tmp2x",)], writes=[(cid, "lidx", tc, kk_)])
            k.op("vector", lambda e, tc=tc, kk_=kk_: e.tensor_add(
                out=a["tmp2"][:, 5:6], in0=a["emaj"][:, tc, kk_:kk_ + 1],
                in1=a["wsum"][:, :]),
                reads=[(cid, "emaj", tc), ("wsumx",), ("tmp2x",)], writes=[("tmp2x",)])
            oo = a["oidx"][:, tc, kk_:kk_ + 1]
            k.op("vector", lambda e, oo=oo: e.tensor_mul(
                out=oo, in0=a["tmp2"][:, 5:6], in1=a["tmp2"][:, 1:2]),
                reads=[("tmp2x",)], writes=[(cid, "oidx", tc)])
    for tc in range(16):
        tks = a["tokid"][:, tc:tc + 1]
        for kk_ in range(K_TOP):
            dd = a["lidx"][:, tc, kk_:kk_ + 1]
            sk = (cid, "scat", tc, kk_)
            scat_keys.append(sk)
            k.op("gpsimd", lambda e, dd=dd, tks=tks: e.indirect_dma_start(
                Lflat, IndirectOffsetOnAxis(dd, 0), tks, None),
                reads=[(cid, "lidx", tc, kk_), ("tokid",), (cid, "Lloc")],
                writes=[sk], dma=True)
    k.op("gpsimd", lambda e: e.collective_compute(
        "AllGather", ALU.bypass, replica_groups=[list(range(8))],
        ins=[dram["Lloc"]], outs=[dram["Lall"]]),
        reads=scat_keys + [(cid, "Lloc")], writes=[(cid, "Lall")])
    k.op("sync", lambda e: e.dma_start(
        out=a["Lmrg"][:, :], in_=dram["Lall"][0:128, :]),
        reads=[(cid, "Lall")], writes=[("Lmrg",)], dma=True)
    for s in range(1, 8):
        buf = a["Lin"][:, s % 2, :]
        k.op("sync", lambda e, buf=buf, s=s: e.dma_start(
            out=buf, in_=dram["Lall"][s * 128:(s + 1) * 128, :]),
            reads=[(cid, "Lall")], writes=[("Lin", s % 2)], dma=True)
        k.op("vector", lambda e, buf=buf: e.tensor_max(
            out=a["Lmrg"][:, :], in0=a["Lmrg"][:, :], in1=buf),
            reads=[("Lmrg",), ("Lin", s % 2)], writes=[("Lmrg",)])
    k.op("sync", lambda e: e.dma_start(out=dram["Lfin"], in_=a["Lmrg"][:, :]),
         reads=[("Lmrg",)], writes=[(cid, "Lfin")], dma=True)

    if barrier is not None:
        barrier()

    Lfin_flat = dram["Lfin"].rearrange("a b -> (a b) ()")
    for j in range(64):
        sic = a["slabidx"][:, j:j + 1]
        ov = a["gidx"][:, j:j + 1]
        k.op("gpsimd", lambda e, sic=sic, ov=ov: e.indirect_dma_start(
            ov, None, Lfin_flat, IndirectOffsetOnAxis(sic, 0)),
            reads=[("slabidx",), (cid, "Lfin")], writes=[("gidx", j)], dma=True)
    k.op("vector", lambda e: e.tensor_relu(out=a["gidx"][:, :], in_=a["gidx"][:, :]),
         reads=[("gidx", j2) for j2 in range(64)], writes=[("gidxF",)])
    k.op("vector", lambda e: e.tensor_copy(out=a["gidxi"][:, :], in_=a["gidx"][:, :]),
         reads=[("gidxF",)], writes=[("gidxF",)])

    wg_bf, wg_key = wp["wg_bf"]
    wu_bf, wu_key = wp["wu_bf"]
    wd_bf, wd_key = wp["wd_bf"]
    yslab_keys = []
    for el in range(EPG):
        btm = a["buf_tm"][0]
        for j in range(8):
            gc = a["gidxi"][:, el * 8 + j:el * 8 + j + 1]
            ov = btm[:, j, :]
            k.op("gpsimd", lambda e, gc=gc, ov=ov: e.indirect_dma_start(
                ov, None, dram["xall"], IndirectOffsetOnAxis(gc, 0)),
                reads=[("gidxF",), (cid, "xall")], writes=[("btm", j)], dma=True)
        for j in range(8):
            for dc in range(NC):
                src = btm[:, j, dc * 128:(dc + 1) * 128]
                k.op("tensor", lambda e, src=src: e.transpose(
                    a["pst"][:, :], src, ctx.identb[:, :]),
                    reads=[("btm", j), ctx.identb_key], writes=[("mpst",)])
                ov = a["buf_fm"][:, dc, j * 128:(j + 1) * 128]
                if (j + dc) % 2 == 0:
                    k.op("vector", lambda e, ov=ov: e.tensor_copy(out=ov, in_=a["pst"][:, :]),
                         reads=[("mpst",)], writes=[("bfm", dc, j)])
                else:
                    k.op("scalar", lambda e, ov=ov: e.copy(out=ov, in_=a["pst"][:, :]),
                         reads=[("mpst",)], writes=[("bfm", dc, j)])
        bfm_all = [("bfm", dc, j) for dc in range(NC) for j in range(8)]
        for oi in range(4):
            for dc in range(NC):
                k.op("sync", lambda e, el=el, oi=oi, dc=dc: e.dma_start(
                    out=a["wg_st"][:, dc, :],
                    in_=wg_bf[el, dc * 128:(dc + 1) * 128, oi * 128:(oi + 1) * 128]),
                    reads=[wg_key], writes=[("wgst", dc)], dma=True)
                k.op("sync", lambda e, el=el, oi=oi, dc=dc: e.dma_start(
                    out=a["wu_st"][:, dc, :],
                    in_=wu_bf[el, dc * 128:(dc + 1) * 128, oi * 128:(oi + 1) * 128]),
                    reads=[wu_key], writes=[("wust", dc)], dma=True)
            for st in range(2):
                sl = slice(st * 512, (st + 1) * 512)
                for dc in range(NC):
                    rr = a["buf_fm"][:, dc, sl]
                    k.op("tensor", lambda e, dc=dc, rr=rr: e.matmul(
                        a["psG"][:, :], a["wg_st"][:, dc, :], rr,
                        start=(dc == 0), stop=(dc == NC - 1)),
                        reads=[("wgst", dc), ("bfm", dc, 4 * st), ("bfm", dc, 4 * st + 1),
                               ("bfm", dc, 4 * st + 2), ("bfm", dc, 4 * st + 3)],
                        writes=[("psGx",)])
                k.op("scalar", lambda e: e.activation(
                    out=a["sg"][:, :], in_=a["psG"][:, :], func=AF.Silu),
                    reads=[("psGx",)], writes=[("sgx",)])
                for dc in range(NC):
                    rr = a["buf_fm"][:, dc, sl]
                    k.op("tensor", lambda e, dc=dc, rr=rr: e.matmul(
                        a["psU"][:, :], a["wu_st"][:, dc, :], rr,
                        start=(dc == 0), stop=(dc == NC - 1)),
                        reads=[("wust", dc), ("bfm", dc, 4 * st), ("bfm", dc, 4 * st + 1),
                               ("bfm", dc, 4 * st + 2), ("bfm", dc, 4 * st + 3)],
                        writes=[("psUx",)])
                ov = a["h_fm"][:, oi, sl]
                k.op("vector", lambda e, ov=ov: e.tensor_mul(
                    out=ov, in0=a["psU"][:, :], in1=a["sg"][:, :]),
                    reads=[("psUx",), ("sgx",)], writes=[("hfm", oi, st)])
        for dh in range(2):
            for ic in range(4):
                k.op("sync", lambda e, el=el, dh=dh, ic=ic: e.dma_start(
                    out=a["wd_st"][:, ic, :],
                    in_=wd_bf[el, ic * 128:(ic + 1) * 128, dh * 512:(dh + 1) * 512]),
                    reads=[wd_key], writes=[("wdst", ic)], dma=True)
            for pc in range(8):
                for ic in range(4):
                    lhs = a["h_fm"][:, ic, pc * 128:(pc + 1) * 128]
                    k.op("tensor", lambda e, lhs=lhs, ic=ic: e.matmul(
                        a["psY"][:, :], lhs, a["wd_st"][:, ic, :],
                        start=(ic == 0), stop=(ic == 3)),
                        reads=[("wdst", ic), ("hfm", ic, pc // 4)], writes=[("psYx",)])
                ov = a["ytm"][:, dh * 512:(dh + 1) * 512]
                k.op("vector", lambda e, ov=ov: e.tensor_copy(out=ov, in_=a["psY"][:, :]),
                     reads=[("psYx",)], writes=[("ytm2", dh)])
                yk = (cid, "yslab", el, pc, dh)
                yslab_keys.append(yk)
                k.op("sync", lambda e, el=el, pc=pc, dh=dh: e.dma_start(
                    out=dram["y_slab"][el * 1024 + pc * 128:el * 1024 + (pc + 1) * 128,
                                       dh * 512:(dh + 1) * 512],
                    in_=a["ytm"][:, dh * 512:(dh + 1) * 512]),
                    reads=[("ytm2", dh)], writes=[yk], dma=True)
        el_keys = [kk3 for kk3 in yslab_keys if kk3[2] == el]
        y_all_v = dram["y_all"][el * 8 * CAP:(el + 1) * 8 * CAP, :]
        k.op("gpsimd", lambda e, el=el, y_all_v=y_all_v: e.collective_compute(
            "AllGather", ALU.bypass, replica_groups=[list(range(8))],
            ins=[dram["y_slab"][el * CAP:(el + 1) * CAP, :]], outs=[y_all_v]),
            reads=el_keys, writes=[(cid, "yall", el)])
    yall_keys = [(cid, "yall", el2) for el2 in range(EPG)]

    if barrier is not None:
        barrier()

    for tc in range(16):
        k.op("vector", lambda e: e.memset(a["acc"][:, :], 0.0), writes=[("maccx",)])
        for kk_ in range(K_TOP):
            yd = a["ydst"][kk_ % 3]
            ydk = ("yd", kk_ % 3)
            oo = a["oidx"][:, tc, kk_:kk_ + 1]
            k.op("gpsimd", lambda e, yd=yd, oo=oo: e.indirect_dma_start(
                yd[:, :], None, dram["y_all"], IndirectOffsetOnAxis(oo, 0)),
                reads=[(cid, "oidx", tc)] + yall_keys, writes=[ydk], dma=True)
            wv = a["wn"][:, tc, kk_:kk_ + 1]
            eng = "vector"
            k.op(eng, lambda e, yd=yd, wv=wv: e.scalar_tensor_tensor(
                out=a["acc"][:, :], in0=yd[:, :], scalar=wv, in1=a["acc"][:, :],
                op0=ALU.mult, op1=ALU.add),
                reads=[ydk, (cid, "wn", tc), ("maccx",)], writes=[("maccx",)])
        for dc in range(NC):
            src = a["acc"][:, dc * 128:(dc + 1) * 128]
            k.op("tensor", lambda e, src=src: e.transpose(
                a["pstf"][:, :], src, ctx.identf[:, :]),
                reads=[("maccx",), ctx.identf_key], writes=[("mpstf",)])
            xout_writer(dc, tc, a["pstf"], ("mpstf",))

"""Fused single-launch megakernel: all 8 sublayers in ONE Bass program.

x ping-pongs between DRAM buffers xA/xB (streamed via a shared XDram slot
pool); per-phase SBUF scratch is hand-placed in an overlapping arena
(alloc_sbuf_tensor_at), with K.barrier() fencing phase transitions. PSUM is
scoped per phase via `with` guards. MoE's routing/expert/consumer scratch
groups alias each other inside the moe phases (step_moe's internal barriers).
"""
import contextlib
import numpy as np
import concourse.bass as bass
import concourse.mybir as mybir
from concourse.bass_utils import run_bass_kernel_spmd

LAT = 256
NCORE = 8

_DT_SIZE = {F32: 4, BF16: 2, I32: 4, I16: 2}


class Arena:

    def __init__(self, nc, reserve_tail=1024):
        size = nc.sbuf_bytes_remaining - reserve_tail
        b, e = nc.bump_sbuf(size)
        self.nc, self.base, self.end = nc, b, e
        self.cur = b
        self.hi = b

    def mark(self):
        return self.cur

    def reset(self, m):
        self.cur = m

    def alloc(self, name, shape, dtype):
        per = 1
        for s in shape[1:]:
            per *= s
        per *= _DT_SIZE.get(dtype, 4)
        off = (self.cur + 31) // 32 * 32
        assert off + per <= self.end, (
            f"arena overflow: {name} needs {per}B at {off}, end {self.end}")
        t = self.nc.alloc_sbuf_tensor_at(name, shape, dtype, offset=off)
        self.cur = off + per
        self.hi = max(self.hi, self.cur)
        return t


def mk_tabs(ctx, base):
    half = 32
    theta = (1.0 / base) ** (np.arange(half, dtype=np.float64) / half)
    ang = np.arange(T, dtype=np.float64)[:, None] * theta[None, :]
    cos = np.tile(np.cos(ang).T, (4, 1))
    sin_ = np.sin(ang).T
    sinp = np.concatenate([-sin_, sin_, -sin_, sin_], axis=0)
    cd = ctx.const.dram(f"rc{int(base)}", cos.astype(BF))
    sd = ctx.const.dram(f"rs{int(base)}", sinp.astype(BF))
    return cd, sd


def build_full():
    nc = bass.Bass("TRN2", target_bir_lowering=False, debug=False)
    k = K(nc)
    ctx = Ctx(nc, k)

    xT = nc.dram_tensor("xT", [D, T], F32, kind="ExternalInput").ap()
    yT = nc.dram_tensor("yT", [D, T], F32, kind="ExternalOutput").ap()
    E_ = {}

    def ein(name, shape, dt=F32):
        E_[name] = nc.dram_tensor(name, shape, dt, kind="ExternalInput").ap()
        return E_[name]

    for p in ("a0", "b0"):
        for nm in ("wq", "wk", "wv", "wo"):
            ein(f"{p}_{nm}", [D, D])
        for nm in ("bq", "bk", "bo"):
            ein(f"{p}_{nm}", [128, NC])
        ein(f"{p}_bv", [1, D])
    ein("b0_gb", [128, 2 * NC])
    ein("ffn_wg_i", [D, HID], I32)
    ein("ffn_wu_i", [D, HID], I32)
    ein("ffn_wd_i", [HID, D], I32)
    ein("a1_w2", [2 * D, D])
    ein("a1_w1", [1, D])
    ein("a1_b1", [128, NC])
    B1SH = {"g_wq": (D, D), "g_kv_down": (D, LAT), "g_k_up": (LAT, LAT),
            "g_v_up": (LAT, LAT), "g_wo": (D, D), "l_wq": (D, D),
            "l_wk": (D, LAT), "l_wv": (D, LAT), "l_wo": (D, D)}
    for nm, sh in B1SH.items():
        ein("b1_" + nm, list(sh))
    ein("b1_n1w", [128, NC])
    ein("b1_n2w", [128, NC])
    ein("moe_gate_w", [64, D])
    ein("moe_gate_b", [1, 64])
    ein("moe_wg_i", [EPG, D, HID_I])
    ein("moe_wu_i", [EPG, D, HID_I])
    ein("moe_wd_i", [EPG, HID_I, D])
    ein("moe_tokid", [128, 16])
    ein("moe_slab", [128, 64], I32)
    ein("moe_cmask", [8, 1])

    xA = nc.dram_tensor("xA", [D, T], F32, kind="Internal").ap()
    xB = nc.dram_tensor("xB", [D, T], F32, kind="Internal").ap()
    bf = {}

    def mkbf(name, shape):
        bf[name] = nc.dram_tensor(name + "_bf", shape, BF16, kind="Internal").ap()
        return bf[name]

    for p in ("a0", "b0"):
        for nm in ("wq", "wk", "wv", "wo"):
            mkbf(f"{p}_{nm}", [D, D])
    mkbf("ffn_wg", [D, HID])
    mkbf("ffn_wu", [D, HID])
    mkbf("ffn_wd", [HID, D])
    mkbf("a1_w2", [2 * D, D])
    for nm, sh in B1SH.items():
        mkbf("b1_" + nm, list(sh))
    mkbf("moe_gate", [128, D])
    mkbf("moe_gate_lo", [128, D])
    mkbf("moe_wg", [EPG, D, HID_I])
    mkbf("moe_wu", [EPG, D, HID_I])
    mkbf("moe_wd", [EPG, HID_I, D])
    mdram = {}
    for inst in ("m0", "m1"):
        mdram[inst] = {
            "ag_in": nc.dram_tensor(f"{inst}_agin", [T, D], BF16, kind="Internal").ap(),
            "xall": nc.dram_tensor(f"{inst}_xall", [NCORE * T, D], BF16, kind="Internal",
                                   addr_space="Shared").ap(),
            "cnt_loc": nc.dram_tensor(f"{inst}_cntl", [1, 64], F32, kind="Internal").ap(),
            "cnt_all": nc.dram_tensor(f"{inst}_cnta", [8, 64], F32, kind="Internal",
                                      addr_space="Shared").ap(),
            "Lloc": nc.dram_tensor(f"{inst}_Lloc", [128, LROWS], F32, kind="Internal").ap(),
            "Lall": nc.dram_tensor(f"{inst}_Lall", [8 * 128, LROWS], F32, kind="Internal",
                                   addr_space="Shared").ap(),
            "Lfin": nc.dram_tensor(f"{inst}_Lfin", [128, LROWS], F32, kind="Internal").ap(),
            "y_slab": nc.dram_tensor(f"{inst}_yslab", [EPG * CAP, D], BF16, kind="Internal").ap(),
            "y_all": nc.dram_tensor(f"{inst}_yall", [64 * CAP, D], BF16, kind="Internal",
                                    addr_space="Shared").ap(),
            "tokid": E_["moe_tokid"], "slabidx": E_["moe_slab"],
            "cmask": E_["moe_cmask"],
        }

    with nc.Block() as block:
        setup_common(ctx)
        murow = nc.alloc_sbuf_tensor("murow", [1, T], F32)
        varrow = nc.alloc_sbuf_tensor("varrow", [1, T], F32)
        sq = nc.alloc_sbuf_tensor("lnsq", [128, TT], F32)
        mubc = nc.alloc_sbuf_tensor("mubc", [128, T], BF16)
        rbc = nc.alloc_sbuf_tensor("rbc", [128, T], F32)
        srow = nc.alloc_sbuf_tensor("gsrow", [1, T], F32)
        xln = nc.alloc_sbuf_tensor("xln", [128, NC, T], BF16)
        xlnkeys = [("xln", ci) for ci in range(NC)]
        pp = PsumPool(ctx, 2, [128, TT], F32, "gpp")
        stage = SbufPool(ctx, 2, [128, TT], F32, "gstage")
        xdpool = SbufPool(ctx, 2, [128, TT], F32, "gxd")
        lnsc = {"murow": (murow, "murow"), "varrow": (varrow, "varrow"),
                "sq": (sq, "lnsq"), "mubc": (mubc, "mubc"), "rbc": (rbc, "rbc"),
                "pp": pp}
        bias1 = nc.alloc_sbuf_tensor("gbias1", [128, NC], F32)
        bias2 = nc.alloc_sbuf_tensor("gbias2", [128, NC], F32)
        bias3 = nc.alloc_sbuf_tensor("gbias3", [128, NC], F32)
        nrm = nc.alloc_sbuf_tensor("gnrm", [1, 2 * NH, 4], F32)
        mrow = nc.alloc_sbuf_tensor("gmrow", [1, NH], F32)
        mcols = nc.alloc_sbuf_tensor("gmcols", [128, NH], F32)
        gbt = nc.alloc_sbuf_tensor("ggb", [128, 2 * NC], F32)
        triu_t, triu_k = load_const_tile(
            ctx, np.triu(np.ones((128, 128), dtype=BF)), "triu", BF16)
        wm = np.zeros((128, 384), dtype=BF)
        for kl in range(128):
            wm[kl, kl:min(kl + WIN, 384)] = 1
        wmask_t, wmask_k = load_const_tile(ctx, wm, "wmask", BF16)
        P128 = perm128_const(ctx)
        for v in (EPS_LN, 1.0, 1e-6, 1e-12):
            ctx.sconst(v)
        mb = {}
        mb["wn"] = nc.alloc_sbuf_tensor("moe_wn", [128, 16, 8], F32)
        mb["oidx"] = nc.alloc_sbuf_tensor("moe_oidx", [128, 16, 8], I32)
        mb["lidx"] = nc.alloc_sbuf_tensor("moe_lidx", [128, 16, 8], I32)
        mb["lflat"] = nc.alloc_sbuf_tensor("moe_lflat", [128, 16, 8], F32)
        mb["tokid"] = nc.alloc_sbuf_tensor("ms_tokid", [128, 16], F32)
        mb["slabidx"] = nc.alloc_sbuf_tensor("ms_slabidx", [128, 64], I32)
        mb["gidx"] = nc.alloc_sbuf_tensor("ms_gidx", [128, 64], F32)
        mb["gidxi"] = nc.alloc_sbuf_tensor("ms_gidxi", [128, 64], I32)
        mb["iota64"] = nc.alloc_sbuf_tensor("ms_iota64", [128, 64], F32)
        mb["ytm"] = nc.alloc_sbuf_tensor("moe_ytm", [128, 1024], BF16)
        mb["srow8"] = nc.alloc_sbuf_tensor("moe_srow8", [128, 8], F32)
        mb["emaj"] = nc.alloc_sbuf_tensor("moe_emaj", [128, 16, 8], F32)
        mb["iu1"] = nc.alloc_sbuf_tensor("moe_iu1", [128, 8], mybir.dt.uint32)
        mb["iu2"] = nc.alloc_sbuf_tensor("moe_iu2", [128, 8], mybir.dt.uint32)
        mb["gs"] = nc.alloc_sbuf_tensor("moe_gs", [128, 8], F32)
        mb["gs2"] = nc.alloc_sbuf_tensor("moe_gs2", [128, 8], F32)
        mb["top8"] = nc.alloc_sbuf_tensor("moe_top8", [128, 8], F32)
        mb["idxu"] = nc.alloc_sbuf_tensor("moe_idxu", [128, 8], mybir.dt.uint32)
        mb["wsum"] = nc.alloc_sbuf_tensor("moe_wsum", [128, 1], F32)
        mb["cnt"] = nc.alloc_sbuf_tensor("moe_cnt", [64, 1], F32)
        mb["call"] = nc.alloc_sbuf_tensor("moe_call", [8, 64], F32)
        mb["cmask"] = nc.alloc_sbuf_tensor("ms_cmask", [8, 1], F32)
        mb["offc"] = nc.alloc_sbuf_tensor("moe_offc", [64, 1], F32)

        ar = Arena(nc)
        m0_ = ar.mark()
        rct = ar.alloc("grct", [128, T], BF16)
        rst = ar.alloc("grst", [128, T], BF16)
        m1_ = ar.mark()
        MH = {}
        MH["q"] = ar.alloc("mha_q", [128, NC, T], BF16)
        MH["kk"] = ar.alloc("mha_k", [128, NC, T], BF16)
        MH["v2"] = ar.alloc("mha_v2", [128, 16, 65 * 8], BF16)
        MH["E"] = ar.alloc("mha_E", [128, T], BF16)
        MH["wst"] = ar.alloc("mha_wst", [128, NC, 128], BF16)
        MH["wstv"] = ar.alloc("mha_wstv", [128, NC, 512], BF16)
        MH["ropeA"] = ar.alloc("mha_ropeA", [128, 1, TT], F32)
        MH["ropeB"] = ar.alloc("mha_ropeB", [128, 1, TT], F32)
        MH["brow"] = ar.alloc("mha_brow", [1, D], BF16)
        MH["bbc"] = ar.alloc("mha_bbc", [128, D], BF16)
        ar.reset(m0_)
        FF = {}
        FF["h"] = ar.alloc("ffn_h", [128, NHC, HALF], BF16)
        FF["wg"] = ar.alloc("ffn_wg", [128, 2, NC, 128], BF16)
        FF["wu"] = ar.alloc("ffn_wu", [128, 2, NC, 128], BF16)
        FF["wd"] = ar.alloc("ffn_wd", [128, 2, NHC, 128], BF16)
        FF["sg"] = ar.alloc("ffn_sg", [128, 2, FT], F32)
        ar.reset(m0_)
        A1 = {}
        A1["q"] = ar.alloc("a1_nrm", [128, NC, T], BF16)
        A1["kk"] = ar.alloc("a1_cnt", [128, NC, T], BF16)
        A1["E"] = ar.alloc("a1_E", [128, T], BF16)
        A1["c1"] = ar.alloc("a1_c1", [128, T], BF16)
        A1["wst"] = ar.alloc("a1_wst", [128, NC, 128], BF16)
        A1["wstv"] = ar.alloc("a1_wstv", [128, NC, 128], BF16)
        A1["brow"] = ar.alloc("a1_brow", [1, D], F32)
        ar.reset(m1_)
        B1 = {}
        B1["xi"] = ar.alloc("b1_xi", [128, NC, T], BF16)
        B1["xr"] = ar.alloc("b1_xr", [128, NC, T], BF16)
        B1["kk"] = ar.alloc("b1_k", [128, 2, T], BF16)
        B1["lat"] = ar.alloc("b1_lat", [128, 2, T], BF16)
        B1["v2g"] = ar.alloc("b1_v2", [128, 16, 65 * KVH], BF16)
        B1["E"] = ar.alloc("b1_E", [128, T], BF16)
        B1["c1"] = ar.alloc("b1_c1", [128, T], BF16)
        B1["c2"] = ar.alloc("b1_c2", [128, T], BF16)
        B1["kstage"] = ar.alloc("b1_kst", [128, T], BF16)
        B1["wst"] = ar.alloc("b1_wst", [128, NC, 128], BF16)
        B1["wstv"] = ar.alloc("b1_wstv", [128, NC, 256], BF16)
        ar.reset(m0_)
        MR = {}
        MR["f32c"] = ar.alloc("lnf32c", [128, TT], F32)
        MR["xlo"] = ar.alloc("moe_xlo", [128, NC, T], BF16)
        MR["gwst"] = ar.alloc("moe_gwst", [128, NC, 128], BF16)
        MR["gwst_lo"] = ar.alloc("moe_gwstlo", [128, NC, 128], BF16)
        MR["lgt"] = ar.alloc("moe_lgt", [128, 16, 64], F32)
        MR["sc"] = ar.alloc("moe_sc", [128, 16, 64], F32)
        MR["wa"] = ar.alloc("moe_wa", [128, 16, 64], F32)
        MR["ct"] = ar.alloc("moe_ct", [128, 16, 64], F32)
        MR["sfc"] = ar.alloc("moe_sfc", [128, 64], F32)
        MR["tmp"] = ar.alloc("moe_tmp", [128, 64], F32)
        MR["tmp2"] = ar.alloc("moe_tmp2", [128, 64], F32)
        MR["bbc"] = ar.alloc("moe_bbc", [128, 64], F32)
        MR["idx8"] = ar.alloc("moe_idx8", [128, 16, 8], F32)
        MR["aot"] = ar.alloc("moe_aot", [64, T], F32)
        MR["aot2"] = ar.alloc("moe_aot2", [64, T], F32)
        MR["Lmrg"] = ar.alloc("moe_Lmrg", [128, LROWS], F32)
        MR["Lin"] = ar.alloc("moe_Lin", [128, 2, LROWS], F32)
        m_moe = ar.mark()
        ar.reset(m0_)
        ME = {}
        ME["buf_tm"] = ar.alloc("moe_btm0", [128, 8, 1024], BF16)
        ME["buf_fm"] = ar.alloc("moe_bfm", [128, 8, 1024], BF16)
        ME["h_fm"] = ar.alloc("moe_hfm", [128, 4, 1024], BF16)
        ME["sg"] = ar.alloc("moe_sg", [128, 512], BF16)
        ME["wg_st"] = ar.alloc("moe_wgst", [128, 8, 128], BF16)
        ME["wu_st"] = ar.alloc("moe_wust", [128, 8, 128], BF16)
        ME["wd_st"] = ar.alloc("moe_wdst", [128, 4, 512], BF16)
        ar.reset(m0_)
        MC = {}
        MC["ydst"] = [ar.alloc(f"moe_yd{i}", [128, 1024], BF16) for i in range(3)]
        MC["acc"] = ar.alloc("moe_acc", [128, 1024], F32)
        ar.reset(m0_)

        tabs = {"g": mk_tabs(ctx, 1000000.0), "l": mk_tabs(ctx, 10000.0)}

        def rt_load(which):
            cd, sd = tabs[which]
            k.op("sync", lambda e, cd=cd: e.dma_start(out=rct[:, :], in_=cd),
                 writes=[("rtc",)], dma=True)
            k.op("sync", lambda e, sd=sd: e.dma_start(out=rst[:, :], in_=sd),
                 writes=[("rts",)], dma=True)

        for p in ("a0", "b0"):
            for nm in ("wq", "wk", "wv", "wo"):
                src, dst = E_[f"{p}_{nm}"], bf[f"{p}_{nm}"]
                key = f"{p}_{nm}bf"
                if p == "a0" and nm in ("wq", "wk"):
                    inr = src.rearrange("(h j par) d -> h par j d", par=2, j=32)
                    outr = dst.rearrange("(h par j) d -> h par j d", par=2, j=32)
                    for par in range(2):
                        ia, oa = inr[:, par, :, :], outr[:, par, :, :]
                        k.op("gpsimd", lambda e, ia=ia, oa=oa: e.dma_start(out=oa, in_=ia),
                             writes=[key], dma=True)
                else:
                    k.op("gpsimd", lambda e, src=src, dst=dst: e.dma_start(out=dst, in_=src),
                         writes=[key], dma=True)
        for nm, key in [("ffn_wg", "wgbf"), ("ffn_wu", "wubf"), ("ffn_wd", "wdbf")]:
            src, dst = E_[nm + "_i"], bf[nm]
            k.op("gpsimd", lambda e, src=src, dst=dst: e.dma_start(out=dst, in_=src),
                 writes=[key], dma=True)
        k.op("gpsimd", lambda e: e.dma_start(out=bf["a1_w2"], in_=E_["a1_w2"]),
             writes=["w2bf"], dma=True)
        for nm in B1SH:
            k.op("gpsimd", lambda e, s=E_["b1_" + nm], d=bf["b1_" + nm]: e.dma_start(
                out=d, in_=s), writes=[nm + "bf"], dma=True)
        wpm = {
            "gate_i": (E_["moe_gate_w"], None), "gate_bf": (bf["moe_gate"], "gatebf"),
            "gate_lo_bf": (bf["moe_gate_lo"], "gatelobf"),
            "gate_bias": (E_["moe_gate_b"], None),
            "wg_i": (E_["moe_wg_i"], None), "wg_bf": (bf["moe_wg"], "wgbf_m"),
            "wu_i": (E_["moe_wu_i"], None), "wu_bf": (bf["moe_wu"], "wubf_m"),
            "wd_i": (E_["moe_wd_i"], None), "wd_bf": (bf["moe_wd"], "wdbf_m"),
        }
        moe_prep_weights(ctx, wpm)

        def mk_mha_a(ph):
            a = dict(MH)
            a["bias"], a["bias2"], a["bias3"] = bias1, bias2, bias3
            a["nrm"], a["mrow"], a["mcols"] = nrm, mrow, mcols
            a["srow"], a["gb"] = srow, gbt
            a["stage"] = stage
            a["triu"], a["triu_key"] = triu_t, triu_k
            return a

        def phase_mha(ph, src, dst, variant):
            xsrc = XDram(ctx, src, ("xdram", ph), pool=xdpool)
            a = mk_mha_a(ph)
            with contextlib.ExitStack() as st_:
                a["psS"] = st_.enter_context(nc.psum_tensor(f"{ph}_psS", [128, 1024], F32))
                a["psO"] = [st_.enter_context(nc.psum_tensor(f"{ph}_psO{i}", [65, TT], F32))
                            for i in range(4)]
                if variant == "a0":
                    rt_load("l")
                    W = {"q": (bf["a0_wq"], True, "a0_wqbf"),
                         "k": (bf["a0_wk"], True, "a0_wkbf"),
                         "v": (bf["a0_wv"], True, "a0_wvbf"),
                         "o": (bf["a0_wo"], True, "a0_wobf"),
                         "bq": (E_["a0_bq"], None), "bk": (E_["a0_bk"], None),
                         "bv": (E_["a0_bv"], None), "bo": (E_["a0_bo"], None)}

                    def xout_writer(oc, t0, ps, pkey, bias_info):
                        bcol, bck, occ = bias_info
                        st1, k1 = stage.next()
                        bc = bcol[:, occ:occ + 1]
                        k.op("scalar", lambda e, ps=ps, st1=st1, bc=bc: e.activation(
                            out=st1[:, :], in_=ps[:, 0:TT], func=AF.Identity, bias=bc,
                            scale=1.0), reads=[pkey, bck], writes=[k1])
                        xap, xkey = xsrc.get(ctx, oc, t0, TT)
                        k.op("vector", lambda e, st1=st1, xap=xap: e.tensor_add(
                            out=st1[:, :], in0=st1[:, :], in1=xap),
                            reads=[k1, xkey], writes=[k1])
                        k.op("sync", lambda e, oc=oc, t0=t0, st1=st1: e.dma_start(
                            out=dst[oc * 128:(oc + 1) * 128, t0:t0 + TT], in_=st1[:, :]),
                            reads=[k1], writes=[("yo", ph, oc, t0)], dma=True)

                    step_mha(ctx, a, xsrc, xout_writer, xln, xlnkeys, W, lnsc, pp,
                             {"rope": (rct, ("rtc",), rst, ("rts",)), "post": "res"})
                else:
                    W = {"q": (bf["b0_wq"], False, "b0_wqbf"),
                         "k": (bf["b0_wk"], False, "b0_wkbf"),
                         "v": (bf["b0_wv"], False, "b0_wvbf"),
                         "o": (bf["b0_wo"], False, "b0_wobf"),
                         "bq": (E_["b0_bq"], None), "bk": (E_["b0_bk"], None),
                         "bv": (E_["b0_bv"], None), "bo": (E_["b0_bo"], None)}

                    def xout_writer(oc, t0, ps, pkey, info):
                        flag, zt, zkey = info
                        st1, k1 = stage.next()
                        xap, xkey = xsrc.get(ctx, oc, t0, TT)
                        zap = zt[:, oc, t0:t0 + TT]
                        k.op("vector", lambda e, st1=st1, xap=xap, zap=zap: e.tensor_add(
                            out=st1[:, :], in0=zap, in1=xap),
                            reads=[zkey, xkey], writes=[k1])
                        k.op("sync", lambda e, oc=oc, t0=t0, st1=st1: e.dma_start(
                            out=dst[oc * 128:(oc + 1) * 128, t0:t0 + TT], in_=st1[:, :]),
                            reads=[k1], writes=[("yo", ph, oc, t0)], dma=True)

                    step_mha(ctx, a, xsrc, xout_writer, xln, xlnkeys, W, lnsc, pp,
                             {"rope": None, "post": "lnaff",
                              "gamma_beta": (E_["b0_gb"], None)})

        def phase_ffn(ph, src, dst):
            xsrc = XDram(ctx, src, ("xdram", ph), pool=xdpool)
            a = dict(FF)
            with contextlib.ExitStack() as st_:
                a["psG"] = st_.enter_context(nc.psum_tensor(f"{ph}_psG", [128, 2, FT], F32))
                a["psU"] = st_.enter_context(nc.psum_tensor(f"{ph}_psU", [128, 2, FT], F32))
                a["psY"] = st_.enter_context(nc.psum_tensor(f"{ph}_psY", [128, 2, FT], F32))

                def xout_writer(oc, t0, psY, pkey):
                    st1, k1 = stage.next()
                    xap, xkey = xsrc.get(ctx, oc, t0, FT)
                    k.op("vector", lambda e, st1=st1, psY=psY, xap=xap: e.scalar_tensor_tensor(
                        out=st1[:, :], in0=psY, scalar=S_DQ * S_DQ, in1=xap,
                        op0=ALU.mult, op1=ALU.add),
                        reads=[pkey, xkey], writes=[k1])
                    k.op("sync", lambda e, oc=oc, t0=t0, st1=st1: e.dma_start(
                        out=dst[oc * 128:(oc + 1) * 128, t0:t0 + FT], in_=st1[:, :]),
                        reads=[k1], writes=[("yo", ph, oc, t0)], dma=True)

                step_ffn0x(ctx, xsrc, xout_writer, xln, xlnkeys, bf["ffn_wg"],
                           bf["ffn_wu"], bf["ffn_wd"], a, lnsc, tag=ph,
                           wdram_keys=["wgbf", "wubf", "wdbf"])

        def phase_a1(ph, src, dst):
            xsrc = XDram(ctx, src, ("xdram", ph), pool=xdpool)
            a = dict(A1)
            a["srow"], a["bias"], a["stage"] = srow, bias1, stage
            with contextlib.ExitStack() as st_:
                a["psS"] = st_.enter_context(nc.psum_tensor(f"{ph}_psS", [128, 1024], F32))
                psC = st_.enter_context(nc.psum_tensor(f"{ph}_psC", [1, T], F32))
                wp = {"w2": (bf["a1_w2"], "w2bf"), "w1row": (E_["a1_w1"], None),
                      "b1": (E_["a1_b1"], None)}

                def xout_writer(oc, t0, ps, pkey, info):
                    st1, k1 = stage.next()
                    xap, xkey = xsrc.get(ctx, oc, t0, TT)
                    k.op("vector", lambda e, st1=st1, xap=xap, ps=ps: e.tensor_add(
                        out=st1[:, :], in0=ps[:, 0:TT], in1=xap),
                        reads=[pkey, xkey], writes=[k1])
                    k.op("sync", lambda e, oc=oc, t0=t0, st1=st1: e.dma_start(
                        out=dst[oc * 128:(oc + 1) * 128, t0:t0 + TT], in_=st1[:, :]),
                        reads=[k1], writes=[("yo", ph, oc, t0)], dma=True)

                step_a1(ctx, a, xsrc, xout_writer, xln, xlnkeys, wp, lnsc, pp, psC)

        def phase_moe(ph, src, dst, inst, first):
            xsrc = XDram(ctx, src, ("xdram", ph), pool=xdpool)
            a = {}
            a.update(mb)
            a.update(MR)
            a.update(ME)
            a.update(MC)
            a["buf_tm"] = [ME["buf_tm"]]
            a["xln"] = xln
            a["xlnkeys"] = xlnkeys
            a["xlo"] = MR["xlo"]
            a["xlokeys"] = [("moexlo", ci) for ci in range(NC)]
            a["stage"] = stage
            with contextlib.ExitStack() as st_:
                a["psY"] = st_.enter_context(nc.psum_tensor(f"{ph}_psY", [128, 512], F32))
                a["psG"] = st_.enter_context(nc.psum_tensor(f"{ph}_psG", [128, 512], F32))
                a["psU"] = st_.enter_context(nc.psum_tensor(f"{ph}_psU", [128, 512], F32))
                a["pst"] = st_.enter_context(nc.psum_tensor(f"{ph}_pst", [128, 128], BF16))
                a["pstf"] = st_.enter_context(nc.psum_tensor(f"{ph}_pstf", [128, 128], F32))
                if first:
                    moe_prep(ctx, a, wpm, weights=False)

                def xout_writer(dc, tc, pstf, pkey):
                    st1, k1 = a["stage"].next()
                    xap, xkey = xsrc.get(ctx, dc, tc * 128, 128)
                    k.op("vector", lambda e, st1=st1, xap=xap, pstf=pstf: e.tensor_add(
                        out=st1[:, 0:128], in0=pstf[:, :], in1=xap),
                        reads=[pkey, xkey], writes=[k1])
                    k.op("sync", lambda e, dc=dc, tc=tc, st1=st1: e.dma_start(
                        out=dst[dc * 128:(dc + 1) * 128, tc * 128:(tc + 1) * 128],
                        in_=st1[:, 0:128]),
                        reads=[k1], writes=[("yo", ph, dc, tc)], dma=True)

                mlnsc = dict(lnsc)
                mlnsc["f32c"] = (MR["f32c"], "lnf32c")
                step_moe(ctx, a, xsrc, xout_writer, wpm, mlnsc, pp, mdram[inst], inst,
                         barrier=k.barrier)

        def phase_b1(ph, src, dst):
            xsrc = XDram(ctx, src, ("xdram", ph), pool=xdpool)
            a = dict(B1)
            a["xikeys"] = [("xi", ci) for ci in range(NC)]
            a["xrkeys"] = [("xr", ci) for ci in range(NC)]
            a["q"] = xln
            a["o"] = a["xr"]
            a["okeys"] = a["xrkeys"]
            a["srow"], a["bias"], a["bias2"] = srow, bias1, bias2
            a["wmask"], a["wmask_key"] = wmask_t, wmask_k
            a["triu"], a["triu_key"] = triu_t, triu_k
            a["P128"] = P128
            a["stage"] = stage
            a["rt_load"] = rt_load
            a["rt"] = (rct, ("rtc",), rst, ("rts",))
            b1lnsc = {"murow": (murow, "murow"), "varrow": (varrow, "varrow"),
                      "sq": (sq, "lnsq"), "mubc": (mubc, "mubc"), "rbc": (rbc, "rbc"),
                      "pp": pp}
            with contextlib.ExitStack() as st_:
                a["psS"] = st_.enter_context(nc.psum_tensor(f"{ph}_psS", [128, 1024], F32))
                a["psO"] = [st_.enter_context(nc.psum_tensor(f"{ph}_psO{i}", [65, TT], F32))
                            for i in range(4)]
                W = {nm: (bf["b1_" + nm], nm + "bf") for nm in B1SH}
                W["n1w"] = (E_["b1_n1w"], None)
                W["n2w"] = (E_["b1_n2w"], None)

                def xout_writer(oc, t0, xi_ap, xi_key, info):
                    st1, k1 = stage.next()
                    xap, xkey = xsrc.get(ctx, oc, t0, TT)
                    k.op("vector", lambda e, st1=st1, xap=xap, xi_ap=xi_ap: e.tensor_add(
                        out=st1[:, :], in0=xi_ap, in1=xap),
                        reads=[xi_key, xkey], writes=[k1])
                    k.op("sync", lambda e, oc=oc, t0=t0, st1=st1: e.dma_start(
                        out=dst[oc * 128:(oc + 1) * 128, t0:t0 + TT], in_=st1[:, :]),
                        reads=[k1], writes=[("yo", ph, oc, t0)], dma=True)

                step_b1(ctx, a, xsrc, xout_writer, W, b1lnsc, pp)

        phase_mha("pa0", xT, xA, "a0")
        k.barrier()
        phase_ffn("pf0", xA, xB)
        k.barrier()
        phase_mha("pb0", xB, xA, "b0")
        k.barrier()
        phase_ffn("pf1", xA, xB)
        k.barrier()
        phase_a1("pa1", xB, xA)
        k.barrier()
        phase_moe("pm0", xA, xB, "m0", True)
        k.barrier()
        phase_b1("pb1", xB, xA)
        k.barrier()
        phase_moe("pm1", xA, yT, "m1", False)
        k.emit(block)
    return nc



def _cols(v):
    return np.ascontiguousarray(np.asarray(v, np.float32).reshape(NC, 128).T)


def _rope_perm():
    p = np.zeros(D, np.int64)
    for h in range(16):
        for par in range(2):
            for j in range(32):
                p[64 * h + 32 * par + j] = 64 * h + 2 * j + par
    return p


def kernel(**inp):
    f = {kk2: np.asarray(vv) for kk2, vv in inp.items()}
    x = np.asarray(f["L0_attn_a_x"], np.float32)
    B = x.shape[0]
    perm = _rope_perm()
    com = {
        "a0_wq": np.ascontiguousarray(f["L0_attn_a_W_q"].astype(np.float32)),
        "a0_wk": np.ascontiguousarray(f["L0_attn_a_W_k"].astype(np.float32)),
        "a0_wv": np.ascontiguousarray(f["L0_attn_a_W_v"].astype(np.float32)),
        "a0_wo": np.ascontiguousarray(f["L0_attn_a_W_o"].astype(np.float32)),
        "a0_bq": _cols(np.asarray(f["L0_attn_a_b_q"], np.float32)[perm]),
        "a0_bk": _cols(np.asarray(f["L0_attn_a_b_k"], np.float32)[perm]),
        "a0_bv": np.ascontiguousarray(f["L0_attn_a_b_v"].astype(np.float32).reshape(1, D)),
        "a0_bo": _cols(f["L0_attn_a_b_o"]),
        "b0_wq": np.ascontiguousarray(f["L0_attn_b_Wq"].astype(np.float32)),
        "b0_wk": np.ascontiguousarray(f["L0_attn_b_Wk"].astype(np.float32)),
        "b0_wv": np.ascontiguousarray(f["L0_attn_b_Wv"].astype(np.float32)),
        "b0_wo": np.ascontiguousarray(f["L0_attn_b_Wo"].astype(np.float32)),
        "b0_bq": _cols(f["L0_attn_b_bq"]), "b0_bk": _cols(f["L0_attn_b_bk"]),
        "b0_bv": np.ascontiguousarray(f["L0_attn_b_bv"].astype(np.float32).reshape(1, D)),
        "b0_bo": _cols(f["L0_attn_b_bo"]),
        "b0_gb": np.ascontiguousarray(np.concatenate(
            [_cols(f["L0_attn_b_gamma"]), _cols(f["L0_attn_b_beta"])], axis=1)),
        "ffn_wg_i": np.ascontiguousarray(f["L0_ffn_w_gate"].astype(np.int32).T),
        "ffn_wu_i": np.ascontiguousarray(f["L0_ffn_w_up"].astype(np.int32).T),
        "ffn_wd_i": np.ascontiguousarray(f["L0_ffn_w_down"].astype(np.int32).T),
        "a1_w2": np.ascontiguousarray(f["L1_attn_a_W2"].astype(np.float32)),
        "a1_w1": np.ascontiguousarray(f["L1_attn_a_W1"].astype(np.float32).reshape(1, D)),
        "a1_b1": _cols(f["L1_attn_a_b1"]),
        "b1_n1w": _cols(f["L1_attn_b_norm1_w"]),
        "b1_n2w": _cols(f["L1_attn_b_norm2_w"]),
        "moe_gate_w": np.ascontiguousarray(f["L1_ffn_gate_w"].astype(np.float32)),
        "moe_gate_b": np.ascontiguousarray(
            f["L1_ffn_gate_bias"].astype(np.float32).reshape(1, 64)),
    }
    for nm in ("g_wq", "g_kv_down", "g_k_up", "g_v_up", "g_wo",
               "l_wq", "l_wk", "l_wv", "l_wo"):
        com["b1_" + nm] = np.ascontiguousarray(f["L1_attn_b_" + nm].astype(np.float32))
    wg_all = f["L1_ffn_w_gate"].astype(np.float32)
    wu_all = f["L1_ffn_w_up"].astype(np.float32)
    wd_all = f["L1_ffn_w_down"].astype(np.float32)
    p_ = np.arange(128)

    nc = build_full()
    in_maps = []
    for c in range(B):
        m = dict(com)
        m["xT"] = np.ascontiguousarray(x[c].T)
        m["moe_wg_i"] = np.ascontiguousarray(wg_all[c * EPG:(c + 1) * EPG])
        m["moe_wu_i"] = np.ascontiguousarray(wu_all[c * EPG:(c + 1) * EPG])
        m["moe_wd_i"] = np.ascontiguousarray(wd_all[c * EPG:(c + 1) * EPG])
        m["moe_tokid"] = (c * T + np.arange(16)[None, :] * 128 + p_[:, None]).astype(np.float32)
        m["moe_slab"] = (c * EPG * CAP + np.arange(64)[None, :] * 128 + p_[:, None]).astype(np.int32)
        m["moe_cmask"] = (np.arange(8) < c).astype(np.float32).reshape(8, 1)
        in_maps.append(m)
    res = run_bass_kernel_spmd(nc, in_maps, core_ids=list(range(B)))
    return np.ascontiguousarray(np.stack(
        [np.asarray(res.results[c]["yT"], np.float32).T for c in range(B)]))
